# version 13
# baseline (speedup 1.0000x reference)
from contextlib import ExitStack

import numpy as np
import ml_dtypes

import concourse.bass as bass
import concourse.mybir as mybir
from concourse.bass_utils import run_bass_kernel_spmd
from concourse.alu_op_type import AluOpType as Op

F32 = mybir.dt.float32
BF16 = mybir.dt.bfloat16
I32 = mybir.dt.int32
AF = mybir.ActivationFunctionType
AX = mybir.AxisListType.X

PI = float(np.pi)
B, S, D = 4, 4096, 1024
DH, FH = 2048, 4096
CA = 256
NCH_A = S // CA
TW = S // 2
NCH_C = TW // CA
NP_D = 2
PT = TW // NP_D
CLW = 0.5 - 3e-8


class Ctr:
    def __init__(self, sem):
        self.sem = sem
        self.v = 0

    def inc(self, inst, n=1):
        if inst is not None and self.sem is not None:
            inst.then_inc(self.sem, n)
        self.v += n


class MockInst:
    def then_inc(self, sem, n):
        return self


class MockAP:
    def __getitem__(self, k):
        return self

    def rearrange(self, *a, **k):
        return self


class MockVal:
    def __and__(self, o):
        return self

    def __add__(self, o):
        return self


class MockEng:
    def partition_id(self):
        return MockVal()

    def wait_ge(self, *a, **k):
        return MockInst()

    def __getattr__(self, name):
        return lambda *a, **k: MockInst()


CTR_NAMES = ["wob", "vx", "vt", "sct", "zr", "bg", "bgv", "init", "ginit", "cww", "vinit",
             "vhs", "hmg", "ax", "trE", "xf", "mm", "we", "gm", "ge", "cv",
             "vc", "vs", "ccin", "cc", "cw", "cx", "wo", "lnv", "lnc", "ln",
             "tr2", "x2f", "xd", "dw", "dx", "dmm", "dev", "dvh", "dx2",
             "dlnv", "dlnc", "dln", "dout"]


def build_bodies(T, M, emit):
    C = T["ctr"]

    def rec(key, ctr, d=0):
        if not emit:
            M[key] = ctr.v + d

    def wt(eng, ctr, key):
        if emit:
            eng.wait_ge(ctr.sem, M[key])

    def wv(eng, ctr, val):
        if emit:
            eng.wait_ge(ctr.sem, val)

    mi = MockInst()
    g2 = lambda t, m: t[:, m * CA:(m + 1) * CA]

    def body0_sync(s):
        for dst, src in (("b_in_sb", "b_in_c"), ("b_g_sb", "b_g_c"),
                         ("aprm_sb", "aprm_c"), ("aimp_sb", "aimp_c"),
                         ("csts", "consts_c"), ("bout_sb", "b_out_rep"),
                         ("w3s_sb", "w3s_rep"), ("eye_sb", "eye_f"),
                         ("s1_sb", "s1_r"), ("s2_sb", "s2_r"),
                         ("h1s_sb", "h1s_r"), ("h2s_sb", "h2s_r")):
            C["init"].inc(s.dma_start(out=T[dst][...], in_=T[src]), 16)
        rec("init.done", C["init"])
        for c in range(NCH_A):
            if c > 0:
                wt(s, C["trE"], f"A.tr.{c - 1}")
            C["ax"].inc(s.dma_start(
                out=T["xstage"][...],
                in_=T["xfull"].rearrange("(c t p) d -> c p t d", c=NCH_A, p=128)[c]), 16)
            rec(f"A.x.{c}", C["ax"])

    def body0_gpsimd(g):
        C["ginit"].inc(g.dma_start(
            out=T["win_sb"][...],
            in_=T["w_in_c"].rearrange("(k p) m -> p k m", p=128)), 16)
        C["ginit"].inc(g.dma_start(out=T["wg_sb"][...], in_=T["w_g_c"]), 16)
        rec("init.w", C["ginit"])
        i = g.dma_start(out=T["wo_bf"], in_=T["w_out_c"])
        C["wob"].inc(i, 16)
        rec("A.wob", C["wob"])
        wt(g, C["zr"], "A.zeros")
        wt(g, C["ccin"], "A.ccd.11")
        i = g.collective_compute(
            "ReduceScatter", Op.add,
            replica_groups=[[0, 1], [2, 3], [4, 5], [6, 7]],
            ins=[T["cc_in"]], outs=[T["cc_out"]])
        C["cc"].inc(i, 1)

    def body0_vector(v):
        C["vinit"].inc(v.memset(T["zt"][...], 0.0), 1)
        v.memset(T["ones_a"][...], 1.0)
        v.memset(T["cre"][...], 0.0)
        C["vinit"].inc(v.memset(T["cim"][...], 0.0), 1)
        for gb in range(8):
            wt(v, C["bg"], f"A.bg.{gb}")
            i = v.tensor_scalar(T["b_geff"][:, gb:gb + 1],
                                T["ps_g"][:, gb % 2, 0:1],
                                T["b_g_sb"][:, gb:gb + 1], None, Op.add)
            C["bgv"].inc(i, 1)
            rec(f"A.bgv.{gb}", C["bgv"])
        wt(v, C["cv"], "A.prm")

        def fold(c):
            par = c % 2
            F1, F2, F3 = T["fl1"], T["fl2"], T["fl3"]
            ki = T["p_ki"]
            wv(v, C["ge"], 8 * (c + 1))
            wt(v, C["cv"], f"A.cv.r2.{c}")
            for m in range(4):
                v.tensor_scalar(g2(F1, m), g2(T["gaf"][par], m),
                                T["aim2_sb"][:, m:m + 1], None, Op.mult)
            for m in range(4):
                v.tensor_tensor_scan(g2(F2, m), T["ones_a"][...],
                                     g2(F1, m), 0.0, Op.mult, Op.add)
            v.tensor_copy(ki[...], F2[...])
            v.tensor_copy(F1[...], ki[...])
            v.tensor_sub(F3[...], F2[...], F1[...])
            v.tensor_single_scalar(F2[...], F3[...], 0.5, Op.is_gt)
            v.scalar_tensor_tensor(F1[...], F2[...], -1.0, F3[...],
                                   Op.mult, Op.add)
            i = v.tensor_scalar(T["p_rc"][par][...], F1[...], -CLW, CLW,
                                Op.max, Op.min)
            C["vc"].inc(i, 1)
            rec(f"A.vc.rc.{c}", C["vc"])
            i = v.tensor_single_scalar(T["p_r2c"][par][...], T["p_r2"][par][...],
                                       1.0 - 1e-6, Op.min)
            C["vc"].inc(i, 1)
            rec(f"A.vc.r2c.{c}", C["vc"])

        def main(c):
            par = c % 2
            P2, P3, P5, P6, P7 = (T["pl2"], T["pl3"], T["pl5"], T["pl6"], T["pl7"])
            rr = T["p_rr"][par]
            psin, pcos, pnrm = T["p_sin"], T["p_cos"], T["p_nrm"]
            gxf = T["gxf"][par]
            xre, xim = T["xh_re"][par], T["xh_im"][par]
            wv(v, C["we"], 16 * (c + 1))
            wt(v, C["cv"], f"A.cv.nrm.{c}")
            v.tensor_mul(P2[...], gxf[...], pnrm[...])
            v.tensor_mul(P3[...], xre[...], P2[...])
            v.tensor_mul(P5[...], xim[...], P2[...])
            v.tensor_mul(P6[...], P3[...], pcos[...])
            v.tensor_mul(P7[...], P5[...], psin[...])
            v.tensor_add(P2[...], P6[...], P7[...])
            v.tensor_mul(P6[...], P5[...], pcos[...])
            v.tensor_mul(P7[...], P3[...], psin[...])
            i = v.tensor_sub(P5[...], P6[...], P7[...])
            C["vx"].inc(i, 1)
            rec(f"A.vx.{c}", C["vx"])
            if c >= 1:
                wt(v, C["hmg"], f"A.hmg.{c - 1}")
            for m in range(4):
                v.tensor_tensor_scan(g2(T["p_H1"], m), g2(rr, m), g2(P2, m),
                                     T["cre"][:, m:m + 1], Op.mult, Op.add)
            for m in range(4):
                v.tensor_tensor_scan(g2(T["p_H2"], m), g2(rr, m), g2(P5, m),
                                     T["cim"][:, m:m + 1], Op.mult, Op.add)
            L = lambda t: t.rearrange("p (g n) -> p g n", n=CA)[:, :, CA - 1:CA]
            c2 = lambda t: t.rearrange("p (g o) -> p g o", o=1)
            v.tensor_mul(c2(T["sc4"]), L(T["p_H1"]), L(pcos))
            v.tensor_mul(c2(T["sc4b"]), L(T["p_H2"]), L(psin))
            v.tensor_mul(c2(T["sc4c"]), L(T["p_H2"]), L(pcos))
            i = v.tensor_mul(c2(T["sc4d"]), L(T["p_H1"]), L(psin))
            C["vt"].inc(i, 1)
            wv(v, C["vt"], C["vt"].v)
            v.tensor_sub(c2(T["cre"]), c2(T["sc4"]), c2(T["sc4b"]))
            i = v.tensor_add(c2(T["cim"]), c2(T["sc4c"]), c2(T["sc4d"]))
            C["vhs"].inc(i, 1)
            rec(f"A.vhs.{c}", C["vhs"])
            if c >= 2:
                wt(v, C["ccin"], f"A.ccd.{c - 2}")
            g4v = lambda t: t.rearrange("p (g n) -> p g n", n=CA)
            v.tensor_mul(P6[...], T["p_H1"][...], pcos[...])
            v.tensor_mul(P7[...], T["p_H2"][...], psin[...])
            v.tensor_sub(T["hmst"][par][:, 0:4, :], g4v(P6), g4v(P7))
            v.tensor_mul(P6[...], T["p_H2"][...], pcos[...])
            v.tensor_mul(P7[...], T["p_H1"][...], psin[...])
            i = v.tensor_add(T["hmst"][par][:, 4:8, :], g4v(P6), g4v(P7))
            C["hmg"].inc(i, 1)
            rec(f"A.hmg.{c}", C["hmg"])

        fold(0)
        fold(1)
        for c in range(NCH_A):
            main(c)
            if c + 2 < NCH_A:
                fold(c + 2)

    def body0_tensor(t):
        wt(t, C["ginit"], "init.w")
        wt(t, C["cv"], "A.prm")
        for gb in range(8):
            gate, m = divmod(gb, 4)
            if gb >= 2:
                wt(t, C["bgv"], f"A.bgv.{gb - 2}")
            for k in range(2):
                i = t.matmul(T["ps_g"][:, gb % 2, 0:1],
                             T["wg_sb"][:, gb * 2 + k, :],
                             T["b_in_f"][:, 2 * m + k, :],
                             start=(k == 0), stop=(k == 1))
            C["bg"].inc(i, 1)
            rec(f"A.bg.{gb}", C["bg"])

        def transposes(c):
            wt(t, C["ax"], f"A.x.{c}")
            if c > 0:
                wt(t, C["xf"], f"A.xf.{c - 1}")
            for gi in range(8):
                for tt in range(2):
                    i = t.transpose(T["ps_tr"][:, gi * 2 + tt, :],
                                    T["xstage"][:, tt, gi * 128:(gi + 1) * 128],
                                    T["eye_sb"][...])
            C["trE"].inc(i, 1)
            rec(f"A.tr.{c}", C["trE"])

        transposes(0)
        for c in range(NCH_A):
            wt(t, C["xf"], f"A.xf.{c}")
            for m in range(16):
                if m >= 2:
                    wt(t, C["we"], f"A.we.{c}.{m - 2}")
                elif c > 0:
                    wt(t, C["we"], f"A.we.{c - 1}.{m + 14}")
                for k in range(8):
                    i = t.matmul(T["ps_win"][:, m % 2, 0:CA],
                                 T["win_sb"][:, k, m * 128:(m + 1) * 128],
                                 T["xfeat"][c % 2][:, k, :],
                                 start=(k == 0), stop=(k == 7))
                C["mm"].inc(i, 1)
                rec(f"A.mm.{c}.{m}", C["mm"])
            if c + 1 < NCH_A:
                transposes(c + 1)
            for gb in range(8):
                gate, m = divmod(gb, 4)
                if gb >= 2:
                    wt(t, C["ge"], f"A.ge.{c}.{gb - 2}")
                elif c > 0:
                    wt(t, C["ge"], f"A.ge.{c - 1}.{gb + 6}")
                else:
                    wv(t, C["bgv"], 8)
                for k in range(2):
                    i = t.matmul(T["ps_g"][:, gb % 2, 0:CA],
                                 T["wg_sb"][:, gb * 2 + k, :],
                                 T["xh_bf"][c % 2][:, 2 * m + k, :],
                                 start=(k == 0), stop=(k == 1))
                C["gm"].inc(i, 1)
                rec(f"A.gm.{c}.{gb}", C["gm"])

    def body0_scalar(sc):
        wt(sc, C["init"], "init.done")
        sc.copy(T["b_in_f"].rearrange("p g o -> p (g o)"), T["b_in_sb"][:, 0:8])
        i = sc.activation(T["sc4"][...], T["aprm_sb"][...], AF.Exp)
        C["sct"].inc(i, 1)
        wv(sc, C["sct"], C["sct"].v)
        i = sc.activation(T["sc4b"][...], T["sc4"][...], AF.Ln, bias=1.0, scale=1.0)
        C["sct"].inc(i, 1)
        wv(sc, C["sct"], C["sct"].v)
        sc.mul(T["spc_sb"][...], T["sc4b"][...], -8.0)
        i = sc.mul(T["aim2_sb"][...], T["aimp_sb"][...], 1.0 / (2.0 * PI))
        C["cv"].inc(i, 1)
        rec("A.prm", C["cv"])
        wv(sc, C["vinit"], 1)
        if emit:
            pid = sc.partition_id()
            jv = pid & 1
        else:
            jv = None
        for cct in ("cc_in", "cc_inb"):
            for wdw in range(2):
                for slot in range(2):
                    for gi in range(8):
                        i = sc.dma_start(
                            out=T[cct][wdw, slot, gi * 128:(gi + 1) * 128, :],
                            in_=T["zt"][:, 0:TW // 2])
                        C["zr"].inc(i, 16)
        rec("A.zeros", C["zr"])

        def prework(c):
            par = c % 2
            wt(sc, C["trE"], f"A.tr.{c}")
            if c >= 2:
                wt(sc, C["mm"], f"A.mm.{c - 2}.15")
                wt(sc, C["vx"], f"A.vx.{c - 2}")
            else:
                wv(sc, C["bgv"], 8)
            i = sc.copy(T["xfeat"][par].rearrange("p g (t n) -> p (g t) n", n=128),
                        T["ps_tr"][...])
            C["xf"].inc(i, 1)
            rec(f"A.xf.{c}", C["xf"])
            for m in range(16):
                wt(sc, C["mm"], f"A.mm.{c}.{m}")
                if emit:
                    if m < 8:
                        dst = T["xh_bf"][par][:, m, :]
                    elif m < 12:
                        dst = g2(T["xh_re"][par], m - 8)
                    else:
                        dst = g2(T["xh_im"][par], m - 12)
                else:
                    dst = MockAP()
                bias = 0.0 if m < 8 else T["b_in_sb"][:, m:m + 1]
                i = sc.activation(dst, T["ps_win"][:, m % 2, 0:CA], AF.Identity,
                                  bias=bias, scale=1.0)
                C["we"].inc(i, 1)
                rec(f"A.we.{c}.{m}", C["we"])
            for gb in range(8):
                gate, m = divmod(gb, 4)
                wt(sc, C["gm"], f"A.gm.{c}.{gb}")
                dstt = T["gxf"][par] if gate == 0 else T["gaf"][par]
                i = sc.activation(g2(dstt, m), T["ps_g"][:, gb % 2, 0:CA],
                                  AF.Sigmoid, bias=T["b_geff"][:, gb:gb + 1],
                                  scale=1.0)
                C["ge"].inc(i, 1)
                rec(f"A.ge.{c}.{gb}", C["ge"])
            if c >= 2:
                wt(sc, C["vhs"], f"A.vhs.{c - 2}")
            for m in range(4):
                sc.activation(g2(T["p_rr"][par], m), g2(T["gaf"][par], m),
                              AF.Exp, bias=0.0, scale=T["spc_sb"][:, m:m + 1])
            i = sc.activation(T["p_r2"][par][...], T["p_rr"][par][...], AF.Square)
            C["cv"].inc(i, 1)
            rec(f"A.cv.r2.{c}", C["cv"])

        prework(0)
        prework(1)
        for c in range(NCH_A):
            par = c % 2
            wt(sc, C["vc"], f"A.vc.rc.{c}")
            if c >= 1:
                wt(sc, C["hmg"], f"A.hmg.{c - 1}")
            sc.activation(T["p_sin"][...], T["p_rc"][par][...], AF.Sin,
                          bias=0.0, scale=2.0 * PI)
            sc.activation(T["fl3"][...], T["p_rc"][par][...], AF.Abs)
            i = sc.activation(T["p_cos"][...], T["fl3"][...], AF.Sin,
                              bias=T["csts"][:, 0:1], scale=-2.0 * PI)
            C["cv"].inc(i, 1)
            rec(f"A.cv.sc.{c}", C["cv"])
            wt(sc, C["vc"], f"A.vc.r2c.{c}")
            i = sc.activation(T["p_nrm"][...], T["p_r2c"][par][...], AF.Sqrt,
                              bias=1.0, scale=-1.0)
            C["cv"].inc(i, 1)
            rec(f"A.cv.nrm.{c}", C["cv"])
            if c >= 1:
                cc = c - 1
                if cc == 0:
                    wt(sc, C["zr"], "A.zeros")
                cct = "cc_in" if (cc % NCH_C) < 4 else "cc_inb"
                i = sc.dma_start(
                    out=T[cct].rearrange("w s (g p) t -> w s p g t", p=128)[
                        cc // NCH_C, jv, :, :,
                        (cc % 4) * CA:(cc % 4 + 1) * CA],
                    in_=T["hmst"][cc % 2][...])
                C["ccin"].inc(i, 16)
                rec(f"A.ccd.{cc}", C["ccin"])
            if c + 2 < NCH_A:
                prework(c + 2)
        cc = NCH_A - 1
        wt(sc, C["hmg"], f"A.hmg.{cc}")
        i = sc.dma_start(
            out=T["cc_inb"].rearrange("w s (g p) t -> w s p g t", p=128)[
                cc // NCH_C, jv, :, :,
                (cc % 4) * CA:(cc % 4 + 1) * CA],
            in_=T["hmst"][cc % 2][...])
        C["ccin"].inc(i, 16)
        rec(f"A.ccd.{cc}", C["ccin"])
        rec("B.ccin2", C["ccin"])

    def body1_gpsimd(g):
        wt(g, C["ccin"], "B.ccin2")
        i = g.collective_compute(
            "ReduceScatter", Op.add,
            replica_groups=[[0, 1], [2, 3], [4, 5], [6, 7]],
            ins=[T["cc_inb"]], outs=[T["cc_outb"]])
        C["cc"].inc(i, 1)

    def body1_sync(s):
        wv(s, C["cc"], 1)
        for half in range(2):
            i = s.dma_start(
                out=T["hmfull"][:, half * 8:(half + 1) * 8, 0:TW // 2],
                in_=T["cc_out"].rearrange("w (g p) t -> w p g t", p=128)[half])
            C["cw"].inc(i, 16)
        rec("C.hma", C["cw"])
        for c in range(2):
            i = s.dma_start(
                out=T["xpo"][c % 2][...],
                in_=T["xpost"].rearrange("(c t p) d -> c p t d", c=NCH_C, p=128)[c])
            C["cx"].inc(i, 16)
            rec(f"C.x.{c}", C["cx"])
        wv(s, C["cc"], 2)
        for half in range(2):
            i = s.dma_start(
                out=T["hmfull"][:, half * 8:(half + 1) * 8, TW // 2:TW],
                in_=T["cc_outb"].rearrange("w (g p) t -> w p g t", p=128)[half])
            C["cw"].inc(i, 16)
        rec("C.hm", C["cw"])
        def stores_for(c, s=s):
            for tt in range(2):
                gt = 2 * c + tt
                wt(s, C["ln"], f"C.ln.{gt}")
                i = s.dma_start(
                    out=T["x2spill"][gt * 128:(gt + 1) * 128, :],
                    in_=T["x2tok"][tt][...])
                C["xd"].inc(i, 16)
                rec(f"C.xdt.{gt}", C["xd"])
            wt(s, C["x2f"], f"C.x2f.{c}")
            i = s.dma_start(
                out=T["x2feat"].rearrange("g p t -> p g t")[
                    :, :, c * CA:(c + 1) * CA],
                in_=T["x2fst"][...])
            C["xd"].inc(i, 16)
            rec(f"C.xd.{c}", C["xd"])

        for c in range(2, NCH_C):
            stores_for(c - 2)
            i = s.dma_start(
                out=T["xpo"][c % 2][...],
                in_=T["xpost"].rearrange("(c t p) d -> c p t d", c=NCH_C, p=128)[c])
            C["cx"].inc(i, 16)
            rec(f"C.x.{c}", C["cx"])
        stores_for(NCH_C - 2)
        stores_for(NCH_C - 1)
        rec("C.xd", C["xd"])

    def body1_tensor(t):
        wt(t, C["cww"], "C.w")

        def tr2(c):
            wt(t, C["ln"], f"C.ln.{2 * c + 1}")
            if c > 0:
                wt(t, C["x2f"], f"C.x2f.{c - 1}")
            for tt in range(2):
                for gi in range(8):
                    i = t.transpose(T["ps_tr2"][:, gi * 2 + tt, :],
                                    T["x2tok"][tt][:, gi * 128:(gi + 1) * 128],
                                    T["eye_sb"][...])
            C["tr2"].inc(i, 1)
            rec(f"C.tr2.{c}", C["tr2"])

        for c in range(NCH_C):
            wt(t, C["cx"], f"C.x.{c}")
            wt(t, C["cw"], "C.hma" if c < 4 else "C.hm")
            for tt in range(2):
                gt = 2 * c + tt
                if gt >= 2:
                    wt(t, C["ln"], f"C.ln.{gt - 2}")
                for k in range(16):
                    for n in range(2):
                        i = t.matmul(T["ps_wo"][:, tt, n, :],
                                     T["hmfull"][:, k, gt * 128:(gt + 1) * 128],
                                     T["wo_sb"][:, k, n * 512:(n + 1) * 512],
                                     start=(k == 0), stop=(k == 15))
                C["wo"].inc(i, 1)
                rec(f"C.wo.{gt}", C["wo"])
            if c > 0:
                tr2(c - 1)
        tr2(NCH_C - 1)

    def body1_vector(v):
        for c in range(NCH_C):
            for tt in range(2):
                gt = 2 * c + tt
                wt(v, C["wo"], f"C.wo.{gt}")
                wt(v, C["cx"], f"C.x.{c}")
                if emit:
                    pw = T["ps_wo"].rearrange("p s n x -> p s (n x)")[:, tt, :]
                else:
                    pw = MockAP()
                v.tensor_add(T["c_t2"][...], pw, T["bout_sb"][...])
                v.tensor_add(T["c_t"][...], T["c_t2"][...], T["xpo"][c % 2][:, tt, :])
                i = v.reduce_sum(T["c_sum"][...], T["c_t"][...], axis=AX)
                C["lnv"].inc(i, 1)
                rec(f"C.lnv.t.{gt}", C["lnv"])
                v.tensor_mul(T["c_t2"][...], T["c_t"][...], T["c_t"][...])
                v.reduce_sum(T["c_sq"][...], T["c_t2"][...], axis=AX)
                i = v.tensor_scalar_mul(T["c_m"][...], T["c_sum"][...], 1.0 / D)
                C["vt"].inc(i, 1)
                wv(v, C["vt"], C["vt"].v)
                i = v.tensor_mul(T["c_msq"][...], T["c_m"][...], T["c_m"][...])
                C["vt"].inc(i, 1)
                wv(v, C["vt"], C["vt"].v)
                i = v.scalar_tensor_tensor(T["c_var"][...], T["c_sq"][...],
                                           1.0 / D, T["c_msq"][...],
                                           Op.mult, Op.subtract)
                C["lnv"].inc(i, 1)
                rec(f"C.lnv.var.{gt}", C["lnv"])

                wt(v, C["lnc"], f"C.rstd.{gt}")
                if gt >= 2:
                    wt(v, C["xd"], f"C.xdt.{gt - 2}")
                    wt(v, C["tr2"], f"C.tr2.{(gt - 2) // 2}")
                i = v.reciprocal(T["c_rstd2"][...], T["c_rstd"][...])
                C["vt"].inc(i, 1)
                wv(v, C["vt"], C["vt"].v)
                i = v.tensor_scalar(T["x2tok"][tt][...], T["c_t"][...],
                                    T["c_m"][...], T["c_rstd2"][...],
                                    Op.subtract, Op.mult)
                C["ln"].inc(i, 1)
                rec(f"C.ln.{gt}", C["ln"])

    def body1_scalar(sc):
        wt(sc, C["wob"], "A.wob")
        i = sc.dma_start(
            out=T["wo_sb"][...],
            in_=T["wo_bf"].rearrange("(k p) m -> p k m", p=128))
        C["cww"].inc(i, 16)
        rec("C.w", C["cww"])
        for c in range(NCH_C):
            for tt in range(2):
                gt = 2 * c + tt
                wt(sc, C["lnv"], f"C.lnv.var.{gt}")
                i = sc.activation(T["c_rstd"][...], T["c_var"][...], AF.Sqrt,
                                  bias=T["csts"][:, 1:2], scale=1.0)
                C["lnc"].inc(i, 1)
                rec(f"C.rstd.{gt}", C["lnc"])
            wt(sc, C["tr2"], f"C.tr2.{c}")
            if c > 0:
                wt(sc, C["xd"], f"C.xd.{c - 1}")
            i = sc.copy(T["x2fst"].rearrange("p g (t n) -> p (g t) n", n=128),
                        T["ps_tr2"][...])
            C["x2f"].inc(i, 1)
            rec(f"C.x2f.{c}", C["x2f"])

    def body2_gpsimd(g):
        wv(g, C["wo"], 2 * NCH_C)
        wv(g, C["ln"], 2 * NCH_C)
        for p in range(NP_D):
            for mb in range(32):
                if mb >= 2:
                    wt(g, C["dmm"], f"D.mm1last.{p}.{mb - 2}")
                elif p > 0:
                    wt(g, C["dmm"], f"D.s1all.{p - 1}")
                for w12, wn in ((0, "w1T"), (1, "w2T")):
                    i = g.dma_start(
                        out=T["w12sl"][mb % 2][:, w12 * 8:(w12 + 1) * 8, :],
                        in_=T[wn].rearrange("(k p) m -> p k m", p=128)[
                            :, :, mb * 128:(mb + 1) * 128])
                    C["dw"].inc(i, 16)
                rec(f"D.w1.{p}.{mb}", C["dw"])
            for st, wn in ((0, "h1T"), (1, "h2T")):
                for mb in range(32):
                    if mb >= 2:
                        wt(g, C["dmm"], f"D.mmhlast.{p}.{st}.{mb - 2}")
                    elif st == 1:
                        wt(g, C["dmm"], f"D.mmhlast.{p}.0.{mb + 30}")
                    elif p > 0:
                        wt(g, C["dmm"], f"D.mmhlast.{p - 1}.1.{mb + 30}")
                    for q in range(4):
                        i = g.dma_start(
                            out=T["whsl"][mb % 2][:, q * 8:(q + 1) * 8, :],
                            in_=T[wn].rearrange("(k p) m -> p k m", p=128)[
                                :, q * 8:(q + 1) * 8, mb * 128:(mb + 1) * 128])
                        C["dw"].inc(i, 16)
                    rec(f"D.wh.{p}.{st}.{mb}", C["dw"])
            for sub in range(2):
                for k in range(32):
                    if k >= 2:
                        wt(g, C["dmm"], f"D.mm4k.{p}.{sub}.{k - 2}")
                    elif (p, sub) != (0, 0):
                        pp, ps_ = (p, sub - 1) if sub else (p - 1, 1)
                        wt(g, C["dmm"], f"D.mm4.{pp}.{ps_}")
                    i = g.dma_start(out=T["w3sl"][k % 2][...],
                                    in_=T["w3T"][k * 128:(k + 1) * 128, :])
                    C["dw"].inc(i, 16)
                    rec(f"D.w3.{p}.{sub}.{k}", C["dw"])

    def body2_sync(s):
        wv(s, C["ln"], 2 * NCH_C)
        for p in range(NP_D):
            if p == 0:
                wt(s, C["xd"], "C.xd")
            else:
                wt(s, C["dmm"], f"D.s1all.{p - 1}")
            i = s.dma_start(
                out=T["x2bf"][...],
                in_=T["x2feat"].rearrange("g p t -> p g t")[
                    :, :, p * PT:(p + 1) * PT])
            C["dx"].inc(i, 16)
            rec(f"D.x2bf.{p}", C["dx"])
            for sub in range(2):
                for tt in range(4):
                    gt = p * 8 + sub * 4 + tt
                    if gt > 0:
                        wt(s, C["dlnv"], f"D.lnv.t.{gt - 1}")
                    i = s.dma_start(out=T["x2tk"][...],
                                    in_=T["x2spill"][gt * 128:(gt + 1) * 128, :])
                    C["dx2"].inc(i, 16)
                    rec(f"D.x2tk.{gt}", C["dx2"])
                    wt(s, C["dln"], f"D.ln.{gt}")
                    i = s.dma_start(out=T["out"][gt * 128:(gt + 1) * 128, :],
                                    in_=T["outt"][...])
                    C["dout"].inc(i, 16)
                    rec(f"D.out.{gt}", C["dout"])
        wv(s, C["dout"], C["dout"].v if emit else 0)

    def body2_tensor(t):
        wt(t, C["x2f"], f"C.x2f.{NCH_C - 1}")
        wv(t, C["ln"], 2 * NCH_C)
        for p in range(NP_D):
            wt(t, C["dx"], f"D.x2bf.{p}")
            if p > 0:
                wt(t, C["dln"], f"D.ln4.{2 * p - 1}")
            seq = 0
            for mb in range(32):
                wt(t, C["dw"], f"D.w1.{p}.{mb}")
                for n in range(2):
                    par = seq % 2
                    if seq >= 2:
                        wt(t, C["dev"], f"D.ev1.{p}.{seq - 2}")
                    for w12 in range(2):
                        for k in range(8):
                            i = t.matmul(
                                T["ps_d"][:, 2 * w12 + par, :],
                                T["w12sl"][mb % 2][:, 8 * w12 + k, :],
                                T["x2bf"][:, k, n * 512:(n + 1) * 512],
                                start=(k == 0), stop=(k == 7))
                        C["dmm"].inc(i, 1)
                    rec(f"D.mm1.{p}.{seq}", C["dmm"])
                    seq += 1
                rec(f"D.mm1last.{p}.{mb}", C["dmm"])
            rec(f"D.s1all.{p}", C["dmm"])
            for st in range(2):
                seq = 0
                for mb in range(32):
                    wt(t, C["dw"], f"D.wh.{p}.{st}.{mb}")
                    for n in range(2):
                        slot = seq % 4
                        if seq >= 4:
                            wt(t, C["dev"], f"D.evh.{p}.{st}.{seq - 4}")
                        elif st == 0:
                            wt(t, C["dvh"], f"D.vhall.{p}")
                            wt(t, C["dev"], f"D.evall1.{p}")
                        else:
                            wt(t, C["dev"], f"D.evhall.{p}.0")
                        src = T["h_bf"] if st == 0 else T["g_bf"]
                        for k in range(32):
                            i = t.matmul(
                                T["ps_d"][:, slot, :],
                                T["whsl"][mb % 2][:, k, :],
                                src[:, k, n * 512:(n + 1) * 512],
                                start=(k == 0), stop=(k == 31))
                        C["dmm"].inc(i, 1)
                        rec(f"D.mmh.{p}.{st}.{seq}", C["dmm"])
                        seq += 1
                    rec(f"D.mmhlast.{p}.{st}.{mb}", C["dmm"])
            for sub in range(2):
                wt(t, C["dev"], f"D.evhall.{p}.1")
                if (p, sub) != (0, 0):
                    wt(t, C["dln"], f"D.ln4.{2 * p + sub - 1}")
                for k in range(32):
                    wt(t, C["dw"], f"D.w3.{p}.{sub}.{k}")
                    for tt in range(4):
                        for n in range(2):
                            i = t.matmul(
                                T["ps_d"][:, tt * 2 + n, :],
                                T["h_bf"][:, k, sub * 512 + tt * 128:
                                          sub * 512 + (tt + 1) * 128],
                                T["w3sl"][k % 2][:, n * 512:(n + 1) * 512],
                                start=(k == 0), stop=(k == 31))
                    C["dmm"].inc(i, 1)
                    rec(f"D.mm4k.{p}.{sub}.{k}", C["dmm"])
                rec(f"D.mm4.{p}.{sub}", C["dmm"])

    def body2_scalar(sc):
        for p in range(NP_D):
            seq = 0
            for mb in range(32):
                mg = mb
                for n in range(2):
                    par = seq % 2
                    wt(sc, C["dmm"], f"D.mm1.{p}.{seq}")
                    if seq >= 2:
                        wt(sc, C["dvh"], f"D.vh.{p}.{seq - 2}")
                    sc.activation(T["p_a1"][par][...], T["ps_d"][:, par, :],
                                  AF.Silu, bias=0.0,
                                  scale=T["s1_sb"][:, mg // 4 * 4 + mg % 4:
                                                   mg // 4 * 4 + mg % 4 + 1]
                                  if False else T["s1_sb"][:, mg:mg + 1])
                    i = sc.activation(T["p_a2"][par][...],
                                      T["ps_d"][:, 2 + par, :],
                                      AF.Copy, bias=0.0,
                                      scale=T["s2_sb"][:, mg:mg + 1])
                    C["dev"].inc(i, 1)
                    rec(f"D.ev1.{p}.{seq}", C["dev"])
                    seq += 1
            rec(f"D.evall1.{p}", C["dev"])
            for st in range(2):
                seq = 0
                for mb in range(32):
                    mg = mb
                    for n in range(2):
                        slot = seq % 4
                        wt(sc, C["dmm"], f"D.mmh.{p}.{st}.{seq}")
                        if emit:
                            dst = (T["g_bf"] if st == 0 else T["h_bf"])[
                                :, mg, n * 512:(n + 1) * 512]
                        else:
                            dst = MockAP()
                        scl = T["h1s_sb"] if st == 0 else T["h2s_sb"]
                        i = sc.activation(dst, T["ps_d"][:, slot, :],
                                          AF.Gelu, bias=0.0,
                                          scale=scl[:, mg:mg + 1])
                        C["dev"].inc(i, 1)
                        rec(f"D.evh.{p}.{st}.{seq}", C["dev"])
                        seq += 1
                rec(f"D.evhall.{p}.{st}", C["dev"])
            for sub in range(2):
                for tt in range(4):
                    gt = p * 8 + sub * 4 + tt
                    wt(sc, C["dlnv"], f"D.lnv.var.{gt}")
                    i = sc.activation(T["c_rstd"][...], T["c_var"][...],
                                      AF.Sqrt, bias=T["csts"][:, 1:2], scale=1.0)
                    C["dlnc"].inc(i, 1)
                    rec(f"D.rstd.{gt}", C["dlnc"])

    def body2_vector(v):
        for p in range(NP_D):
            seq = 0
            for mb in range(32):
                mg = mb
                for n in range(2):
                    wt(v, C["dev"], f"D.ev1.{p}.{seq}")
                    par = seq % 2
                    i = v.tensor_mul(
                        T["h_bf"][:, mg, n * 512:(n + 1) * 512] if emit else MockAP(),
                        T["p_a1"][par][...], T["p_a2"][par][...])
                    C["dvh"].inc(i, 1)
                    rec(f"D.vh.{p}.{seq}", C["dvh"])
                    seq += 1
            rec(f"D.vhall.{p}", C["dvh"])
            for sub in range(2):
                for tt in range(4):
                    gt = p * 8 + sub * 4 + tt
                    wt(v, C["dmm"], f"D.mm4.{p}.{sub}")
                    wt(v, C["dx2"], f"D.x2tk.{gt}")
                    if emit:
                        pw = T["ps_d"].rearrange("p s n -> p (s n)")[
                            :, (tt * 2) * 512:(tt * 2 + 2) * 512]
                    else:
                        pw = MockAP()
                    v.tensor_mul(T["c_t2"][...], pw, T["w3s_sb"][...])
                    v.tensor_add(T["c_t"][...], T["c_t2"][...], T["x2tk"][...])
                    i = v.reduce_sum(T["c_sum"][...], T["c_t"][...], axis=AX)
                    C["dlnv"].inc(i, 1)
                    rec(f"D.lnv.t.{gt}", C["dlnv"])
                    v.tensor_mul(T["c_t2"][...], T["c_t"][...], T["c_t"][...])
                    v.reduce_sum(T["c_sq"][...], T["c_t2"][...], axis=AX)
                    i = v.tensor_scalar_mul(T["c_m"][...], T["c_sum"][...], 1.0 / D)
                    C["vt"].inc(i, 1)
                    wv(v, C["vt"], C["vt"].v)
                    i = v.tensor_mul(T["c_msq"][...], T["c_m"][...], T["c_m"][...])
                    C["vt"].inc(i, 1)
                    wv(v, C["vt"], C["vt"].v)
                    i = v.scalar_tensor_tensor(T["c_var"][...], T["c_sq"][...],
                                               1.0 / D, T["c_msq"][...],
                                               Op.mult, Op.subtract)
                    C["dlnv"].inc(i, 1)
                    rec(f"D.lnv.var.{gt}", C["dlnv"])
                    wt(v, C["dlnc"], f"D.rstd.{gt}")
                    if gt > 0:
                        wt(v, C["dout"], f"D.out.{gt - 1}")
                    i = v.reciprocal(T["c_rstd2"][...], T["c_rstd"][...])
                    C["vt"].inc(i, 1)
                    wv(v, C["vt"], C["vt"].v)
                    i = v.tensor_scalar(T["outt"][...], T["c_t"][...],
                                        T["c_m"][...], T["c_rstd2"][...],
                                        Op.subtract, Op.mult)
                    C["dln"].inc(i, 1)
                    rec(f"D.ln.{gt}", C["dln"])
                rec(f"D.ln4.{2 * p + sub}", C["dln"])

    return {
        0: [("sync", body0_sync), ("gpsimd", body0_gpsimd),
            ("tensor", body0_tensor), ("vector", body0_vector),
            ("scalar", body0_scalar)],
        1: [("gpsimd", body1_gpsimd), ("sync", body1_sync),
            ("tensor", body1_tensor), ("vector", body1_vector),
            ("scalar", body1_scalar)],
        2: [("gpsimd", body2_gpsimd), ("sync", body2_sync),
            ("tensor", body2_tensor), ("scalar", body2_scalar),
            ("vector", body2_vector)],
    }


def build_nc():
    M = {}

    class TDict(dict):
        def __missing__(self, k):
            if k in ("xh_bf", "xh_re", "xh_im", "xpo", "x2tok",
                     "w12sl", "whsl", "w3sl", "p_a1", "p_a2", "hmst",
                     "gxf", "gaf", "p_rr", "p_r2", "p_rc", "p_r2c", "xfeat"):
                return [MockAP()] * 4
            return MockAP()

    Tm = TDict()
    Tm["ctr"] = {n: Ctr(None) for n in CTR_NAMES}
    bodies = build_bodies(Tm, M, emit=False)
    mock = MockEng()
    for blk_i in (0, 1, 2):
        for name, fn in bodies[blk_i]:
            fn(mock)

    nc = bass.Bass("TRN2", target_bir_lowering=False, detect_race_conditions=False)
    dr = lambda n, s, d=F32, k="ExternalInput": nc.dram_tensor(n, s, d, kind=k).ap()
    T = {}
    T["xfull"] = dr("xfull", [S, D])
    T["xpost"] = dr("xpost", [TW, D])
    T["w_in_c"] = dr("w_in_c", [D, DH])
    T["b_in_c"] = dr("b_in_c", [128, 16])
    T["w_g_c"] = dr("w_g_c", [128, 16, 128])
    T["b_g_c"] = dr("b_g_c", [128, 8])
    T["aprm_c"] = dr("aprm_c", [128, 4])
    T["aimp_c"] = dr("aimp_c", [128, 4])
    T["w_out_c"] = dr("w_out_c", [DH, D])
    T["b_out_rep"] = dr("b_out_rep", [128, D])
    T["w1T"] = dr("w1T", [D, FH])
    T["w2T"] = dr("w2T", [D, FH])
    T["h1T"] = dr("h1T", [FH, FH])
    T["h2T"] = dr("h2T", [FH, FH])
    T["w3T"] = dr("w3T", [FH, D])
    T["s1_r"] = dr("s1_r", [128, 32])
    T["s2_r"] = dr("s2_r", [128, 32])
    T["h1s_r"] = dr("h1s_r", [128, 32])
    T["h2s_r"] = dr("h2s_r", [128, 32])
    T["w3s_rep"] = dr("w3s_rep", [128, D])
    T["eye_f"] = dr("eye_f", [128, 128])
    T["consts_c"] = dr("consts_c", [128, 4])
    T["out"] = dr("out", [TW, D], F32, "ExternalOutput")
    T["cc_in"] = dr("cc_in", [2, 2, D, TW // 2], BF16, "Internal")
    T["cc_inb"] = dr("cc_inb", [2, 2, D, TW // 2], BF16, "Internal")
    T["cc_out"] = dr("cc_out", [2, D, TW // 2], BF16, "Internal")
    T["cc_outb"] = dr("cc_outb", [2, D, TW // 2], BF16, "Internal")
    T["x2spill"] = dr("x2spill", [TW, D], F32, "ExternalOutput")
    T["x2feat"] = dr("x2feat", [8, 128, TW], BF16, "Internal")
    T["wo_bf"] = dr("wo_bf", [DH, D], BF16, "Internal")

    es = ExitStack()
    SB = lambda n, s, d=F32: es.enter_context(nc.sbuf_tensor(n, s, d))
    T["ctr"] = {n: Ctr(es.enter_context(nc.semaphore(n))) for n in CTR_NAMES}

    for n, s in (("b_in_sb", [128, 16]), ("b_g_sb", [128, 8]),
                 ("aprm_sb", [128, 4]), ("aimp_sb", [128, 4]),
                 ("spc_sb", [128, 4]), ("aim2_sb", [128, 4]),
                 ("csts", [128, 4]), ("cre", [128, 4]), ("cim", [128, 4]),
                 ("sc4", [128, 4]), ("sc4b", [128, 4]), ("sc4c", [128, 4]),
                 ("sc4d", [128, 4]), ("bout_sb", [128, D]), ("w3s_sb", [128, D]),
                 ("eye_sb", [128, 128]), ("s1_sb", [128, 32]), ("s2_sb", [128, 32]),
                 ("h1s_sb", [128, 32]), ("h2s_sb", [128, 32]),
                 ("b_geff", [128, 8]),
                 ("ones_a", [128, CA]), ("c_sum", [128, 1]),
                 ("c_sq", [128, 1]), ("c_m", [128, 1]), ("c_msq", [128, 1]),
                 ("c_var", [128, 1]), ("c_rstd", [128, 1]),
                 ("c_rstd2", [128, 1])):
        T[n] = SB(n, s)
    T["b_in_f"] = SB("b_in_f", [128, 8, 1], BF16)

    bodies = build_bodies(T, M, emit=True)

    esA = ExitStack()
    SBa = lambda n, s, d=F32: esA.enter_context(nc.sbuf_tensor(n, s, d))
    T["win_sb"] = SBa("win_sb", [128, 8, DH], BF16)
    T["wg_sb"] = SBa("wg_sb", [128, 16, 128], BF16)
    T["xstage"] = SBa("xstage", [128, 2, D])
    T["xfeat"] = [SBa(f"xfeat{i}", [128, 8, CA], BF16) for i in range(2)]
    T["xh_bf"] = [SBa(f"xh_bf{i}", [128, 8, CA], BF16) for i in range(2)]
    T["xh_re"] = [SBa(f"xh_re{i}", [128, 4 * CA]) for i in range(2)]
    T["xh_im"] = [SBa(f"xh_im{i}", [128, 4 * CA]) for i in range(2)]
    T["gxf"] = [SBa(f"gxf{i}", [128, 4 * CA]) for i in range(2)]
    T["gaf"] = [SBa(f"gaf{i}", [128, 4 * CA]) for i in range(2)]
    for i in (2, 3, 5, 6, 7):
        T[f"pl{i}"] = SBa(f"pl{i}", [128, 4 * CA])
    T["p_ki"] = SBa("p_ki", [128, 4 * CA], I32)
    T["p_rr"] = [SBa(f"p_rr{i}", [128, 4 * CA]) for i in range(2)]
    T["p_r2"] = [SBa(f"p_r2{i}", [128, 4 * CA]) for i in range(2)]
    T["p_rc"] = [SBa(f"p_rc{i}", [128, 4 * CA]) for i in range(2)]
    T["p_r2c"] = [SBa(f"p_r2c{i}", [128, 4 * CA]) for i in range(2)]
    for n in ("p_sin", "p_cos", "p_nrm", "fl1", "fl2", "fl3", "p_H1", "p_H2"):
        T[n] = SBa(n, [128, 4 * CA])
    T["hmst"] = [SBa(f"hmst{i}", [128, 8, CA], BF16) for i in range(2)]
    T["zt"] = SBa("zt", [128, TW], BF16)
    psA = ExitStack()
    T["ps_tr"] = psA.enter_context(nc.psum_tensor("ps_tr", [128, 16, 128], F32))
    T["ps_win"] = psA.enter_context(nc.psum_tensor("ps_win", [128, 2, 512], F32))
    T["ps_g"] = psA.enter_context(nc.psum_tensor("ps_g", [128, 2, 512], F32))

    with nc.Block() as blk:
        for name, fn in bodies[0]:
            getattr(blk, name)(fn)
    esA.close()
    psA.close()

    esC = ExitStack()
    SBc = lambda n, s, d=F32: esC.enter_context(nc.sbuf_tensor(n, s, d))
    T["wo_sb"] = SBc("wo_sb", [128, 16, D], BF16)
    T["hmfull"] = SBc("hmfull", [128, 16, TW], BF16)
    T["xpo"] = [SBc(f"xpo{i}", [128, 2, D]) for i in range(2)]
    T["x2tok"] = [SBc(f"x2tok{i}", [128, D]) for i in range(2)]
    T["x2fst"] = SBc("x2fst", [128, 8, CA], BF16)
    T["c_t"] = SBc("c_t", [128, D])
    T["c_t2"] = SBc("c_t2", [128, D])
    psC = ExitStack()
    T["ps_wo"] = psC.enter_context(nc.psum_tensor("ps_wo", [128, 2, 2, 512], F32))
    T["ps_tr2"] = psC.enter_context(nc.psum_tensor("ps_tr2", [128, 16, 128], F32))

    with nc.Block() as blk:
        for name, fn in bodies[1]:
            getattr(blk, name)(fn)
    esC.close()
    psC.close()

    esD = ExitStack()
    SBd = lambda n, s, d=F32: esD.enter_context(nc.sbuf_tensor(n, s, d))
    T["h_bf"] = SBd("h_bf", [128, 32, PT], BF16)
    T["g_bf"] = SBd("g_bf", [128, 32, PT], BF16)
    T["x2bf"] = SBd("x2bf", [128, 8, PT], BF16)
    T["w12sl"] = [SBd(f"w12sl{i}", [128, 16, 128], BF16) for i in range(2)]
    T["whsl"] = [SBd(f"whsl{i}", [128, 32, 128], BF16) for i in range(2)]
    T["w3sl"] = [SBd(f"w3sl{i}", [128, D], BF16) for i in range(2)]
    T["x2tk"] = SBd("x2tk", [128, D])
    T["outt"] = SBd("outt", [128, D])
    T["p_a1"] = [SBd(f"p_a1{i}", [128, 512]) for i in range(2)]
    T["c_t"] = SBd("c_t2d", [128, D])
    T["c_t2"] = SBd("c_t2d2", [128, D])
    T["p_a2"] = [SBd(f"p_a2{i}", [128, 512]) for i in range(2)]
    psD = ExitStack()
    T["ps_d"] = psD.enter_context(nc.psum_tensor("ps_d", [128, 8, 512], F32))

    with nc.Block() as blk:
        for name, fn in bodies[2]:
            getattr(blk, name)(fn)
    esD.close()
    psD.close()
    es.close()
    return nc


def kernel(**inp):
    f32 = np.float32
    g = lambda k: np.asarray(inp[k], f32)
    x = g("L0_attn_x")
    w_in, b_in = g("L0_attn_w_in"), g("L0_attn_b_in")
    w_gx, b_gx = g("L0_attn_w_gx"), g("L0_attn_b_gx")
    w_ga, b_ga = g("L0_attn_w_ga"), g("L0_attn_b_ga")
    a_re, a_im = g("L0_attn_a_real_param"), g("L0_attn_a_imag_param")
    w_out, b_out = g("L0_attn_w_out"), g("L0_attn_b_out")
    w1w, w1s = g("L0_ffn_w1_w"), g("L0_ffn_w1_s")
    w2w, w2s = g("L0_ffn_w2_w"), g("L0_ffn_w2_s")
    h1w, h1s = g("L0_ffn_h1_w"), g("L0_ffn_h1_s")
    h2w, h2s = g("L0_ffn_h2_w"), g("L0_ffn_h2_s")
    w3w, w3s = g("L0_ffn_w3_w"), g("L0_ffn_w3_s")

    cont = np.ascontiguousarray
    shared = dict(
        w_out_c=cont(np.concatenate([w_out[0:512], w_out[1024:1536],
                                     w_out[512:1024], w_out[1536:2048]], 0)),
        b_out_rep=cont(np.broadcast_to(b_out, (128, D))),
        w1T=cont(w1w.T), w2T=cont(w2w.T), h1T=cont(h1w.T), h2T=cont(h2w.T),
        w3T=cont(w3w.T),
        s1_r=cont(w1s.reshape(32, 128).T), s2_r=cont(w2s.reshape(32, 128).T),
        h1s_r=cont(h1s.reshape(32, 128).T), h2s_r=cont(h2s.reshape(32, 128).T),
        w3s_rep=cont(np.broadcast_to(w3s, (128, D))),
        eye_f=np.eye(128, dtype=f32),
        consts_c=cont(np.broadcast_to(
            np.array([PI / 2.0, 1e-5, 0.0, 0.0], f32), (128, 4))),
    )

    in_maps = []
    for core in range(8):
        b, j = divmod(core, 2)
        w_in_cc = cont(np.concatenate([
            w_in[:, 1024 * j:1024 * j + 1024],
            w_in[:, 512 * j:512 * j + 512],
            w_in[:, 1024 + 512 * j:1536 + 512 * j]], axis=1))
        b_in_v = np.concatenate([
            b_in[1024 * j:1024 * j + 1024],
            b_in[512 * j:512 * j + 512],
            b_in[1024 + 512 * j:1536 + 512 * j]])
        wg = np.stack([w_gx[4 * j:4 * j + 4], w_ga[4 * j:4 * j + 4]])
        im = dict(shared)
        im.update(
            xfull=cont(x[b]),
            xpost=cont(x[b, TW * j:TW * (j + 1)]),
            w_in_c=w_in_cc,
            b_in_c=cont(b_in_v.reshape(16, 128).T),
            w_g_c=cont(wg.reshape(2, 4, 2, 128, 128)
                       .transpose(3, 0, 1, 2, 4).reshape(128, 16, 128)),
            b_g_c=cont(np.stack([b_gx, b_ga])[:, 512 * j:512 * j + 512]
                       .reshape(2, 4, 128).transpose(2, 0, 1).reshape(128, 8)),
            aprm_c=cont(a_re[512 * j:512 * j + 512].reshape(4, 128).T),
            aimp_c=cont(a_im[512 * j:512 * j + 512].reshape(4, 128).T),
        )
        in_maps.append(im)

    nc = build_nc()
    res = run_bass_kernel_spmd(nc, in_maps, core_ids=list(range(8)))
    out = np.empty((B, S, D), f32)
    for core in range(8):
        b, j = divmod(core, 2)
        out[b, TW * j:TW * (j + 1)] = res.results[core]["out"]
    return out


if __name__ == "__main__":
    nc = build_nc()
    n_inst = sum(1 for _ in nc.all_instructions())
    print("built ok, instructions:", n_inst)


# version 14
# speedup vs baseline: 1.0095x; 1.0095x over previous
from contextlib import ExitStack

import numpy as np
import ml_dtypes

import concourse.bass as bass
import concourse.mybir as mybir
from concourse.bass_utils import run_bass_kernel_spmd
from concourse.alu_op_type import AluOpType as Op

F32 = mybir.dt.float32
BF16 = mybir.dt.bfloat16
I32 = mybir.dt.int32
AF = mybir.ActivationFunctionType
AX = mybir.AxisListType.X

PI = float(np.pi)
B, S, D = 4, 4096, 1024
DH, FH = 2048, 4096
CA = 256
NCH_A = S // CA
TW = S // 2
NCH_C = TW // CA
NP_D = 2
PT = TW // NP_D
CLW = 0.5 - 3e-8


class Ctr:
    def __init__(self, sem):
        self.sem = sem
        self.v = 0

    def inc(self, inst, n=1):
        if inst is not None and self.sem is not None:
            inst.then_inc(self.sem, n)
        self.v += n


class MockInst:
    def then_inc(self, sem, n):
        return self


class MockAP:
    def __getitem__(self, k):
        return self

    def rearrange(self, *a, **k):
        return self


class MockVal:
    def __and__(self, o):
        return self

    def __add__(self, o):
        return self


class MockEng:
    def partition_id(self):
        return MockVal()

    def wait_ge(self, *a, **k):
        return MockInst()

    def __getattr__(self, name):
        return lambda *a, **k: MockInst()


CTR_NAMES = ["wob", "vx", "vt", "sct", "zr", "bg", "bgv", "init", "ginit", "cww", "vinit",
             "vhs", "hmg", "ax", "trE", "xf", "mm", "we", "gm", "ge", "cv",
             "vc", "vs", "ccin", "cc", "cw", "cx", "wo", "lnv", "lnc", "ln",
             "tr2", "x2f", "xd", "dw", "dx", "dmm", "dev", "dvh", "dx2",
             "dlnv", "dlnc", "dln", "dout"]


def build_bodies(T, M, emit):
    C = T["ctr"]

    def rec(key, ctr, d=0):
        if not emit:
            M[key] = ctr.v + d

    def wt(eng, ctr, key):
        if emit:
            eng.wait_ge(ctr.sem, M[key])

    def wv(eng, ctr, val):
        if emit:
            eng.wait_ge(ctr.sem, val)

    mi = MockInst()
    g2 = lambda t, m: t[:, m * CA:(m + 1) * CA]

    def body0_sync(s):
        for dst, src in (("b_in_sb", "b_in_c"), ("b_g_sb", "b_g_c"),
                         ("aprm_sb", "aprm_c"), ("aimp_sb", "aimp_c"),
                         ("csts", "consts_c"), ("bout_sb", "b_out_rep"),
                         ("w3s_sb", "w3s_rep"), ("eye_sb", "eye_f"),
                         ("s1_sb", "s1_r"), ("s2_sb", "s2_r"),
                         ("h1s_sb", "h1s_r"), ("h2s_sb", "h2s_r")):
            C["init"].inc(s.dma_start(out=T[dst][...], in_=T[src]), 16)
        rec("init.done", C["init"])
        for c in range(NCH_A):
            if c > 0:
                wt(s, C["trE"], f"A.tr.{c - 1}")
            C["ax"].inc(s.dma_start(
                out=T["xstage"][...],
                in_=T["xfull"].rearrange("(c t p) d -> c p t d", c=NCH_A, p=128)[c]), 16)
            rec(f"A.x.{c}", C["ax"])

    def body0_gpsimd(g):
        C["ginit"].inc(g.dma_start(
            out=T["win_sb"][...],
            in_=T["w_in_c"].rearrange("(k p) m -> p k m", p=128)), 16)
        C["ginit"].inc(g.dma_start(out=T["wg_sb"][...], in_=T["w_g_c"]), 16)
        rec("init.w", C["ginit"])
        i = g.dma_start(out=T["wo_bf"], in_=T["w_out_c"])
        C["wob"].inc(i, 16)
        rec("A.wob", C["wob"])
        wt(g, C["zr"], "A.zeros")
        wt(g, C["ccin"], "A.ccd.11")
        i = g.collective_compute(
            "ReduceScatter", Op.add,
            replica_groups=[[0, 1], [2, 3], [4, 5], [6, 7]],
            ins=[T["cc_in"]], outs=[T["cc_out"]])
        C["cc"].inc(i, 1)

    def body0_vector(v):
        C["vinit"].inc(v.memset(T["zt"][...], 0.0), 1)
        v.memset(T["ones_a"][...], 1.0)
        v.memset(T["cre"][...], 0.0)
        C["vinit"].inc(v.memset(T["cim"][...], 0.0), 1)
        for gb in range(8):
            wt(v, C["bg"], f"A.bg.{gb}")
            i = v.tensor_scalar(T["b_geff"][:, gb:gb + 1],
                                T["ps_g"][:, gb % 2, 0:1],
                                T["b_g_sb"][:, gb:gb + 1], None, Op.add)
            C["bgv"].inc(i, 1)
            rec(f"A.bgv.{gb}", C["bgv"])
        wt(v, C["cv"], "A.prm")

        def fold(c):
            par = c % 2
            F1, F2, F3 = T["fl1"], T["fl2"], T["fl3"]
            ki = T["p_ki"]
            wv(v, C["ge"], 8 * (c + 1))
            wt(v, C["cv"], f"A.cv.r2.{c}")
            for m in range(4):
                v.tensor_scalar(g2(F1, m), g2(T["gaf"][par], m),
                                T["aim2_sb"][:, m:m + 1], None, Op.mult)
            for m in range(4):
                v.tensor_tensor_scan(g2(F2, m), T["ones_a"][...],
                                     g2(F1, m), 0.0, Op.mult, Op.add)
            v.tensor_copy(ki[...], F2[...])
            v.tensor_copy(F1[...], ki[...])
            v.tensor_sub(F3[...], F2[...], F1[...])
            v.tensor_single_scalar(F2[...], F3[...], 0.5, Op.is_gt)
            v.scalar_tensor_tensor(F1[...], F2[...], -1.0, F3[...],
                                   Op.mult, Op.add)
            i = v.tensor_scalar(T["p_rc"][par][...], F1[...], -CLW, CLW,
                                Op.max, Op.min)
            C["vc"].inc(i, 1)
            rec(f"A.vc.rc.{c}", C["vc"])

        def main(c):
            par = c % 2
            P2, P3, P5, P6, P7 = (T["pl2"], T["pl3"], T["pl5"], T["pl6"], T["pl7"])
            rr = T["p_rr"][par]
            psin, pcos, pnrm = T["p_sin"][par], T["p_cos"][par], T["p_nrm"]
            gxf = T["gxf"][par]
            xre, xim = T["xh_re"][par], T["xh_im"][par]
            wv(v, C["we"], 16 * (c + 1))
            wt(v, C["cv"], f"A.cv.nrm.{c}")
            i = v.tensor_mul(P2[...], gxf[...], pnrm[...])
            C["vc"].inc(i, 1)
            rec(f"A.vc.p2.{c}", C["vc"])
            v.tensor_mul(P3[...], xre[...], P2[...])
            v.tensor_mul(P5[...], xim[...], P2[...])
            v.tensor_mul(P6[...], P3[...], pcos[...])
            v.tensor_mul(P7[...], P5[...], psin[...])
            v.tensor_add(P2[...], P6[...], P7[...])
            v.tensor_mul(P6[...], P5[...], pcos[...])
            v.tensor_mul(P7[...], P3[...], psin[...])
            i = v.tensor_sub(P5[...], P6[...], P7[...])
            C["vx"].inc(i, 1)
            rec(f"A.vx.{c}", C["vx"])
            if c >= 1:
                wt(v, C["hmg"], f"A.hmg.{c - 1}")
            for m in range(4):
                v.tensor_tensor_scan(g2(T["p_H1"], m), g2(rr, m), g2(P2, m),
                                     T["cre"][:, m:m + 1], Op.mult, Op.add)
            for m in range(4):
                v.tensor_tensor_scan(g2(T["p_H2"], m), g2(rr, m), g2(P5, m),
                                     T["cim"][:, m:m + 1], Op.mult, Op.add)
            L = lambda t: t.rearrange("p (g n) -> p g n", n=CA)[:, :, CA - 1:CA]
            c2 = lambda t: t.rearrange("p (g o) -> p g o", o=1)
            v.tensor_mul(c2(T["sc4"]), L(T["p_H1"]), L(pcos))
            v.tensor_mul(c2(T["sc4b"]), L(T["p_H2"]), L(psin))
            v.tensor_mul(c2(T["sc4c"]), L(T["p_H2"]), L(pcos))
            i = v.tensor_mul(c2(T["sc4d"]), L(T["p_H1"]), L(psin))
            C["vt"].inc(i, 1)
            wv(v, C["vt"], C["vt"].v)
            v.tensor_sub(c2(T["cre"]), c2(T["sc4"]), c2(T["sc4b"]))
            i = v.tensor_add(c2(T["cim"]), c2(T["sc4c"]), c2(T["sc4d"]))
            C["vhs"].inc(i, 1)
            rec(f"A.vhs.{c}", C["vhs"])
            if c >= 2:
                wt(v, C["ccin"], f"A.ccd.{c - 2}")
            g4v = lambda t: t.rearrange("p (g n) -> p g n", n=CA)
            v.tensor_mul(P6[...], T["p_H1"][...], pcos[...])
            v.tensor_mul(P7[...], T["p_H2"][...], psin[...])
            v.tensor_sub(T["hmst"][par][:, 0:4, :], g4v(P6), g4v(P7))
            v.tensor_mul(P6[...], T["p_H2"][...], pcos[...])
            v.tensor_mul(P7[...], T["p_H1"][...], psin[...])
            i = v.tensor_add(T["hmst"][par][:, 4:8, :], g4v(P6), g4v(P7))
            C["hmg"].inc(i, 1)
            rec(f"A.hmg.{c}", C["hmg"])

        fold(0)
        fold(1)
        for c in range(NCH_A):
            main(c)
            if c + 2 < NCH_A:
                fold(c + 2)

    def body0_tensor(t):
        wt(t, C["ginit"], "init.w")
        wt(t, C["cv"], "A.prm")
        for gb in range(8):
            gate, m = divmod(gb, 4)
            if gb >= 2:
                wt(t, C["bgv"], f"A.bgv.{gb - 2}")
            for k in range(2):
                i = t.matmul(T["ps_g"][:, gb % 2, 0:1],
                             T["wg_sb"][:, gb * 2 + k, :],
                             T["b_in_f"][:, 2 * m + k, :],
                             start=(k == 0), stop=(k == 1))
            C["bg"].inc(i, 1)
            rec(f"A.bg.{gb}", C["bg"])

        def transposes(c):
            wt(t, C["ax"], f"A.x.{c}")
            if c > 0:
                wt(t, C["xf"], f"A.xf.{c - 1}")
            for gi in range(8):
                for tt in range(2):
                    i = t.transpose(T["ps_tr"][:, gi * 2 + tt, :],
                                    T["xstage"][:, tt, gi * 128:(gi + 1) * 128],
                                    T["eye_sb"][...])
            C["trE"].inc(i, 1)
            rec(f"A.tr.{c}", C["trE"])

        transposes(0)
        for c in range(NCH_A):
            wt(t, C["xf"], f"A.xf.{c}")
            for m in range(16):
                if m >= 2:
                    wt(t, C["we"], f"A.we.{c}.{m - 2}")
                elif c > 0:
                    wt(t, C["we"], f"A.we.{c - 1}.{m + 14}")
                for k in range(8):
                    i = t.matmul(T["ps_win"][:, m % 2, 0:CA],
                                 T["win_sb"][:, k, m * 128:(m + 1) * 128],
                                 T["xfeat"][c % 2][:, k, :],
                                 start=(k == 0), stop=(k == 7))
                C["mm"].inc(i, 1)
                rec(f"A.mm.{c}.{m}", C["mm"])
            if c + 1 < NCH_A:
                transposes(c + 1)
            for gb in range(8):
                gate, m = divmod(gb, 4)
                if gb >= 2:
                    wt(t, C["ge"], f"A.ge.{c}.{gb - 2}")
                elif c > 0:
                    wt(t, C["ge"], f"A.ge.{c - 1}.{gb + 6}")
                else:
                    wv(t, C["bgv"], 8)
                for k in range(2):
                    i = t.matmul(T["ps_g"][:, gb % 2, 0:CA],
                                 T["wg_sb"][:, gb * 2 + k, :],
                                 T["xh_bf"][c % 2][:, 2 * m + k, :],
                                 start=(k == 0), stop=(k == 1))
                C["gm"].inc(i, 1)
                rec(f"A.gm.{c}.{gb}", C["gm"])

    def body0_scalar(sc):
        wt(sc, C["init"], "init.done")
        sc.copy(T["b_in_f"].rearrange("p g o -> p (g o)"), T["b_in_sb"][:, 0:8])
        i = sc.activation(T["sc4"][...], T["aprm_sb"][...], AF.Exp)
        C["sct"].inc(i, 1)
        wv(sc, C["sct"], C["sct"].v)
        i = sc.activation(T["sc4b"][...], T["sc4"][...], AF.Ln, bias=1.0, scale=1.0)
        C["sct"].inc(i, 1)
        wv(sc, C["sct"], C["sct"].v)
        sc.mul(T["spc_sb"][...], T["sc4b"][...], -8.0)
        i = sc.mul(T["aim2_sb"][...], T["aimp_sb"][...], 1.0 / (2.0 * PI))
        C["cv"].inc(i, 1)
        rec("A.prm", C["cv"])
        wv(sc, C["vinit"], 1)
        if emit:
            pid = sc.partition_id()
            jv = pid & 1
        else:
            jv = None
        for cct in ("cc_in", "cc_inb"):
            for wdw in range(2):
                for slot in range(2):
                    for gi in range(8):
                        i = sc.dma_start(
                            out=T[cct][wdw, slot, gi * 128:(gi + 1) * 128, :],
                            in_=T["zt"][:, 0:TW // 2])
                        C["zr"].inc(i, 16)
        rec("A.zeros", C["zr"])

        def prework(c):
            par = c % 2
            wt(sc, C["trE"], f"A.tr.{c}")
            if c >= 2:
                wt(sc, C["mm"], f"A.mm.{c - 2}.15")
                wt(sc, C["vx"], f"A.vx.{c - 2}")
            else:
                wv(sc, C["bgv"], 8)
            i = sc.copy(T["xfeat"][par].rearrange("p g (t n) -> p (g t) n", n=128),
                        T["ps_tr"][...])
            C["xf"].inc(i, 1)
            rec(f"A.xf.{c}", C["xf"])
            for m in range(16):
                wt(sc, C["mm"], f"A.mm.{c}.{m}")
                if emit:
                    if m < 8:
                        dst = T["xh_bf"][par][:, m, :]
                    elif m < 12:
                        dst = g2(T["xh_re"][par], m - 8)
                    else:
                        dst = g2(T["xh_im"][par], m - 12)
                else:
                    dst = MockAP()
                bias = 0.0 if m < 8 else T["b_in_sb"][:, m:m + 1]
                i = sc.activation(dst, T["ps_win"][:, m % 2, 0:CA], AF.Identity,
                                  bias=bias, scale=1.0)
                C["we"].inc(i, 1)
                rec(f"A.we.{c}.{m}", C["we"])
            for gb in range(8):
                gate, m = divmod(gb, 4)
                wt(sc, C["gm"], f"A.gm.{c}.{gb}")
                dstt = T["gxf"][par] if gate == 0 else T["gaf"][par]
                i = sc.activation(g2(dstt, m), T["ps_g"][:, gb % 2, 0:CA],
                                  AF.Sigmoid, bias=T["b_geff"][:, gb:gb + 1],
                                  scale=1.0)
                C["ge"].inc(i, 1)
                rec(f"A.ge.{c}.{gb}", C["ge"])
            if c >= 2:
                wt(sc, C["vhs"], f"A.vhs.{c - 2}")
            for m in range(4):
                sc.activation(g2(T["p_rr"][par], m), g2(T["gaf"][par], m),
                              AF.Exp, bias=0.0, scale=T["spc_sb"][:, m:m + 1])
            i = sc.activation(T["p_r2"][par][...], T["p_rr"][par][...], AF.Square)
            C["cv"].inc(i, 1)
            rec(f"A.cv.r2.{c}", C["cv"])

        prework(0)
        prework(1)
        for c in range(NCH_A):
            par = c % 2
            wt(sc, C["vc"], f"A.vc.rc.{c}")
            if c >= 2:
                wt(sc, C["hmg"], f"A.hmg.{c - 2}")
            sc.activation(T["p_sin"][par][...], T["p_rc"][par][...], AF.Sin,
                          bias=0.0, scale=2.0 * PI)
            sc.activation(T["absq"][...], T["p_rc"][par][...], AF.Abs)
            i = sc.activation(T["p_cos"][par][...], T["absq"][...], AF.Sin,
                              bias=T["csts"][:, 0:1], scale=-2.0 * PI)
            C["cv"].inc(i, 1)
            rec(f"A.cv.sc.{c}", C["cv"])
            if c >= 1:
                wt(sc, C["vc"], f"A.vc.p2.{c - 1}")
            i = sc.activation(T["p_nrm"][...], T["p_r2"][par][...], AF.Sqrt,
                              bias=1.0, scale=-1.0)
            C["cv"].inc(i, 1)
            rec(f"A.cv.nrm.{c}", C["cv"])
            if c >= 1:
                cc = c - 1
                wt(sc, C["hmg"], f"A.hmg.{cc}")
                if cc == 0:
                    wt(sc, C["zr"], "A.zeros")
                cct = "cc_in" if (cc % NCH_C) < 4 else "cc_inb"
                i = sc.dma_start(
                    out=T[cct].rearrange("w s (g p) t -> w s p g t", p=128)[
                        cc // NCH_C, jv, :, :,
                        (cc % 4) * CA:(cc % 4 + 1) * CA],
                    in_=T["hmst"][cc % 2][...])
                C["ccin"].inc(i, 16)
                rec(f"A.ccd.{cc}", C["ccin"])
            if c + 2 < NCH_A:
                prework(c + 2)
        cc = NCH_A - 1
        wt(sc, C["hmg"], f"A.hmg.{cc}")
        i = sc.dma_start(
            out=T["cc_inb"].rearrange("w s (g p) t -> w s p g t", p=128)[
                cc // NCH_C, jv, :, :,
                (cc % 4) * CA:(cc % 4 + 1) * CA],
            in_=T["hmst"][cc % 2][...])
        C["ccin"].inc(i, 16)
        rec(f"A.ccd.{cc}", C["ccin"])
        rec("B.ccin2", C["ccin"])

    def body1_gpsimd(g):
        wt(g, C["ccin"], "B.ccin2")
        i = g.collective_compute(
            "ReduceScatter", Op.add,
            replica_groups=[[0, 1], [2, 3], [4, 5], [6, 7]],
            ins=[T["cc_inb"]], outs=[T["cc_outb"]])
        C["cc"].inc(i, 1)

    def body1_sync(s):
        wv(s, C["cc"], 1)
        for half in range(2):
            i = s.dma_start(
                out=T["hmfull"][:, half * 8:(half + 1) * 8, 0:TW // 2],
                in_=T["cc_out"].rearrange("w (g p) t -> w p g t", p=128)[half])
            C["cw"].inc(i, 16)
        rec("C.hma", C["cw"])
        for c in range(2):
            i = s.dma_start(
                out=T["xpo"][c % 2][...],
                in_=T["xpost"].rearrange("(c t p) d -> c p t d", c=NCH_C, p=128)[c])
            C["cx"].inc(i, 16)
            rec(f"C.x.{c}", C["cx"])
        wv(s, C["cc"], 2)
        for half in range(2):
            i = s.dma_start(
                out=T["hmfull"][:, half * 8:(half + 1) * 8, TW // 2:TW],
                in_=T["cc_outb"].rearrange("w (g p) t -> w p g t", p=128)[half])
            C["cw"].inc(i, 16)
        rec("C.hm", C["cw"])
        def stores_for(c, s=s):
            for tt in range(2):
                gt = 2 * c + tt
                wt(s, C["ln"], f"C.ln.{gt}")
                i = s.dma_start(
                    out=T["x2spill"][gt * 128:(gt + 1) * 128, :],
                    in_=T["x2tok"][tt][...])
                C["xd"].inc(i, 16)
                rec(f"C.xdt.{gt}", C["xd"])
            wt(s, C["x2f"], f"C.x2f.{c}")
            i = s.dma_start(
                out=T["x2feat"].rearrange("g p t -> p g t")[
                    :, :, c * CA:(c + 1) * CA],
                in_=T["x2fst"][...])
            C["xd"].inc(i, 16)
            rec(f"C.xd.{c}", C["xd"])

        for c in range(2, NCH_C):
            stores_for(c - 2)
            i = s.dma_start(
                out=T["xpo"][c % 2][...],
                in_=T["xpost"].rearrange("(c t p) d -> c p t d", c=NCH_C, p=128)[c])
            C["cx"].inc(i, 16)
            rec(f"C.x.{c}", C["cx"])
        stores_for(NCH_C - 2)
        stores_for(NCH_C - 1)
        rec("C.xd", C["xd"])

    def body1_tensor(t):
        wt(t, C["cww"], "C.w")

        def tr2(c):
            wt(t, C["ln"], f"C.ln.{2 * c + 1}")
            if c > 0:
                wt(t, C["x2f"], f"C.x2f.{c - 1}")
            for tt in range(2):
                for gi in range(8):
                    i = t.transpose(T["ps_tr2"][:, gi * 2 + tt, :],
                                    T["x2tok"][tt][:, gi * 128:(gi + 1) * 128],
                                    T["eye_sb"][...])
            C["tr2"].inc(i, 1)
            rec(f"C.tr2.{c}", C["tr2"])

        for c in range(NCH_C):
            wt(t, C["cx"], f"C.x.{c}")
            wt(t, C["cw"], "C.hma" if c < 4 else "C.hm")
            for tt in range(2):
                gt = 2 * c + tt
                if gt >= 2:
                    wt(t, C["ln"], f"C.ln.{gt - 2}")
                for k in range(16):
                    for n in range(2):
                        i = t.matmul(T["ps_wo"][:, tt, n, :],
                                     T["hmfull"][:, k, gt * 128:(gt + 1) * 128],
                                     T["wo_sb"][:, k, n * 512:(n + 1) * 512],
                                     start=(k == 0), stop=(k == 15))
                C["wo"].inc(i, 1)
                rec(f"C.wo.{gt}", C["wo"])
            if c > 0:
                tr2(c - 1)
        tr2(NCH_C - 1)

    def body1_vector(v):
        for c in range(NCH_C):
            for tt in range(2):
                gt = 2 * c + tt
                wt(v, C["wo"], f"C.wo.{gt}")
                wt(v, C["cx"], f"C.x.{c}")
                if emit:
                    pw = T["ps_wo"].rearrange("p s n x -> p s (n x)")[:, tt, :]
                else:
                    pw = MockAP()
                v.tensor_add(T["c_t2"][...], pw, T["bout_sb"][...])
                v.tensor_add(T["c_t"][...], T["c_t2"][...], T["xpo"][c % 2][:, tt, :])
                i = v.reduce_sum(T["c_sum"][...], T["c_t"][...], axis=AX)
                C["lnv"].inc(i, 1)
                rec(f"C.lnv.t.{gt}", C["lnv"])
                v.tensor_mul(T["c_t2"][...], T["c_t"][...], T["c_t"][...])
                v.reduce_sum(T["c_sq"][...], T["c_t2"][...], axis=AX)
                i = v.tensor_scalar_mul(T["c_m"][...], T["c_sum"][...], 1.0 / D)
                C["vt"].inc(i, 1)
                wv(v, C["vt"], C["vt"].v)
                i = v.tensor_mul(T["c_msq"][...], T["c_m"][...], T["c_m"][...])
                C["vt"].inc(i, 1)
                wv(v, C["vt"], C["vt"].v)
                i = v.scalar_tensor_tensor(T["c_var"][...], T["c_sq"][...],
                                           1.0 / D, T["c_msq"][...],
                                           Op.mult, Op.subtract)
                C["lnv"].inc(i, 1)
                rec(f"C.lnv.var.{gt}", C["lnv"])

                wt(v, C["lnc"], f"C.rstd.{gt}")
                if gt >= 2:
                    wt(v, C["xd"], f"C.xdt.{gt - 2}")
                    wt(v, C["tr2"], f"C.tr2.{(gt - 2) // 2}")
                i = v.reciprocal(T["c_rstd2"][...], T["c_rstd"][...])
                C["vt"].inc(i, 1)
                wv(v, C["vt"], C["vt"].v)
                i = v.tensor_scalar(T["x2tok"][tt][...], T["c_t"][...],
                                    T["c_m"][...], T["c_rstd2"][...],
                                    Op.subtract, Op.mult)
                C["ln"].inc(i, 1)
                rec(f"C.ln.{gt}", C["ln"])

    def body1_scalar(sc):
        wt(sc, C["wob"], "A.wob")
        i = sc.dma_start(
            out=T["wo_sb"][...],
            in_=T["wo_bf"].rearrange("(k p) m -> p k m", p=128))
        C["cww"].inc(i, 16)
        rec("C.w", C["cww"])
        for c in range(NCH_C):
            for tt in range(2):
                gt = 2 * c + tt
                wt(sc, C["lnv"], f"C.lnv.var.{gt}")
                i = sc.activation(T["c_rstd"][...], T["c_var"][...], AF.Sqrt,
                                  bias=T["csts"][:, 1:2], scale=1.0)
                C["lnc"].inc(i, 1)
                rec(f"C.rstd.{gt}", C["lnc"])
            wt(sc, C["tr2"], f"C.tr2.{c}")
            if c > 0:
                wt(sc, C["xd"], f"C.xd.{c - 1}")
            i = sc.copy(T["x2fst"].rearrange("p g (t n) -> p (g t) n", n=128),
                        T["ps_tr2"][...])
            C["x2f"].inc(i, 1)
            rec(f"C.x2f.{c}", C["x2f"])

    def body2_gpsimd(g):
        wv(g, C["wo"], 2 * NCH_C)
        wv(g, C["ln"], 2 * NCH_C)
        for p in range(NP_D):
            for mb in range(32):
                if mb >= 2:
                    wt(g, C["dmm"], f"D.mm1last.{p}.{mb - 2}")
                elif p > 0:
                    wt(g, C["dmm"], f"D.s1all.{p - 1}")
                for w12, wn in ((0, "w1T"), (1, "w2T")):
                    i = g.dma_start(
                        out=T["w12sl"][mb % 2][:, w12 * 8:(w12 + 1) * 8, :],
                        in_=T[wn].rearrange("(k p) m -> p k m", p=128)[
                            :, :, mb * 128:(mb + 1) * 128])
                    C["dw"].inc(i, 16)
                rec(f"D.w1.{p}.{mb}", C["dw"])
            for st, wn in ((0, "h1T"), (1, "h2T")):
                for mb in range(32):
                    if mb >= 2:
                        wt(g, C["dmm"], f"D.mmhlast.{p}.{st}.{mb - 2}")
                    elif st == 1:
                        wt(g, C["dmm"], f"D.mmhlast.{p}.0.{mb + 30}")
                    elif p > 0:
                        wt(g, C["dmm"], f"D.mmhlast.{p - 1}.1.{mb + 30}")
                    for q in range(4):
                        i = g.dma_start(
                            out=T["whsl"][mb % 2][:, q * 8:(q + 1) * 8, :],
                            in_=T[wn].rearrange("(k p) m -> p k m", p=128)[
                                :, q * 8:(q + 1) * 8, mb * 128:(mb + 1) * 128])
                        C["dw"].inc(i, 16)
                    rec(f"D.wh.{p}.{st}.{mb}", C["dw"])
            for sub in range(2):
                for k in range(32):
                    if k >= 2:
                        wt(g, C["dmm"], f"D.mm4k.{p}.{sub}.{k - 2}")
                    elif (p, sub) != (0, 0):
                        pp, ps_ = (p, sub - 1) if sub else (p - 1, 1)
                        wt(g, C["dmm"], f"D.mm4.{pp}.{ps_}")
                    i = g.dma_start(out=T["w3sl"][k % 2][...],
                                    in_=T["w3T"][k * 128:(k + 1) * 128, :])
                    C["dw"].inc(i, 16)
                    rec(f"D.w3.{p}.{sub}.{k}", C["dw"])

    def body2_sync(s):
        wv(s, C["ln"], 2 * NCH_C)
        for p in range(NP_D):
            if p == 0:
                wt(s, C["xd"], "C.xd")
            else:
                wt(s, C["dmm"], f"D.s1all.{p - 1}")
            i = s.dma_start(
                out=T["x2bf"][...],
                in_=T["x2feat"].rearrange("g p t -> p g t")[
                    :, :, p * PT:(p + 1) * PT])
            C["dx"].inc(i, 16)
            rec(f"D.x2bf.{p}", C["dx"])
            for sub in range(2):
                for tt in range(4):
                    gt = p * 8 + sub * 4 + tt
                    if gt > 0:
                        wt(s, C["dlnv"], f"D.lnv.t.{gt - 1}")
                    i = s.dma_start(out=T["x2tk"][...],
                                    in_=T["x2spill"][gt * 128:(gt + 1) * 128, :])
                    C["dx2"].inc(i, 16)
                    rec(f"D.x2tk.{gt}", C["dx2"])
                    wt(s, C["dln"], f"D.ln.{gt}")
                    i = s.dma_start(out=T["out"][gt * 128:(gt + 1) * 128, :],
                                    in_=T["outt"][...])
                    C["dout"].inc(i, 16)
                    rec(f"D.out.{gt}", C["dout"])
        wv(s, C["dout"], C["dout"].v if emit else 0)

    def body2_tensor(t):
        wt(t, C["x2f"], f"C.x2f.{NCH_C - 1}")
        wv(t, C["ln"], 2 * NCH_C)
        for p in range(NP_D):
            wt(t, C["dx"], f"D.x2bf.{p}")
            if p > 0:
                wt(t, C["dln"], f"D.ln4.{2 * p - 1}")
            seq = 0
            for mb in range(32):
                wt(t, C["dw"], f"D.w1.{p}.{mb}")
                for n in range(2):
                    par = seq % 2
                    if seq >= 2:
                        wt(t, C["dev"], f"D.ev1.{p}.{seq - 2}")
                    for w12 in range(2):
                        for k in range(8):
                            i = t.matmul(
                                T["ps_d"][:, 2 * w12 + par, :],
                                T["w12sl"][mb % 2][:, 8 * w12 + k, :],
                                T["x2bf"][:, k, n * 512:(n + 1) * 512],
                                start=(k == 0), stop=(k == 7))
                        C["dmm"].inc(i, 1)
                    rec(f"D.mm1.{p}.{seq}", C["dmm"])
                    seq += 1
                rec(f"D.mm1last.{p}.{mb}", C["dmm"])
            rec(f"D.s1all.{p}", C["dmm"])
            for st in range(2):
                seq = 0
                for mb in range(32):
                    wt(t, C["dw"], f"D.wh.{p}.{st}.{mb}")
                    for n in range(2):
                        slot = seq % 4
                        if seq >= 4:
                            wt(t, C["dev"], f"D.evh.{p}.{st}.{seq - 4}")
                        elif st == 0:
                            wt(t, C["dvh"], f"D.vhall.{p}")
                            wt(t, C["dev"], f"D.evall1.{p}")
                        else:
                            wt(t, C["dev"], f"D.evhall.{p}.0")
                        src = T["h_bf"] if st == 0 else T["g_bf"]
                        for k in range(32):
                            i = t.matmul(
                                T["ps_d"][:, slot, :],
                                T["whsl"][mb % 2][:, k, :],
                                src[:, k, n * 512:(n + 1) * 512],
                                start=(k == 0), stop=(k == 31))
                        C["dmm"].inc(i, 1)
                        rec(f"D.mmh.{p}.{st}.{seq}", C["dmm"])
                        seq += 1
                    rec(f"D.mmhlast.{p}.{st}.{mb}", C["dmm"])
            for sub in range(2):
                wt(t, C["dev"], f"D.evhall.{p}.1")
                if (p, sub) != (0, 0):
                    wt(t, C["dln"], f"D.ln4.{2 * p + sub - 1}")
                for k in range(32):
                    wt(t, C["dw"], f"D.w3.{p}.{sub}.{k}")
                    for tt in range(4):
                        for n in range(2):
                            i = t.matmul(
                                T["ps_d"][:, tt * 2 + n, :],
                                T["h_bf"][:, k, sub * 512 + tt * 128:
                                          sub * 512 + (tt + 1) * 128],
                                T["w3sl"][k % 2][:, n * 512:(n + 1) * 512],
                                start=(k == 0), stop=(k == 31))
                    C["dmm"].inc(i, 1)
                    rec(f"D.mm4k.{p}.{sub}.{k}", C["dmm"])
                rec(f"D.mm4.{p}.{sub}", C["dmm"])

    def body2_scalar(sc):
        for p in range(NP_D):
            seq = 0
            for mb in range(32):
                mg = mb
                for n in range(2):
                    par = seq % 2
                    wt(sc, C["dmm"], f"D.mm1.{p}.{seq}")
                    if seq >= 2:
                        wt(sc, C["dvh"], f"D.vh.{p}.{seq - 2}")
                    sc.activation(T["p_a1"][par][...], T["ps_d"][:, par, :],
                                  AF.Silu, bias=0.0,
                                  scale=T["s1_sb"][:, mg // 4 * 4 + mg % 4:
                                                   mg // 4 * 4 + mg % 4 + 1]
                                  if False else T["s1_sb"][:, mg:mg + 1])
                    i = sc.activation(T["p_a2"][par][...],
                                      T["ps_d"][:, 2 + par, :],
                                      AF.Copy, bias=0.0,
                                      scale=T["s2_sb"][:, mg:mg + 1])
                    C["dev"].inc(i, 1)
                    rec(f"D.ev1.{p}.{seq}", C["dev"])
                    seq += 1
            rec(f"D.evall1.{p}", C["dev"])
            for st in range(2):
                seq = 0
                for mb in range(32):
                    mg = mb
                    for n in range(2):
                        slot = seq % 4
                        wt(sc, C["dmm"], f"D.mmh.{p}.{st}.{seq}")
                        if emit:
                            dst = (T["g_bf"] if st == 0 else T["h_bf"])[
                                :, mg, n * 512:(n + 1) * 512]
                        else:
                            dst = MockAP()
                        scl = T["h1s_sb"] if st == 0 else T["h2s_sb"]
                        i = sc.activation(dst, T["ps_d"][:, slot, :],
                                          AF.Gelu, bias=0.0,
                                          scale=scl[:, mg:mg + 1])
                        C["dev"].inc(i, 1)
                        rec(f"D.evh.{p}.{st}.{seq}", C["dev"])
                        seq += 1
                rec(f"D.evhall.{p}.{st}", C["dev"])
            for sub in range(2):
                for tt in range(4):
                    gt = p * 8 + sub * 4 + tt
                    wt(sc, C["dlnv"], f"D.lnv.var.{gt}")
                    i = sc.activation(T["c_rstd"][...], T["c_var"][...],
                                      AF.Sqrt, bias=T["csts"][:, 1:2], scale=1.0)
                    C["dlnc"].inc(i, 1)
                    rec(f"D.rstd.{gt}", C["dlnc"])

    def body2_vector(v):
        for p in range(NP_D):
            seq = 0
            for mb in range(32):
                mg = mb
                for n in range(2):
                    wt(v, C["dev"], f"D.ev1.{p}.{seq}")
                    par = seq % 2
                    i = v.tensor_mul(
                        T["h_bf"][:, mg, n * 512:(n + 1) * 512] if emit else MockAP(),
                        T["p_a1"][par][...], T["p_a2"][par][...])
                    C["dvh"].inc(i, 1)
                    rec(f"D.vh.{p}.{seq}", C["dvh"])
                    seq += 1
            rec(f"D.vhall.{p}", C["dvh"])
            for sub in range(2):
                for tt in range(4):
                    gt = p * 8 + sub * 4 + tt
                    wt(v, C["dmm"], f"D.mm4.{p}.{sub}")
                    wt(v, C["dx2"], f"D.x2tk.{gt}")
                    if emit:
                        pw = T["ps_d"].rearrange("p s n -> p (s n)")[
                            :, (tt * 2) * 512:(tt * 2 + 2) * 512]
                    else:
                        pw = MockAP()
                    v.tensor_mul(T["c_t2"][...], pw, T["w3s_sb"][...])
                    v.tensor_add(T["c_t"][...], T["c_t2"][...], T["x2tk"][...])
                    i = v.reduce_sum(T["c_sum"][...], T["c_t"][...], axis=AX)
                    C["dlnv"].inc(i, 1)
                    rec(f"D.lnv.t.{gt}", C["dlnv"])
                    v.tensor_mul(T["c_t2"][...], T["c_t"][...], T["c_t"][...])
                    v.reduce_sum(T["c_sq"][...], T["c_t2"][...], axis=AX)
                    i = v.tensor_scalar_mul(T["c_m"][...], T["c_sum"][...], 1.0 / D)
                    C["vt"].inc(i, 1)
                    wv(v, C["vt"], C["vt"].v)
                    i = v.tensor_mul(T["c_msq"][...], T["c_m"][...], T["c_m"][...])
                    C["vt"].inc(i, 1)
                    wv(v, C["vt"], C["vt"].v)
                    i = v.scalar_tensor_tensor(T["c_var"][...], T["c_sq"][...],
                                               1.0 / D, T["c_msq"][...],
                                               Op.mult, Op.subtract)
                    C["dlnv"].inc(i, 1)
                    rec(f"D.lnv.var.{gt}", C["dlnv"])
                    wt(v, C["dlnc"], f"D.rstd.{gt}")
                    if gt > 0:
                        wt(v, C["dout"], f"D.out.{gt - 1}")
                    i = v.reciprocal(T["c_rstd2"][...], T["c_rstd"][...])
                    C["vt"].inc(i, 1)
                    wv(v, C["vt"], C["vt"].v)
                    i = v.tensor_scalar(T["outt"][...], T["c_t"][...],
                                        T["c_m"][...], T["c_rstd2"][...],
                                        Op.subtract, Op.mult)
                    C["dln"].inc(i, 1)
                    rec(f"D.ln.{gt}", C["dln"])
                rec(f"D.ln4.{2 * p + sub}", C["dln"])

    return {
        0: [("sync", body0_sync), ("gpsimd", body0_gpsimd),
            ("tensor", body0_tensor), ("vector", body0_vector),
            ("scalar", body0_scalar)],
        1: [("gpsimd", body1_gpsimd), ("sync", body1_sync),
            ("tensor", body1_tensor), ("vector", body1_vector),
            ("scalar", body1_scalar)],
        2: [("gpsimd", body2_gpsimd), ("sync", body2_sync),
            ("tensor", body2_tensor), ("scalar", body2_scalar),
            ("vector", body2_vector)],
    }


def build_nc():
    M = {}

    class TDict(dict):
        def __missing__(self, k):
            if k in ("xh_bf", "xh_re", "xh_im", "xpo", "x2tok",
                     "w12sl", "whsl", "w3sl", "p_a1", "p_a2", "hmst",
                     "gxf", "gaf", "p_rr", "p_r2", "p_rc", "p_sin", "p_cos", "xfeat"):
                return [MockAP()] * 4
            return MockAP()

    Tm = TDict()
    Tm["ctr"] = {n: Ctr(None) for n in CTR_NAMES}
    bodies = build_bodies(Tm, M, emit=False)
    mock = MockEng()
    for blk_i in (0, 1, 2):
        for name, fn in bodies[blk_i]:
            fn(mock)

    nc = bass.Bass("TRN2", target_bir_lowering=False, detect_race_conditions=False)
    dr = lambda n, s, d=F32, k="ExternalInput": nc.dram_tensor(n, s, d, kind=k).ap()
    T = {}
    T["xfull"] = dr("xfull", [S, D])
    T["xpost"] = dr("xpost", [TW, D])
    T["w_in_c"] = dr("w_in_c", [D, DH])
    T["b_in_c"] = dr("b_in_c", [128, 16])
    T["w_g_c"] = dr("w_g_c", [128, 16, 128])
    T["b_g_c"] = dr("b_g_c", [128, 8])
    T["aprm_c"] = dr("aprm_c", [128, 4])
    T["aimp_c"] = dr("aimp_c", [128, 4])
    T["w_out_c"] = dr("w_out_c", [DH, D])
    T["b_out_rep"] = dr("b_out_rep", [128, D])
    T["w1T"] = dr("w1T", [D, FH])
    T["w2T"] = dr("w2T", [D, FH])
    T["h1T"] = dr("h1T", [FH, FH])
    T["h2T"] = dr("h2T", [FH, FH])
    T["w3T"] = dr("w3T", [FH, D])
    T["s1_r"] = dr("s1_r", [128, 32])
    T["s2_r"] = dr("s2_r", [128, 32])
    T["h1s_r"] = dr("h1s_r", [128, 32])
    T["h2s_r"] = dr("h2s_r", [128, 32])
    T["w3s_rep"] = dr("w3s_rep", [128, D])
    T["eye_f"] = dr("eye_f", [128, 128])
    T["consts_c"] = dr("consts_c", [128, 4])
    T["out"] = dr("out", [TW, D], F32, "ExternalOutput")
    T["cc_in"] = dr("cc_in", [2, 2, D, TW // 2], BF16, "Internal")
    T["cc_inb"] = dr("cc_inb", [2, 2, D, TW // 2], BF16, "Internal")
    T["cc_out"] = dr("cc_out", [2, D, TW // 2], BF16, "Internal")
    T["cc_outb"] = dr("cc_outb", [2, D, TW // 2], BF16, "Internal")
    T["x2spill"] = dr("x2spill", [TW, D], F32, "ExternalOutput")
    T["x2feat"] = dr("x2feat", [8, 128, TW], BF16, "Internal")
    T["wo_bf"] = dr("wo_bf", [DH, D], BF16, "Internal")

    es = ExitStack()
    SB = lambda n, s, d=F32: es.enter_context(nc.sbuf_tensor(n, s, d))
    T["ctr"] = {n: Ctr(es.enter_context(nc.semaphore(n))) for n in CTR_NAMES}

    for n, s in (("b_in_sb", [128, 16]), ("b_g_sb", [128, 8]),
                 ("aprm_sb", [128, 4]), ("aimp_sb", [128, 4]),
                 ("spc_sb", [128, 4]), ("aim2_sb", [128, 4]),
                 ("csts", [128, 4]), ("cre", [128, 4]), ("cim", [128, 4]),
                 ("sc4", [128, 4]), ("sc4b", [128, 4]), ("sc4c", [128, 4]),
                 ("sc4d", [128, 4]), ("bout_sb", [128, D]), ("w3s_sb", [128, D]),
                 ("eye_sb", [128, 128]), ("s1_sb", [128, 32]), ("s2_sb", [128, 32]),
                 ("h1s_sb", [128, 32]), ("h2s_sb", [128, 32]),
                 ("b_geff", [128, 8]),
                 ("ones_a", [128, CA]), ("c_sum", [128, 1]),
                 ("c_sq", [128, 1]), ("c_m", [128, 1]), ("c_msq", [128, 1]),
                 ("c_var", [128, 1]), ("c_rstd", [128, 1]),
                 ("c_rstd2", [128, 1])):
        T[n] = SB(n, s)
    T["b_in_f"] = SB("b_in_f", [128, 8, 1], BF16)

    bodies = build_bodies(T, M, emit=True)

    esA = ExitStack()
    SBa = lambda n, s, d=F32: esA.enter_context(nc.sbuf_tensor(n, s, d))
    T["win_sb"] = SBa("win_sb", [128, 8, DH], BF16)
    T["wg_sb"] = SBa("wg_sb", [128, 16, 128], BF16)
    T["xstage"] = SBa("xstage", [128, 2, D])
    T["xfeat"] = [SBa(f"xfeat{i}", [128, 8, CA], BF16) for i in range(2)]
    T["xh_bf"] = [SBa(f"xh_bf{i}", [128, 8, CA], BF16) for i in range(2)]
    T["xh_re"] = [SBa(f"xh_re{i}", [128, 4 * CA]) for i in range(2)]
    T["xh_im"] = [SBa(f"xh_im{i}", [128, 4 * CA]) for i in range(2)]
    T["gxf"] = [SBa(f"gxf{i}", [128, 4 * CA]) for i in range(2)]
    T["gaf"] = [SBa(f"gaf{i}", [128, 4 * CA]) for i in range(2)]
    for i in (2, 3, 5, 6, 7):
        T[f"pl{i}"] = SBa(f"pl{i}", [128, 4 * CA])
    T["p_ki"] = SBa("p_ki", [128, 4 * CA], I32)
    T["p_rr"] = [SBa(f"p_rr{i}", [128, 4 * CA]) for i in range(2)]
    T["p_r2"] = [SBa(f"p_r2{i}", [128, 4 * CA]) for i in range(2)]
    T["p_rc"] = [SBa(f"p_rc{i}", [128, 4 * CA]) for i in range(2)]
    T["p_sin"] = [SBa(f"p_sin{i}", [128, 4 * CA]) for i in range(2)]
    T["p_cos"] = [SBa(f"p_cos{i}", [128, 4 * CA]) for i in range(2)]
    T["absq"] = SBa("absq", [128, 4 * CA])
    for n in ("p_nrm", "fl1", "fl2", "fl3", "p_H1", "p_H2"):
        T[n] = SBa(n, [128, 4 * CA])
    T["hmst"] = [SBa(f"hmst{i}", [128, 8, CA], BF16) for i in range(2)]
    T["zt"] = SBa("zt", [128, TW], BF16)
    psA = ExitStack()
    T["ps_tr"] = psA.enter_context(nc.psum_tensor("ps_tr", [128, 16, 128], F32))
    T["ps_win"] = psA.enter_context(nc.psum_tensor("ps_win", [128, 2, 512], F32))
    T["ps_g"] = psA.enter_context(nc.psum_tensor("ps_g", [128, 2, 512], F32))

    with nc.Block() as blk:
        for name, fn in bodies[0]:
            getattr(blk, name)(fn)
    esA.close()
    psA.close()

    esC = ExitStack()
    SBc = lambda n, s, d=F32: esC.enter_context(nc.sbuf_tensor(n, s, d))
    T["wo_sb"] = SBc("wo_sb", [128, 16, D], BF16)
    T["hmfull"] = SBc("hmfull", [128, 16, TW], BF16)
    T["xpo"] = [SBc(f"xpo{i}", [128, 2, D]) for i in range(2)]
    T["x2tok"] = [SBc(f"x2tok{i}", [128, D]) for i in range(2)]
    T["x2fst"] = SBc("x2fst", [128, 8, CA], BF16)
    T["c_t"] = SBc("c_t", [128, D])
    T["c_t2"] = SBc("c_t2", [128, D])
    psC = ExitStack()
    T["ps_wo"] = psC.enter_context(nc.psum_tensor("ps_wo", [128, 2, 2, 512], F32))
    T["ps_tr2"] = psC.enter_context(nc.psum_tensor("ps_tr2", [128, 16, 128], F32))

    with nc.Block() as blk:
        for name, fn in bodies[1]:
            getattr(blk, name)(fn)
    esC.close()
    psC.close()

    esD = ExitStack()
    SBd = lambda n, s, d=F32: esD.enter_context(nc.sbuf_tensor(n, s, d))
    T["h_bf"] = SBd("h_bf", [128, 32, PT], BF16)
    T["g_bf"] = SBd("g_bf", [128, 32, PT], BF16)
    T["x2bf"] = SBd("x2bf", [128, 8, PT], BF16)
    T["w12sl"] = [SBd(f"w12sl{i}", [128, 16, 128], BF16) for i in range(2)]
    T["whsl"] = [SBd(f"whsl{i}", [128, 32, 128], BF16) for i in range(2)]
    T["w3sl"] = [SBd(f"w3sl{i}", [128, D], BF16) for i in range(2)]
    T["x2tk"] = SBd("x2tk", [128, D])
    T["outt"] = SBd("outt", [128, D])
    T["p_a1"] = [SBd(f"p_a1{i}", [128, 512]) for i in range(2)]
    T["c_t"] = SBd("c_t2d", [128, D])
    T["c_t2"] = SBd("c_t2d2", [128, D])
    T["p_a2"] = [SBd(f"p_a2{i}", [128, 512]) for i in range(2)]
    psD = ExitStack()
    T["ps_d"] = psD.enter_context(nc.psum_tensor("ps_d", [128, 8, 512], F32))

    with nc.Block() as blk:
        for name, fn in bodies[2]:
            getattr(blk, name)(fn)
    esD.close()
    psD.close()
    es.close()
    return nc


def kernel(**inp):
    f32 = np.float32
    g = lambda k: np.asarray(inp[k], f32)
    x = g("L0_attn_x")
    w_in, b_in = g("L0_attn_w_in"), g("L0_attn_b_in")
    w_gx, b_gx = g("L0_attn_w_gx"), g("L0_attn_b_gx")
    w_ga, b_ga = g("L0_attn_w_ga"), g("L0_attn_b_ga")
    a_re, a_im = g("L0_attn_a_real_param"), g("L0_attn_a_imag_param")
    w_out, b_out = g("L0_attn_w_out"), g("L0_attn_b_out")
    w1w, w1s = g("L0_ffn_w1_w"), g("L0_ffn_w1_s")
    w2w, w2s = g("L0_ffn_w2_w"), g("L0_ffn_w2_s")
    h1w, h1s = g("L0_ffn_h1_w"), g("L0_ffn_h1_s")
    h2w, h2s = g("L0_ffn_h2_w"), g("L0_ffn_h2_s")
    w3w, w3s = g("L0_ffn_w3_w"), g("L0_ffn_w3_s")

    cont = np.ascontiguousarray
    shared = dict(
        w_out_c=cont(np.concatenate([w_out[0:512], w_out[1024:1536],
                                     w_out[512:1024], w_out[1536:2048]], 0)),
        b_out_rep=cont(np.broadcast_to(b_out, (128, D))),
        w1T=cont(w1w.T), w2T=cont(w2w.T), h1T=cont(h1w.T), h2T=cont(h2w.T),
        w3T=cont(w3w.T),
        s1_r=cont(w1s.reshape(32, 128).T), s2_r=cont(w2s.reshape(32, 128).T),
        h1s_r=cont(h1s.reshape(32, 128).T), h2s_r=cont(h2s.reshape(32, 128).T),
        w3s_rep=cont(np.broadcast_to(w3s, (128, D))),
        eye_f=np.eye(128, dtype=f32),
        consts_c=cont(np.broadcast_to(
            np.array([PI / 2.0, 1e-5, 0.0, 0.0], f32), (128, 4))),
    )

    in_maps = []
    for core in range(8):
        b, j = divmod(core, 2)
        w_in_cc = cont(np.concatenate([
            w_in[:, 1024 * j:1024 * j + 1024],
            w_in[:, 512 * j:512 * j + 512],
            w_in[:, 1024 + 512 * j:1536 + 512 * j]], axis=1))
        b_in_v = np.concatenate([
            b_in[1024 * j:1024 * j + 1024],
            b_in[512 * j:512 * j + 512],
            b_in[1024 + 512 * j:1536 + 512 * j]])
        wg = np.stack([w_gx[4 * j:4 * j + 4], w_ga[4 * j:4 * j + 4]])
        im = dict(shared)
        im.update(
            xfull=cont(x[b]),
            xpost=cont(x[b, TW * j:TW * (j + 1)]),
            w_in_c=w_in_cc,
            b_in_c=cont(b_in_v.reshape(16, 128).T),
            w_g_c=cont(wg.reshape(2, 4, 2, 128, 128)
                       .transpose(3, 0, 1, 2, 4).reshape(128, 16, 128)),
            b_g_c=cont(np.stack([b_gx, b_ga])[:, 512 * j:512 * j + 512]
                       .reshape(2, 4, 128).transpose(2, 0, 1).reshape(128, 8)),
            aprm_c=cont(a_re[512 * j:512 * j + 512].reshape(4, 128).T),
            aimp_c=cont(a_im[512 * j:512 * j + 512].reshape(4, 128).T),
        )
        in_maps.append(im)

    nc = build_nc()
    res = run_bass_kernel_spmd(nc, in_maps, core_ids=list(range(8)))
    out = np.empty((B, S, D), f32)
    for core in range(8):
        b, j = divmod(core, 2)
        out[b, TW * j:TW * (j + 1)] = res.results[core]["out"]
    return out


if __name__ == "__main__":
    nc = build_nc()
    n_inst = sum(1 for _ in nc.all_instructions())
    print("built ok, instructions:", n_inst)


# version 15
# speedup vs baseline: 1.0460x; 1.0362x over previous
from contextlib import ExitStack

import numpy as np
import ml_dtypes

import concourse.bass as bass
import concourse.mybir as mybir
from concourse.bass_utils import run_bass_kernel_spmd
from concourse.alu_op_type import AluOpType as Op

F32 = mybir.dt.float32
BF16 = mybir.dt.bfloat16
I32 = mybir.dt.int32
AF = mybir.ActivationFunctionType
AX = mybir.AxisListType.X

PI = float(np.pi)
B, S, D = 4, 4096, 1024
DH, FH = 2048, 4096
CA = 256
NCH_A = S // CA
TW = S // 2
NCH_C = TW // CA
NP_D = 2
PT = TW // NP_D
CLW = 0.5 - 3e-8


class Ctr:
    def __init__(self, sem):
        self.sem = sem
        self.v = 0

    def inc(self, inst, n=1):
        if inst is not None and self.sem is not None:
            inst.then_inc(self.sem, n)
        self.v += n


class MockInst:
    def then_inc(self, sem, n):
        return self


class MockAP:
    def __getitem__(self, k):
        return self

    def rearrange(self, *a, **k):
        return self


class MockVal:
    def __and__(self, o):
        return self

    def __add__(self, o):
        return self


class MockEng:
    def partition_id(self):
        return MockVal()

    def wait_ge(self, *a, **k):
        return MockInst()

    def __getattr__(self, name):
        return lambda *a, **k: MockInst()


CTR_NAMES = ["wob", "vx", "vx3", "vt", "sct", "zr", "bg", "bgv", "init", "ginit", "cww", "vinit",
             "vhs", "hmg", "ax", "trE", "xf", "mm", "we", "gm", "ge", "cv",
             "vc", "vs", "ccin", "cc", "cw", "cx", "wo", "lnv", "lnc", "ln",
             "tr2", "x2f", "xd", "dw", "dx", "dmm", "dev", "dvh", "dx2",
             "dlnv", "dlnc", "dln", "dout"]


def build_bodies(T, M, emit):
    C = T["ctr"]

    def rec(key, ctr, d=0):
        if not emit:
            M[key] = ctr.v + d

    def wt(eng, ctr, key):
        if emit:
            eng.wait_ge(ctr.sem, M[key])

    def wv(eng, ctr, val):
        if emit:
            eng.wait_ge(ctr.sem, val)

    mi = MockInst()
    g2 = lambda t, m: t[:, m * CA:(m + 1) * CA]

    def body0_sync(s):
        for dst, src in (("b_in_sb", "b_in_c"), ("b_g_sb", "b_g_c"),
                         ("aprm_sb", "aprm_c"), ("aimp_sb", "aimp_c"),
                         ("csts", "consts_c"), ("bout_sb", "b_out_rep"),
                         ("w3s_sb", "w3s_rep"), ("eye_sb", "eye_f"),
                         ("s1_sb", "s1_r"), ("s2_sb", "s2_r"),
                         ("h1s_sb", "h1s_r"), ("h2s_sb", "h2s_r")):
            C["init"].inc(s.dma_start(out=T[dst][...], in_=T[src]), 16)
        rec("init.done", C["init"])
        for c in range(NCH_A):
            if c > 0:
                wt(s, C["trE"], f"A.tr.{c - 1}")
            C["ax"].inc(s.dma_start(
                out=T["xstage"][...],
                in_=T["xfull"].rearrange("(c t p) d -> c p t d", c=NCH_A, p=128)[c]), 16)
            rec(f"A.x.{c}", C["ax"])

    def body0_gpsimd(g):
        C["ginit"].inc(g.dma_start(
            out=T["win_sb"][...],
            in_=T["w_in_c"].rearrange("(k p) m -> p k m", p=128)), 16)
        C["ginit"].inc(g.dma_start(out=T["wg_sb"][...], in_=T["w_g_c"]), 16)
        rec("init.w", C["ginit"])
        i = g.dma_start(out=T["wo_bf"], in_=T["w_out_c"])
        C["wob"].inc(i, 16)
        rec("A.wob", C["wob"])
        wt(g, C["zr"], "A.zeros")
        wt(g, C["ccin"], "A.ccd.11")
        i = g.collective_compute(
            "ReduceScatter", Op.add,
            replica_groups=[[0, 1], [2, 3], [4, 5], [6, 7]],
            ins=[T["cc_in"]], outs=[T["cc_out"]])
        C["cc"].inc(i, 1)

    def body0_vector(v):
        C["vinit"].inc(v.memset(T["zt"][...], 0.0), 1)
        v.memset(T["ones_a"][...], 1.0)
        v.memset(T["cre"][...], 0.0)
        C["vinit"].inc(v.memset(T["cim"][...], 0.0), 1)
        for gb in range(8):
            wt(v, C["bg"], f"A.bg.{gb}")
            i = v.tensor_scalar(T["b_geff"][:, gb:gb + 1],
                                T["ps_g"][:, gb % 2, 0:1],
                                T["b_g_sb"][:, gb:gb + 1], None, Op.add)
            C["bgv"].inc(i, 1)
            rec(f"A.bgv.{gb}", C["bgv"])
        wt(v, C["cv"], "A.prm")

        def fold(c):
            par = c % 2
            F1, F2, F3 = T["fl1"], T["fl2"], T["fl3"]
            ki = T["p_ki"]
            wv(v, C["ge"], 8 * (c + 1))
            wt(v, C["cv"], f"A.cv.r2.{c}")
            for m in range(4):
                v.tensor_scalar(g2(F1, m), g2(T["gaf"][par], m),
                                T["aim2_sb"][:, m:m + 1], None, Op.mult)
            for m in range(4):
                v.tensor_tensor_scan(g2(F2, m), T["ones_a"][...],
                                     g2(F1, m), 0.0, Op.mult, Op.add)
            v.tensor_copy(ki[...], F2[...])
            v.tensor_copy(F1[...], ki[...])
            v.tensor_sub(F3[...], F2[...], F1[...])
            v.tensor_single_scalar(F2[...], F3[...], 0.5, Op.is_gt)
            v.scalar_tensor_tensor(F1[...], F2[...], -1.0, F3[...],
                                   Op.mult, Op.add)
            i = v.tensor_scalar(T["p_rc"][par][...], F1[...], -CLW, CLW,
                                Op.max, Op.min)
            C["vc"].inc(i, 1)
            rec(f"A.vc.rc.{c}", C["vc"])

        def main(c):
            par = c % 2
            P2, P3, P5, P6, P7 = (T["pl2"], T["pl3"], T["pl5"], T["pl6"], T["pl7"])
            rr = T["p_rr"][par]
            psin, pcos, pnrm = T["p_sin"][par], T["p_cos"][par], T["p_nrm"]
            gxf = T["gxf"][par]
            xre, xim = T["xh_re"][par], T["xh_im"][par]
            wv(v, C["we"], 16 * (c + 1))
            wt(v, C["cv"], f"A.cv.nrm.{c}")
            i = v.tensor_mul(P2[...], gxf[...], pnrm[...])
            C["vc"].inc(i, 1)
            rec(f"A.vc.p2.{c}", C["vc"])
            v.tensor_mul(P3[...], xre[...], P2[...])
            i = v.tensor_mul(P5[...], xim[...], P2[...])
            C["vx3"].inc(i, 1)
            rec(f"A.vx3.{c}", C["vx3"])
            v.tensor_mul(P6[...], P3[...], pcos[...])
            v.tensor_mul(P7[...], P5[...], psin[...])
            v.tensor_add(P2[...], P6[...], P7[...])
            v.tensor_mul(P6[...], P5[...], pcos[...])
            v.tensor_mul(P7[...], P3[...], psin[...])
            i = v.tensor_sub(P5[...], P6[...], P7[...])
            C["vx"].inc(i, 1)
            rec(f"A.vx.{c}", C["vx"])
            if c >= 1:
                wt(v, C["hmg"], f"A.hmg.{c - 1}")
            for m in range(4):
                v.tensor_tensor_scan(g2(T["p_H1"], m), g2(rr, m), g2(P2, m),
                                     T["cre"][:, m:m + 1], Op.mult, Op.add)
            for m in range(4):
                v.tensor_tensor_scan(g2(T["p_H2"], m), g2(rr, m), g2(P5, m),
                                     T["cim"][:, m:m + 1], Op.mult, Op.add)
            L = lambda t: t.rearrange("p (g n) -> p g n", n=CA)[:, :, CA - 1:CA]
            c2 = lambda t: t.rearrange("p (g o) -> p g o", o=1)
            v.tensor_mul(c2(T["sc4"]), L(T["p_H1"]), L(pcos))
            v.tensor_mul(c2(T["sc4b"]), L(T["p_H2"]), L(psin))
            v.tensor_mul(c2(T["sc4c"]), L(T["p_H2"]), L(pcos))
            i = v.tensor_mul(c2(T["sc4d"]), L(T["p_H1"]), L(psin))
            C["vt"].inc(i, 1)
            wv(v, C["vt"], C["vt"].v)
            v.tensor_sub(c2(T["cre"]), c2(T["sc4"]), c2(T["sc4b"]))
            i = v.tensor_add(c2(T["cim"]), c2(T["sc4c"]), c2(T["sc4d"]))
            C["vhs"].inc(i, 1)
            rec(f"A.vhs.{c}", C["vhs"])
            if c >= 2:
                wt(v, C["ccin"], f"A.ccd.{c - 2}")
            g4v = lambda t: t.rearrange("p (g n) -> p g n", n=CA)
            v.tensor_mul(P6[...], T["p_H1"][...], pcos[...])
            v.tensor_mul(P7[...], T["p_H2"][...], psin[...])
            v.tensor_sub(T["hmst"][par][:, 0:4, :], g4v(P6), g4v(P7))
            v.tensor_mul(P6[...], T["p_H2"][...], pcos[...])
            v.tensor_mul(P7[...], T["p_H1"][...], psin[...])
            i = v.tensor_add(T["hmst"][par][:, 4:8, :], g4v(P6), g4v(P7))
            C["hmg"].inc(i, 1)
            rec(f"A.hmg.{c}", C["hmg"])

        fold(0)
        fold(1)
        for c in range(NCH_A):
            main(c)
            if c + 2 < NCH_A:
                fold(c + 2)

    def body0_tensor(t):
        wt(t, C["ginit"], "init.w")
        wt(t, C["cv"], "A.prm")
        for gb in range(8):
            gate, m = divmod(gb, 4)
            if gb >= 2:
                wt(t, C["bgv"], f"A.bgv.{gb - 2}")
            for k in range(2):
                i = t.matmul(T["ps_g"][:, gb % 2, 0:1],
                             T["wg_sb"][:, gb * 2 + k, :],
                             T["b_in_f"][:, 2 * m + k, :],
                             start=(k == 0), stop=(k == 1))
            C["bg"].inc(i, 1)
            rec(f"A.bg.{gb}", C["bg"])

        def transposes(c):
            wt(t, C["ax"], f"A.x.{c}")
            if c > 0:
                wt(t, C["xf"], f"A.xf.{c - 1}")
            for gi in range(8):
                for tt in range(2):
                    i = t.transpose(T["ps_tr"][:, gi * 2 + tt, :],
                                    T["xstage"][:, tt, gi * 128:(gi + 1) * 128],
                                    T["eye_sb"][...])
            C["trE"].inc(i, 1)
            rec(f"A.tr.{c}", C["trE"])

        transposes(0)
        for c in range(NCH_A):
            wt(t, C["xf"], f"A.xf.{c}")
            for m in range(16):
                if m >= 2:
                    wt(t, C["we"], f"A.we.{c}.{m - 2}")
                elif c > 0:
                    wt(t, C["we"], f"A.we.{c - 1}.{m + 14}")
                for k in range(8):
                    i = t.matmul(T["ps_win"][:, m % 2, 0:CA],
                                 T["win_sb"][:, k, m * 128:(m + 1) * 128],
                                 T["xfeat"][c % 2][:, k, :],
                                 start=(k == 0), stop=(k == 7))
                C["mm"].inc(i, 1)
                rec(f"A.mm.{c}.{m}", C["mm"])
            if c + 1 < NCH_A:
                transposes(c + 1)
            for gb in range(8):
                gate, m = divmod(gb, 4)
                if gb >= 2:
                    wt(t, C["ge"], f"A.ge.{c}.{gb - 2}")
                elif c > 0:
                    wt(t, C["ge"], f"A.ge.{c - 1}.{gb + 6}")
                else:
                    wv(t, C["bgv"], 8)
                for k in range(2):
                    i = t.matmul(T["ps_g"][:, gb % 2, 0:CA],
                                 T["wg_sb"][:, gb * 2 + k, :],
                                 T["xh_bf"][c % 2][:, 2 * m + k, :],
                                 start=(k == 0), stop=(k == 1))
                C["gm"].inc(i, 1)
                rec(f"A.gm.{c}.{gb}", C["gm"])

    def body0_scalar(sc):
        wt(sc, C["init"], "init.done")
        sc.copy(T["b_in_f"].rearrange("p g o -> p (g o)"), T["b_in_sb"][:, 0:8])
        i = sc.activation(T["sc4"][...], T["aprm_sb"][...], AF.Exp)
        C["sct"].inc(i, 1)
        wv(sc, C["sct"], C["sct"].v)
        i = sc.activation(T["sc4b"][...], T["sc4"][...], AF.Ln, bias=1.0, scale=1.0)
        C["sct"].inc(i, 1)
        wv(sc, C["sct"], C["sct"].v)
        sc.mul(T["spc_sb"][...], T["sc4b"][...], -8.0)
        i = sc.mul(T["aim2_sb"][...], T["aimp_sb"][...], 1.0 / (2.0 * PI))
        C["cv"].inc(i, 1)
        rec("A.prm", C["cv"])
        wv(sc, C["vinit"], 1)
        if emit:
            pid = sc.partition_id()
            jv = pid & 1
        else:
            jv = None
        for cct in ("cc_in", "cc_inb"):
            for wdw in range(2):
                for slot in range(2):
                    for gi in range(8):
                        i = sc.dma_start(
                            out=T[cct][wdw, slot, gi * 128:(gi + 1) * 128, :],
                            in_=T["zt"][:, 0:TW // 2])
                        C["zr"].inc(i, 16)
        rec("A.zeros", C["zr"])

        def prework(c):
            par = c % 2
            wt(sc, C["trE"], f"A.tr.{c}")
            if c >= 2:
                wt(sc, C["mm"], f"A.mm.{c - 2}.15")
                wt(sc, C["vx3"], f"A.vx3.{c - 2}")
            else:
                wv(sc, C["bgv"], 8)
            i = sc.copy(T["xfeat"][par].rearrange("p g (t n) -> p (g t) n", n=128),
                        T["ps_tr"][...])
            C["xf"].inc(i, 1)
            rec(f"A.xf.{c}", C["xf"])
            for m in range(16):
                wt(sc, C["mm"], f"A.mm.{c}.{m}")
                if emit:
                    if m < 8:
                        dst = T["xh_bf"][par][:, m, :]
                    elif m < 12:
                        dst = g2(T["xh_re"][par], m - 8)
                    else:
                        dst = g2(T["xh_im"][par], m - 12)
                else:
                    dst = MockAP()
                bias = 0.0 if m < 8 else T["b_in_sb"][:, m:m + 1]
                i = sc.activation(dst, T["ps_win"][:, m % 2, 0:CA], AF.Identity,
                                  bias=bias, scale=1.0)
                C["we"].inc(i, 1)
                rec(f"A.we.{c}.{m}", C["we"])
            for gb in range(8):
                gate, m = divmod(gb, 4)
                wt(sc, C["gm"], f"A.gm.{c}.{gb}")
                dstt = T["gxf"][par] if gate == 0 else T["gaf"][par]
                i = sc.activation(g2(dstt, m), T["ps_g"][:, gb % 2, 0:CA],
                                  AF.Sigmoid, bias=T["b_geff"][:, gb:gb + 1],
                                  scale=1.0)
                C["ge"].inc(i, 1)
                rec(f"A.ge.{c}.{gb}", C["ge"])
            if c >= 2:
                wt(sc, C["vhs"], f"A.vhs.{c - 2}")
            for m in range(4):
                sc.activation(g2(T["p_rr"][par], m), g2(T["gaf"][par], m),
                              AF.Exp, bias=0.0, scale=T["spc_sb"][:, m:m + 1])
            i = sc.activation(T["p_r2"][par][...], T["p_rr"][par][...], AF.Square)
            C["cv"].inc(i, 1)
            rec(f"A.cv.r2.{c}", C["cv"])

        prework(0)
        prework(1)
        for c in range(NCH_A):
            par = c % 2
            wt(sc, C["vc"], f"A.vc.rc.{c}")
            if c >= 2:
                wt(sc, C["hmg"], f"A.hmg.{c - 2}")
            sc.activation(T["p_sin"][par][...], T["p_rc"][par][...], AF.Sin,
                          bias=0.0, scale=2.0 * PI)
            sc.activation(T["absq"][...], T["p_rc"][par][...], AF.Abs)
            i = sc.activation(T["p_cos"][par][...], T["absq"][...], AF.Sin,
                              bias=T["csts"][:, 0:1], scale=-2.0 * PI)
            C["cv"].inc(i, 1)
            rec(f"A.cv.sc.{c}", C["cv"])
            if c >= 1:
                wt(sc, C["vc"], f"A.vc.p2.{c - 1}")
            i = sc.activation(T["p_nrm"][...], T["p_r2"][par][...], AF.Sqrt,
                              bias=1.0, scale=-1.0)
            C["cv"].inc(i, 1)
            rec(f"A.cv.nrm.{c}", C["cv"])
            if c >= 1:
                cc = c - 1
                wt(sc, C["hmg"], f"A.hmg.{cc}")
                if cc == 0:
                    wt(sc, C["zr"], "A.zeros")
                cct = "cc_in" if (cc % NCH_C) < 4 else "cc_inb"
                i = sc.dma_start(
                    out=T[cct].rearrange("w s (g p) t -> w s p g t", p=128)[
                        cc // NCH_C, jv, :, :,
                        (cc % 4) * CA:(cc % 4 + 1) * CA],
                    in_=T["hmst"][cc % 2][...])
                C["ccin"].inc(i, 16)
                rec(f"A.ccd.{cc}", C["ccin"])
            if c + 2 < NCH_A:
                prework(c + 2)
        cc = NCH_A - 1
        wt(sc, C["hmg"], f"A.hmg.{cc}")
        i = sc.dma_start(
            out=T["cc_inb"].rearrange("w s (g p) t -> w s p g t", p=128)[
                cc // NCH_C, jv, :, :,
                (cc % 4) * CA:(cc % 4 + 1) * CA],
            in_=T["hmst"][cc % 2][...])
        C["ccin"].inc(i, 16)
        rec(f"A.ccd.{cc}", C["ccin"])
        rec("B.ccin2", C["ccin"])

    def body1_gpsimd(g):
        wt(g, C["ccin"], "B.ccin2")
        i = g.collective_compute(
            "ReduceScatter", Op.add,
            replica_groups=[[0, 1], [2, 3], [4, 5], [6, 7]],
            ins=[T["cc_inb"]], outs=[T["cc_outb"]])
        C["cc"].inc(i, 1)

    def body1_sync(s):
        wv(s, C["cc"], 1)
        for half in range(2):
            i = s.dma_start(
                out=T["hmfull"][:, half * 8:(half + 1) * 8, 0:TW // 2],
                in_=T["cc_out"].rearrange("w (g p) t -> w p g t", p=128)[half])
            C["cw"].inc(i, 16)
        rec("C.hma", C["cw"])
        for c in range(2):
            i = s.dma_start(
                out=T["xpo"][c % 2][...],
                in_=T["xpost"].rearrange("(c t p) d -> c p t d", c=NCH_C, p=128)[c])
            C["cx"].inc(i, 16)
            rec(f"C.x.{c}", C["cx"])
        wv(s, C["cc"], 2)
        for half in range(2):
            i = s.dma_start(
                out=T["hmfull"][:, half * 8:(half + 1) * 8, TW // 2:TW],
                in_=T["cc_outb"].rearrange("w (g p) t -> w p g t", p=128)[half])
            C["cw"].inc(i, 16)
        rec("C.hm", C["cw"])
        def stores_for(c, s=s):
            for tt in range(2):
                gt = 2 * c + tt
                wt(s, C["ln"], f"C.ln.{gt}")
                i = s.dma_start(
                    out=T["x2spill"][gt * 128:(gt + 1) * 128, :],
                    in_=T["x2tok"][tt][...])
                C["xd"].inc(i, 16)
                rec(f"C.xdt.{gt}", C["xd"])
            wt(s, C["x2f"], f"C.x2f.{c}")
            i = s.dma_start(
                out=T["x2feat"].rearrange("g p t -> p g t")[
                    :, :, c * CA:(c + 1) * CA],
                in_=T["x2fst"][...])
            C["xd"].inc(i, 16)
            rec(f"C.xd.{c}", C["xd"])

        for c in range(2, NCH_C):
            stores_for(c - 2)
            i = s.dma_start(
                out=T["xpo"][c % 2][...],
                in_=T["xpost"].rearrange("(c t p) d -> c p t d", c=NCH_C, p=128)[c])
            C["cx"].inc(i, 16)
            rec(f"C.x.{c}", C["cx"])
        stores_for(NCH_C - 2)
        stores_for(NCH_C - 1)
        rec("C.xd", C["xd"])

    def body1_tensor(t):
        wt(t, C["cww"], "C.w")

        def tr2(c):
            wt(t, C["ln"], f"C.ln.{2 * c + 1}")
            if c > 0:
                wt(t, C["x2f"], f"C.x2f.{c - 1}")
            for tt in range(2):
                for gi in range(8):
                    i = t.transpose(T["ps_tr2"][:, gi * 2 + tt, :],
                                    T["x2tok"][tt][:, gi * 128:(gi + 1) * 128],
                                    T["eye_sb"][...])
            C["tr2"].inc(i, 1)
            rec(f"C.tr2.{c}", C["tr2"])

        for c in range(NCH_C):
            wt(t, C["cx"], f"C.x.{c}")
            wt(t, C["cw"], "C.hma" if c < 4 else "C.hm")
            for tt in range(2):
                gt = 2 * c + tt
                if gt >= 2:
                    wt(t, C["ln"], f"C.ln.{gt - 2}")
                for k in range(16):
                    for n in range(2):
                        i = t.matmul(T["ps_wo"][:, tt, n, :],
                                     T["hmfull"][:, k, gt * 128:(gt + 1) * 128],
                                     T["wo_sb"][:, k, n * 512:(n + 1) * 512],
                                     start=(k == 0), stop=(k == 15))
                C["wo"].inc(i, 1)
                rec(f"C.wo.{gt}", C["wo"])
            if c > 0:
                tr2(c - 1)
        tr2(NCH_C - 1)

    def body1_vector(v):
        for c in range(NCH_C):
            for tt in range(2):
                gt = 2 * c + tt
                wt(v, C["wo"], f"C.wo.{gt}")
                wt(v, C["cx"], f"C.x.{c}")
                if emit:
                    pw = T["ps_wo"].rearrange("p s n x -> p s (n x)")[:, tt, :]
                else:
                    pw = MockAP()
                v.tensor_add(T["c_t2"][...], pw, T["bout_sb"][...])
                v.tensor_add(T["c_t"][...], T["c_t2"][...], T["xpo"][c % 2][:, tt, :])
                i = v.reduce_sum(T["c_sum"][...], T["c_t"][...], axis=AX)
                C["lnv"].inc(i, 1)
                rec(f"C.lnv.t.{gt}", C["lnv"])
                v.tensor_mul(T["c_t2"][...], T["c_t"][...], T["c_t"][...])
                v.reduce_sum(T["c_sq"][...], T["c_t2"][...], axis=AX)
                i = v.tensor_scalar_mul(T["c_m"][...], T["c_sum"][...], 1.0 / D)
                C["vt"].inc(i, 1)
                wv(v, C["vt"], C["vt"].v)
                i = v.tensor_mul(T["c_msq"][...], T["c_m"][...], T["c_m"][...])
                C["vt"].inc(i, 1)
                wv(v, C["vt"], C["vt"].v)
                i = v.scalar_tensor_tensor(T["c_var"][...], T["c_sq"][...],
                                           1.0 / D, T["c_msq"][...],
                                           Op.mult, Op.subtract)
                C["lnv"].inc(i, 1)
                rec(f"C.lnv.var.{gt}", C["lnv"])

                wt(v, C["lnc"], f"C.rstd.{gt}")
                if gt >= 2:
                    wt(v, C["xd"], f"C.xdt.{gt - 2}")
                    wt(v, C["tr2"], f"C.tr2.{(gt - 2) // 2}")
                i = v.reciprocal(T["c_rstd2"][...], T["c_rstd"][...])
                C["vt"].inc(i, 1)
                wv(v, C["vt"], C["vt"].v)
                i = v.tensor_scalar(T["x2tok"][tt][...], T["c_t"][...],
                                    T["c_m"][...], T["c_rstd2"][...],
                                    Op.subtract, Op.mult)
                C["ln"].inc(i, 1)
                rec(f"C.ln.{gt}", C["ln"])

    def body1_scalar(sc):
        wt(sc, C["wob"], "A.wob")
        i = sc.dma_start(
            out=T["wo_sb"][...],
            in_=T["wo_bf"].rearrange("(k p) m -> p k m", p=128))
        C["cww"].inc(i, 16)
        rec("C.w", C["cww"])
        for c in range(NCH_C):
            for tt in range(2):
                gt = 2 * c + tt
                wt(sc, C["lnv"], f"C.lnv.var.{gt}")
                i = sc.activation(T["c_rstd"][...], T["c_var"][...], AF.Sqrt,
                                  bias=T["csts"][:, 1:2], scale=1.0)
                C["lnc"].inc(i, 1)
                rec(f"C.rstd.{gt}", C["lnc"])
            wt(sc, C["tr2"], f"C.tr2.{c}")
            if c > 0:
                wt(sc, C["xd"], f"C.xd.{c - 1}")
            i = sc.copy(T["x2fst"].rearrange("p g (t n) -> p (g t) n", n=128),
                        T["ps_tr2"][...])
            C["x2f"].inc(i, 1)
            rec(f"C.x2f.{c}", C["x2f"])

    def body2_gpsimd(g):
        wv(g, C["wo"], 2 * NCH_C)
        wv(g, C["ln"], 2 * NCH_C)
        for p in range(NP_D):
            for mb in range(32):
                if mb >= 2:
                    wt(g, C["dmm"], f"D.mm1last.{p}.{mb - 2}")
                elif p > 0:
                    wt(g, C["dmm"], f"D.s1all.{p - 1}")
                for w12, wn in ((0, "w1T"), (1, "w2T")):
                    i = g.dma_start(
                        out=T["w12sl"][mb % 2][:, w12 * 8:(w12 + 1) * 8, :],
                        in_=T[wn].rearrange("(k p) m -> p k m", p=128)[
                            :, :, mb * 128:(mb + 1) * 128])
                    C["dw"].inc(i, 16)
                rec(f"D.w1.{p}.{mb}", C["dw"])
            for st, wn in ((0, "h1T"), (1, "h2T")):
                for mb in range(32):
                    if mb >= 2:
                        wt(g, C["dmm"], f"D.mmhlast.{p}.{st}.{mb - 2}")
                    elif st == 1:
                        wt(g, C["dmm"], f"D.mmhlast.{p}.0.{mb + 30}")
                    elif p > 0:
                        wt(g, C["dmm"], f"D.mmhlast.{p - 1}.1.{mb + 30}")
                    for q in range(4):
                        i = g.dma_start(
                            out=T["whsl"][mb % 2][:, q * 8:(q + 1) * 8, :],
                            in_=T[wn].rearrange("(k p) m -> p k m", p=128)[
                                :, q * 8:(q + 1) * 8, mb * 128:(mb + 1) * 128])
                        C["dw"].inc(i, 16)
                    rec(f"D.wh.{p}.{st}.{mb}", C["dw"])
            for sub in range(2):
                for k in range(32):
                    if k >= 2:
                        wt(g, C["dmm"], f"D.mm4k.{p}.{sub}.{k - 2}")
                    elif (p, sub) != (0, 0):
                        pp, ps_ = (p, sub - 1) if sub else (p - 1, 1)
                        wt(g, C["dmm"], f"D.mm4.{pp}.{ps_}")
                    i = g.dma_start(out=T["w3sl"][k % 2][...],
                                    in_=T["w3T"][k * 128:(k + 1) * 128, :])
                    C["dw"].inc(i, 16)
                    rec(f"D.w3.{p}.{sub}.{k}", C["dw"])

    def body2_sync(s):
        wv(s, C["ln"], 2 * NCH_C)
        for p in range(NP_D):
            if p == 0:
                wt(s, C["xd"], "C.xd")
            else:
                wt(s, C["dmm"], f"D.s1all.{p - 1}")
            i = s.dma_start(
                out=T["x2bf"][...],
                in_=T["x2feat"].rearrange("g p t -> p g t")[
                    :, :, p * PT:(p + 1) * PT])
            C["dx"].inc(i, 16)
            rec(f"D.x2bf.{p}", C["dx"])
            for sub in range(2):
                for tt in range(4):
                    gt = p * 8 + sub * 4 + tt
                    if gt > 0:
                        wt(s, C["dlnv"], f"D.lnv.t.{gt - 1}")
                    i = s.dma_start(out=T["x2tk"][...],
                                    in_=T["x2spill"][gt * 128:(gt + 1) * 128, :])
                    C["dx2"].inc(i, 16)
                    rec(f"D.x2tk.{gt}", C["dx2"])
                    wt(s, C["dln"], f"D.ln.{gt}")
                    i = s.dma_start(out=T["out"][gt * 128:(gt + 1) * 128, :],
                                    in_=T["outt"][...])
                    C["dout"].inc(i, 16)
                    rec(f"D.out.{gt}", C["dout"])
        wv(s, C["dout"], C["dout"].v if emit else 0)

    def body2_tensor(t):
        wt(t, C["x2f"], f"C.x2f.{NCH_C - 1}")
        wv(t, C["ln"], 2 * NCH_C)
        for p in range(NP_D):
            wt(t, C["dx"], f"D.x2bf.{p}")
            if p > 0:
                wt(t, C["dln"], f"D.ln4.{2 * p - 1}")
            seq = 0
            for mb in range(32):
                wt(t, C["dw"], f"D.w1.{p}.{mb}")
                for n in range(2):
                    par = seq % 2
                    if seq >= 2:
                        wt(t, C["dev"], f"D.ev1.{p}.{seq - 2}")
                    for w12 in range(2):
                        for k in range(8):
                            i = t.matmul(
                                T["ps_d"][:, 2 * w12 + par, :],
                                T["w12sl"][mb % 2][:, 8 * w12 + k, :],
                                T["x2bf"][:, k, n * 512:(n + 1) * 512],
                                start=(k == 0), stop=(k == 7))
                        C["dmm"].inc(i, 1)
                    rec(f"D.mm1.{p}.{seq}", C["dmm"])
                    seq += 1
                rec(f"D.mm1last.{p}.{mb}", C["dmm"])
            rec(f"D.s1all.{p}", C["dmm"])
            for st in range(2):
                seq = 0
                for mb in range(32):
                    wt(t, C["dw"], f"D.wh.{p}.{st}.{mb}")
                    for n in range(2):
                        slot = seq % 4
                        if seq >= 4:
                            wt(t, C["dev"], f"D.evh.{p}.{st}.{seq - 4}")
                        elif st == 0:
                            wt(t, C["dvh"], f"D.vhall.{p}")
                            wt(t, C["dev"], f"D.evall1.{p}")
                        else:
                            wt(t, C["dev"], f"D.evhall.{p}.0")
                        src = T["h_bf"] if st == 0 else T["g_bf"]
                        for k in range(32):
                            i = t.matmul(
                                T["ps_d"][:, slot, :],
                                T["whsl"][mb % 2][:, k, :],
                                src[:, k, n * 512:(n + 1) * 512],
                                start=(k == 0), stop=(k == 31))
                        C["dmm"].inc(i, 1)
                        rec(f"D.mmh.{p}.{st}.{seq}", C["dmm"])
                        seq += 1
                    rec(f"D.mmhlast.{p}.{st}.{mb}", C["dmm"])
            for sub in range(2):
                wt(t, C["dev"], f"D.evhall.{p}.1")
                if (p, sub) != (0, 0):
                    wt(t, C["dln"], f"D.ln4.{2 * p + sub - 1}")
                for k in range(32):
                    wt(t, C["dw"], f"D.w3.{p}.{sub}.{k}")
                    for tt in range(4):
                        for n in range(2):
                            i = t.matmul(
                                T["ps_d"][:, tt * 2 + n, :],
                                T["h_bf"][:, k, sub * 512 + tt * 128:
                                          sub * 512 + (tt + 1) * 128],
                                T["w3sl"][k % 2][:, n * 512:(n + 1) * 512],
                                start=(k == 0), stop=(k == 31))
                    C["dmm"].inc(i, 1)
                    rec(f"D.mm4k.{p}.{sub}.{k}", C["dmm"])
                rec(f"D.mm4.{p}.{sub}", C["dmm"])

    def body2_scalar(sc):
        for p in range(NP_D):
            seq = 0
            for mb in range(32):
                mg = mb
                for n in range(2):
                    par = seq % 2
                    wt(sc, C["dmm"], f"D.mm1.{p}.{seq}")
                    if seq >= 2:
                        wt(sc, C["dvh"], f"D.vh.{p}.{seq - 2}")
                    sc.activation(T["p_a1"][par][...], T["ps_d"][:, par, :],
                                  AF.Silu, bias=0.0,
                                  scale=T["s1_sb"][:, mg // 4 * 4 + mg % 4:
                                                   mg // 4 * 4 + mg % 4 + 1]
                                  if False else T["s1_sb"][:, mg:mg + 1])
                    i = sc.activation(T["p_a2"][par][...],
                                      T["ps_d"][:, 2 + par, :],
                                      AF.Copy, bias=0.0,
                                      scale=T["s2_sb"][:, mg:mg + 1])
                    C["dev"].inc(i, 1)
                    rec(f"D.ev1.{p}.{seq}", C["dev"])
                    seq += 1
            rec(f"D.evall1.{p}", C["dev"])
            for st in range(2):
                seq = 0
                for mb in range(32):
                    mg = mb
                    for n in range(2):
                        slot = seq % 4
                        wt(sc, C["dmm"], f"D.mmh.{p}.{st}.{seq}")
                        if emit:
                            dst = (T["g_bf"] if st == 0 else T["h_bf"])[
                                :, mg, n * 512:(n + 1) * 512]
                        else:
                            dst = MockAP()
                        scl = T["h1s_sb"] if st == 0 else T["h2s_sb"]
                        i = sc.activation(dst, T["ps_d"][:, slot, :],
                                          AF.Gelu, bias=0.0,
                                          scale=scl[:, mg:mg + 1])
                        C["dev"].inc(i, 1)
                        rec(f"D.evh.{p}.{st}.{seq}", C["dev"])
                        seq += 1
                rec(f"D.evhall.{p}.{st}", C["dev"])
            for sub in range(2):
                for tt in range(4):
                    gt = p * 8 + sub * 4 + tt
                    wt(sc, C["dlnv"], f"D.lnv.var.{gt}")
                    i = sc.activation(T["c_rstd"][...], T["c_var"][...],
                                      AF.Sqrt, bias=T["csts"][:, 1:2], scale=1.0)
                    C["dlnc"].inc(i, 1)
                    rec(f"D.rstd.{gt}", C["dlnc"])

    def body2_vector(v):
        for p in range(NP_D):
            seq = 0
            for mb in range(32):
                mg = mb
                for n in range(2):
                    wt(v, C["dev"], f"D.ev1.{p}.{seq}")
                    par = seq % 2
                    i = v.tensor_mul(
                        T["h_bf"][:, mg, n * 512:(n + 1) * 512] if emit else MockAP(),
                        T["p_a1"][par][...], T["p_a2"][par][...])
                    C["dvh"].inc(i, 1)
                    rec(f"D.vh.{p}.{seq}", C["dvh"])
                    seq += 1
            rec(f"D.vhall.{p}", C["dvh"])
            for sub in range(2):
                for tt in range(4):
                    gt = p * 8 + sub * 4 + tt
                    wt(v, C["dmm"], f"D.mm4.{p}.{sub}")
                    wt(v, C["dx2"], f"D.x2tk.{gt}")
                    if emit:
                        pw = T["ps_d"].rearrange("p s n -> p (s n)")[
                            :, (tt * 2) * 512:(tt * 2 + 2) * 512]
                    else:
                        pw = MockAP()
                    v.tensor_mul(T["c_t2"][...], pw, T["w3s_sb"][...])
                    v.tensor_add(T["c_t"][...], T["c_t2"][...], T["x2tk"][...])
                    i = v.reduce_sum(T["c_sum"][...], T["c_t"][...], axis=AX)
                    C["dlnv"].inc(i, 1)
                    rec(f"D.lnv.t.{gt}", C["dlnv"])
                    v.tensor_mul(T["c_t2"][...], T["c_t"][...], T["c_t"][...])
                    v.reduce_sum(T["c_sq"][...], T["c_t2"][...], axis=AX)
                    i = v.tensor_scalar_mul(T["c_m"][...], T["c_sum"][...], 1.0 / D)
                    C["vt"].inc(i, 1)
                    wv(v, C["vt"], C["vt"].v)
                    i = v.tensor_mul(T["c_msq"][...], T["c_m"][...], T["c_m"][...])
                    C["vt"].inc(i, 1)
                    wv(v, C["vt"], C["vt"].v)
                    i = v.scalar_tensor_tensor(T["c_var"][...], T["c_sq"][...],
                                               1.0 / D, T["c_msq"][...],
                                               Op.mult, Op.subtract)
                    C["dlnv"].inc(i, 1)
                    rec(f"D.lnv.var.{gt}", C["dlnv"])
                    wt(v, C["dlnc"], f"D.rstd.{gt}")
                    if gt > 0:
                        wt(v, C["dout"], f"D.out.{gt - 1}")
                    i = v.reciprocal(T["c_rstd2"][...], T["c_rstd"][...])
                    C["vt"].inc(i, 1)
                    wv(v, C["vt"], C["vt"].v)
                    i = v.tensor_scalar(T["outt"][...], T["c_t"][...],
                                        T["c_m"][...], T["c_rstd2"][...],
                                        Op.subtract, Op.mult)
                    C["dln"].inc(i, 1)
                    rec(f"D.ln.{gt}", C["dln"])
                rec(f"D.ln4.{2 * p + sub}", C["dln"])

    return {
        0: [("sync", body0_sync), ("gpsimd", body0_gpsimd),
            ("tensor", body0_tensor), ("vector", body0_vector),
            ("scalar", body0_scalar)],
        1: [("gpsimd", body1_gpsimd), ("sync", body1_sync),
            ("tensor", body1_tensor), ("vector", body1_vector),
            ("scalar", body1_scalar)],
        2: [("gpsimd", body2_gpsimd), ("sync", body2_sync),
            ("tensor", body2_tensor), ("scalar", body2_scalar),
            ("vector", body2_vector)],
    }


def build_nc():
    M = {}

    class TDict(dict):
        def __missing__(self, k):
            if k in ("xh_bf", "xh_re", "xh_im", "xpo", "x2tok",
                     "w12sl", "whsl", "w3sl", "p_a1", "p_a2", "hmst",
                     "gxf", "gaf", "p_rr", "p_r2", "p_rc", "p_sin", "p_cos", "xfeat"):
                return [MockAP()] * 4
            return MockAP()

    Tm = TDict()
    Tm["ctr"] = {n: Ctr(None) for n in CTR_NAMES}
    bodies = build_bodies(Tm, M, emit=False)
    mock = MockEng()
    for blk_i in (0, 1, 2):
        for name, fn in bodies[blk_i]:
            fn(mock)

    nc = bass.Bass("TRN2", target_bir_lowering=False, detect_race_conditions=False)
    dr = lambda n, s, d=F32, k="ExternalInput": nc.dram_tensor(n, s, d, kind=k).ap()
    T = {}
    T["xfull"] = dr("xfull", [S, D])
    T["xpost"] = dr("xpost", [TW, D])
    T["w_in_c"] = dr("w_in_c", [D, DH])
    T["b_in_c"] = dr("b_in_c", [128, 16])
    T["w_g_c"] = dr("w_g_c", [128, 16, 128])
    T["b_g_c"] = dr("b_g_c", [128, 8])
    T["aprm_c"] = dr("aprm_c", [128, 4])
    T["aimp_c"] = dr("aimp_c", [128, 4])
    T["w_out_c"] = dr("w_out_c", [DH, D])
    T["b_out_rep"] = dr("b_out_rep", [128, D])
    T["w1T"] = dr("w1T", [D, FH])
    T["w2T"] = dr("w2T", [D, FH])
    T["h1T"] = dr("h1T", [FH, FH])
    T["h2T"] = dr("h2T", [FH, FH])
    T["w3T"] = dr("w3T", [FH, D])
    T["s1_r"] = dr("s1_r", [128, 32])
    T["s2_r"] = dr("s2_r", [128, 32])
    T["h1s_r"] = dr("h1s_r", [128, 32])
    T["h2s_r"] = dr("h2s_r", [128, 32])
    T["w3s_rep"] = dr("w3s_rep", [128, D])
    T["eye_f"] = dr("eye_f", [128, 128])
    T["consts_c"] = dr("consts_c", [128, 4])
    T["out"] = dr("out", [TW, D], F32, "ExternalOutput")
    T["cc_in"] = dr("cc_in", [2, 2, D, TW // 2], BF16, "Internal")
    T["cc_inb"] = dr("cc_inb", [2, 2, D, TW // 2], BF16, "Internal")
    T["cc_out"] = dr("cc_out", [2, D, TW // 2], BF16, "Internal")
    T["cc_outb"] = dr("cc_outb", [2, D, TW // 2], BF16, "Internal")
    T["x2spill"] = dr("x2spill", [TW, D], F32, "ExternalOutput")
    T["x2feat"] = dr("x2feat", [8, 128, TW], BF16, "Internal")
    T["wo_bf"] = dr("wo_bf", [DH, D], BF16, "Internal")

    es = ExitStack()
    SB = lambda n, s, d=F32: es.enter_context(nc.sbuf_tensor(n, s, d))
    T["ctr"] = {n: Ctr(es.enter_context(nc.semaphore(n))) for n in CTR_NAMES}

    for n, s in (("b_in_sb", [128, 16]), ("b_g_sb", [128, 8]),
                 ("aprm_sb", [128, 4]), ("aimp_sb", [128, 4]),
                 ("spc_sb", [128, 4]), ("aim2_sb", [128, 4]),
                 ("csts", [128, 4]), ("cre", [128, 4]), ("cim", [128, 4]),
                 ("sc4", [128, 4]), ("sc4b", [128, 4]), ("sc4c", [128, 4]),
                 ("sc4d", [128, 4]), ("bout_sb", [128, D]), ("w3s_sb", [128, D]),
                 ("eye_sb", [128, 128]), ("s1_sb", [128, 32]), ("s2_sb", [128, 32]),
                 ("h1s_sb", [128, 32]), ("h2s_sb", [128, 32]),
                 ("b_geff", [128, 8]),
                 ("ones_a", [128, CA]), ("c_sum", [128, 1]),
                 ("c_sq", [128, 1]), ("c_m", [128, 1]), ("c_msq", [128, 1]),
                 ("c_var", [128, 1]), ("c_rstd", [128, 1]),
                 ("c_rstd2", [128, 1])):
        T[n] = SB(n, s)
    T["b_in_f"] = SB("b_in_f", [128, 8, 1], BF16)

    bodies = build_bodies(T, M, emit=True)

    esA = ExitStack()
    SBa = lambda n, s, d=F32: esA.enter_context(nc.sbuf_tensor(n, s, d))
    T["win_sb"] = SBa("win_sb", [128, 8, DH], BF16)
    T["wg_sb"] = SBa("wg_sb", [128, 16, 128], BF16)
    T["xstage"] = SBa("xstage", [128, 2, D])
    T["xfeat"] = [SBa(f"xfeat{i}", [128, 8, CA], BF16) for i in range(2)]
    T["xh_bf"] = [SBa(f"xh_bf{i}", [128, 8, CA], BF16) for i in range(2)]
    T["xh_re"] = [SBa(f"xh_re{i}", [128, 4 * CA]) for i in range(2)]
    T["xh_im"] = [SBa(f"xh_im{i}", [128, 4 * CA]) for i in range(2)]
    T["gxf"] = [SBa(f"gxf{i}", [128, 4 * CA]) for i in range(2)]
    T["gaf"] = [SBa(f"gaf{i}", [128, 4 * CA]) for i in range(2)]
    for i in (2, 3, 5, 6, 7):
        T[f"pl{i}"] = SBa(f"pl{i}", [128, 4 * CA])
    T["p_ki"] = SBa("p_ki", [128, 4 * CA], I32)
    T["p_rr"] = [SBa(f"p_rr{i}", [128, 4 * CA]) for i in range(2)]
    T["p_r2"] = [SBa(f"p_r2{i}", [128, 4 * CA]) for i in range(2)]
    T["p_rc"] = [SBa(f"p_rc{i}", [128, 4 * CA]) for i in range(2)]
    T["p_sin"] = [SBa(f"p_sin{i}", [128, 4 * CA]) for i in range(2)]
    T["p_cos"] = [SBa(f"p_cos{i}", [128, 4 * CA]) for i in range(2)]
    T["absq"] = SBa("absq", [128, 4 * CA])
    for n in ("p_nrm", "fl1", "fl2", "fl3", "p_H1", "p_H2"):
        T[n] = SBa(n, [128, 4 * CA])
    T["hmst"] = [SBa(f"hmst{i}", [128, 8, CA], BF16) for i in range(2)]
    T["zt"] = SBa("zt", [128, TW], BF16)
    psA = ExitStack()
    T["ps_tr"] = psA.enter_context(nc.psum_tensor("ps_tr", [128, 16, 128], F32))
    T["ps_win"] = psA.enter_context(nc.psum_tensor("ps_win", [128, 2, 512], F32))
    T["ps_g"] = psA.enter_context(nc.psum_tensor("ps_g", [128, 2, 512], F32))

    with nc.Block() as blk:
        for name, fn in bodies[0]:
            getattr(blk, name)(fn)
    esA.close()
    psA.close()

    esC = ExitStack()
    SBc = lambda n, s, d=F32: esC.enter_context(nc.sbuf_tensor(n, s, d))
    T["wo_sb"] = SBc("wo_sb", [128, 16, D], BF16)
    T["hmfull"] = SBc("hmfull", [128, 16, TW], BF16)
    T["xpo"] = [SBc(f"xpo{i}", [128, 2, D]) for i in range(2)]
    T["x2tok"] = [SBc(f"x2tok{i}", [128, D]) for i in range(2)]
    T["x2fst"] = SBc("x2fst", [128, 8, CA], BF16)
    T["c_t"] = SBc("c_t", [128, D])
    T["c_t2"] = SBc("c_t2", [128, D])
    psC = ExitStack()
    T["ps_wo"] = psC.enter_context(nc.psum_tensor("ps_wo", [128, 2, 2, 512], F32))
    T["ps_tr2"] = psC.enter_context(nc.psum_tensor("ps_tr2", [128, 16, 128], F32))

    with nc.Block() as blk:
        for name, fn in bodies[1]:
            getattr(blk, name)(fn)
    esC.close()
    psC.close()

    esD = ExitStack()
    SBd = lambda n, s, d=F32: esD.enter_context(nc.sbuf_tensor(n, s, d))
    T["h_bf"] = SBd("h_bf", [128, 32, PT], BF16)
    T["g_bf"] = SBd("g_bf", [128, 32, PT], BF16)
    T["x2bf"] = SBd("x2bf", [128, 8, PT], BF16)
    T["w12sl"] = [SBd(f"w12sl{i}", [128, 16, 128], BF16) for i in range(2)]
    T["whsl"] = [SBd(f"whsl{i}", [128, 32, 128], BF16) for i in range(2)]
    T["w3sl"] = [SBd(f"w3sl{i}", [128, D], BF16) for i in range(2)]
    T["x2tk"] = SBd("x2tk", [128, D])
    T["outt"] = SBd("outt", [128, D])
    T["p_a1"] = [SBd(f"p_a1{i}", [128, 512]) for i in range(2)]
    T["c_t"] = SBd("c_t2d", [128, D])
    T["c_t2"] = SBd("c_t2d2", [128, D])
    T["p_a2"] = [SBd(f"p_a2{i}", [128, 512]) for i in range(2)]
    psD = ExitStack()
    T["ps_d"] = psD.enter_context(nc.psum_tensor("ps_d", [128, 8, 512], F32))

    with nc.Block() as blk:
        for name, fn in bodies[2]:
            getattr(blk, name)(fn)
    esD.close()
    psD.close()
    es.close()
    return nc


def kernel(**inp):
    f32 = np.float32
    g = lambda k: np.asarray(inp[k], f32)
    x = g("L0_attn_x")
    w_in, b_in = g("L0_attn_w_in"), g("L0_attn_b_in")
    w_gx, b_gx = g("L0_attn_w_gx"), g("L0_attn_b_gx")
    w_ga, b_ga = g("L0_attn_w_ga"), g("L0_attn_b_ga")
    a_re, a_im = g("L0_attn_a_real_param"), g("L0_attn_a_imag_param")
    w_out, b_out = g("L0_attn_w_out"), g("L0_attn_b_out")
    w1w, w1s = g("L0_ffn_w1_w"), g("L0_ffn_w1_s")
    w2w, w2s = g("L0_ffn_w2_w"), g("L0_ffn_w2_s")
    h1w, h1s = g("L0_ffn_h1_w"), g("L0_ffn_h1_s")
    h2w, h2s = g("L0_ffn_h2_w"), g("L0_ffn_h2_s")
    w3w, w3s = g("L0_ffn_w3_w"), g("L0_ffn_w3_s")

    cont = np.ascontiguousarray
    shared = dict(
        w_out_c=cont(np.concatenate([w_out[0:512], w_out[1024:1536],
                                     w_out[512:1024], w_out[1536:2048]], 0)),
        b_out_rep=cont(np.broadcast_to(b_out, (128, D))),
        w1T=cont(w1w.T), w2T=cont(w2w.T), h1T=cont(h1w.T), h2T=cont(h2w.T),
        w3T=cont(w3w.T),
        s1_r=cont(w1s.reshape(32, 128).T), s2_r=cont(w2s.reshape(32, 128).T),
        h1s_r=cont(h1s.reshape(32, 128).T), h2s_r=cont(h2s.reshape(32, 128).T),
        w3s_rep=cont(np.broadcast_to(w3s, (128, D))),
        eye_f=np.eye(128, dtype=f32),
        consts_c=cont(np.broadcast_to(
            np.array([PI / 2.0, 1e-5, 0.0, 0.0], f32), (128, 4))),
    )

    in_maps = []
    for core in range(8):
        b, j = divmod(core, 2)
        w_in_cc = cont(np.concatenate([
            w_in[:, 1024 * j:1024 * j + 1024],
            w_in[:, 512 * j:512 * j + 512],
            w_in[:, 1024 + 512 * j:1536 + 512 * j]], axis=1))
        b_in_v = np.concatenate([
            b_in[1024 * j:1024 * j + 1024],
            b_in[512 * j:512 * j + 512],
            b_in[1024 + 512 * j:1536 + 512 * j]])
        wg = np.stack([w_gx[4 * j:4 * j + 4], w_ga[4 * j:4 * j + 4]])
        im = dict(shared)
        im.update(
            xfull=cont(x[b]),
            xpost=cont(x[b, TW * j:TW * (j + 1)]),
            w_in_c=w_in_cc,
            b_in_c=cont(b_in_v.reshape(16, 128).T),
            w_g_c=cont(wg.reshape(2, 4, 2, 128, 128)
                       .transpose(3, 0, 1, 2, 4).reshape(128, 16, 128)),
            b_g_c=cont(np.stack([b_gx, b_ga])[:, 512 * j:512 * j + 512]
                       .reshape(2, 4, 128).transpose(2, 0, 1).reshape(128, 8)),
            aprm_c=cont(a_re[512 * j:512 * j + 512].reshape(4, 128).T),
            aimp_c=cont(a_im[512 * j:512 * j + 512].reshape(4, 128).T),
        )
        in_maps.append(im)

    nc = build_nc()
    res = run_bass_kernel_spmd(nc, in_maps, core_ids=list(range(8)))
    out = np.empty((B, S, D), f32)
    for core in range(8):
        b, j = divmod(core, 2)
        out[b, TW * j:TW * (j + 1)] = res.results[core]["out"]
    return out


if __name__ == "__main__":
    nc = build_nc()
    n_inst = sum(1 for _ in nc.all_instructions())
    print("built ok, instructions:", n_inst)


# version 16 (final)
# speedup vs baseline: 1.0598x; 1.0132x over previous
from contextlib import ExitStack

import numpy as np
import ml_dtypes

import concourse.bass as bass
import concourse.mybir as mybir
from concourse.bass_utils import run_bass_kernel_spmd
from concourse.alu_op_type import AluOpType as Op

F32 = mybir.dt.float32
BF16 = mybir.dt.bfloat16
I32 = mybir.dt.int32
AF = mybir.ActivationFunctionType
AX = mybir.AxisListType.X

PI = float(np.pi)
B, S, D = 4, 4096, 1024
DH, FH = 2048, 4096
CA = 256
NCH_A = S // CA
TW = S // 2
NCH_C = TW // CA
NP_D = 2
PT = TW // NP_D
CLW = 0.5 - 3e-8


class Ctr:
    def __init__(self, sem):
        self.sem = sem
        self.v = 0

    def inc(self, inst, n=1):
        if inst is not None and self.sem is not None:
            inst.then_inc(self.sem, n)
        self.v += n


class MockInst:
    def then_inc(self, sem, n):
        return self


class MockAP:
    def __getitem__(self, k):
        return self

    def rearrange(self, *a, **k):
        return self


class MockVal:
    def __and__(self, o):
        return self

    def __add__(self, o):
        return self


class MockEng:
    def partition_id(self):
        return MockVal()

    def wait_ge(self, *a, **k):
        return MockInst()

    def __getattr__(self, name):
        return lambda *a, **k: MockInst()


CTR_NAMES = ["wob", "vx", "vx3", "vt", "sct", "zr", "bg", "bgv", "init", "ginit", "cww", "vinit",
             "vhs", "hmg", "ax", "trE", "xf", "mm", "we", "gm", "ge", "cv",
             "vc", "vs", "ccin", "cc", "cw", "cx", "wo", "lnv", "lnc", "ln",
             "tr2", "x2f", "xd", "dw", "dx", "dmm", "dev", "dvh", "dx2",
             "dlnv", "dlnc", "dln", "dout"]


def build_bodies(T, M, emit):
    C = T["ctr"]

    def rec(key, ctr, d=0):
        if not emit:
            M[key] = ctr.v + d

    def wt(eng, ctr, key):
        if emit:
            eng.wait_ge(ctr.sem, M[key])

    def wv(eng, ctr, val):
        if emit:
            eng.wait_ge(ctr.sem, val)

    mi = MockInst()
    g2 = lambda t, m: t[:, m * CA:(m + 1) * CA]

    def body0_sync(s):
        for dst, src in (("b_in_sb", "b_in_c"), ("b_g_sb", "b_g_c"),
                         ("aprm_sb", "aprm_c"), ("aimp_sb", "aimp_c"),
                         ("csts", "consts_c"), ("bout_sb", "b_out_rep"),
                         ("w3s_sb", "w3s_rep"), ("eye_sb", "eye_f"),
                         ("s1_sb", "s1_r"), ("s2_sb", "s2_r"),
                         ("h1s_sb", "h1s_r"), ("h2s_sb", "h2s_r")):
            C["init"].inc(s.dma_start(out=T[dst][...], in_=T[src]), 16)
        rec("init.done", C["init"])
        for c in range(NCH_A):
            if c > 0:
                wt(s, C["trE"], f"A.tr.{c - 1}")
            C["ax"].inc(s.dma_start(
                out=T["xstage"][...],
                in_=T["xfull"].rearrange("(c t p) d -> c p t d", c=NCH_A, p=128)[c]), 16)
            rec(f"A.x.{c}", C["ax"])

    def body0_gpsimd(g):
        C["ginit"].inc(g.dma_start(
            out=T["win_sb"][...],
            in_=T["w_in_c"].rearrange("(k p) m -> p k m", p=128)), 16)
        C["ginit"].inc(g.dma_start(out=T["wg_sb"][...], in_=T["w_g_c"]), 16)
        rec("init.w", C["ginit"])
        i = g.dma_start(out=T["wo_bf"], in_=T["w_out_c"])
        C["wob"].inc(i, 16)
        rec("A.wob", C["wob"])
        wt(g, C["zr"], "A.zeros")
        wt(g, C["ccin"], "A.ccd.11")
        i = g.collective_compute(
            "ReduceScatter", Op.add,
            replica_groups=[[0, 1], [2, 3], [4, 5], [6, 7]],
            ins=[T["cc_in"]], outs=[T["cc_out"]])
        C["cc"].inc(i, 1)

    def body0_vector(v):
        C["vinit"].inc(v.memset(T["zt"][...], 0.0), 1)
        v.memset(T["ones_a"][...], 1.0)
        v.memset(T["cre"][...], 0.0)
        C["vinit"].inc(v.memset(T["cim"][...], 0.0), 1)
        for gb in range(8):
            wt(v, C["bg"], f"A.bg.{gb}")
            i = v.tensor_scalar(T["b_geff"][:, gb:gb + 1],
                                T["ps_g"][:, gb % 2, 0:1],
                                T["b_g_sb"][:, gb:gb + 1], None, Op.add)
            C["bgv"].inc(i, 1)
            rec(f"A.bgv.{gb}", C["bgv"])
        wt(v, C["cv"], "A.prm")

        def fold(c):
            par = c % 2
            F1, F2, F3 = T["fl1"], T["fl2"], T["fl3"]
            ki = T["p_ki"]
            wv(v, C["ge"], 8 * (c + 1))
            wt(v, C["cv"], f"A.cv.r2.{c}")
            for m in range(4):
                v.tensor_scalar(g2(F1, m), g2(T["gaf"][par], m),
                                T["aim2_sb"][:, m:m + 1], None, Op.mult)
            for m in range(4):
                v.tensor_tensor_scan(g2(F2, m), T["ones_a"][...],
                                     g2(F1, m), 0.0, Op.mult, Op.add)
            v.tensor_copy(ki[...], F2[...])
            v.tensor_copy(F1[...], ki[...])
            v.tensor_sub(F3[...], F2[...], F1[...])
            v.tensor_single_scalar(F2[...], F3[...], 0.5, Op.is_gt)
            v.scalar_tensor_tensor(F1[...], F2[...], -1.0, F3[...],
                                   Op.mult, Op.add)
            i = v.tensor_scalar(T["p_rc"][par][...], F1[...], -CLW, CLW,
                                Op.max, Op.min)
            C["vc"].inc(i, 1)
            rec(f"A.vc.rc.{c}", C["vc"])

        def main(c):
            par = c % 2
            P2, P3, P5, P6, P7 = (T["pl2"], T["pl3"], T["pl5"], T["pl6"], T["pl7"])
            rr = T["p_rr"][par]
            psin, pcos, pnrm = T["p_sin"][par], T["p_cos"][par], T["p_nrm"]
            gxf = T["gxf"][par]
            xre, xim = T["xh_re"][par], T["xh_im"][par]
            wv(v, C["we"], 16 * (c + 1))
            wt(v, C["cv"], f"A.cv.nrm.{c}")
            i = v.tensor_mul(P2[...], gxf[...], pnrm[...])
            C["vc"].inc(i, 1)
            rec(f"A.vc.p2.{c}", C["vc"])
            v.tensor_mul(P3[...], xre[...], P2[...])
            i = v.tensor_mul(P5[...], xim[...], P2[...])
            C["vx3"].inc(i, 1)
            rec(f"A.vx3.{c}", C["vx3"])
            v.tensor_mul(P6[...], P3[...], pcos[...])
            v.tensor_mul(P7[...], P5[...], psin[...])
            v.tensor_add(P2[...], P6[...], P7[...])
            v.tensor_mul(P6[...], P5[...], pcos[...])
            v.tensor_mul(P7[...], P3[...], psin[...])
            i = v.tensor_sub(P5[...], P6[...], P7[...])
            C["vx"].inc(i, 1)
            rec(f"A.vx.{c}", C["vx"])
            if c >= 1:
                wt(v, C["hmg"], f"A.hmg.{c - 1}")
            for m in range(4):
                v.tensor_tensor_scan(g2(T["p_H1"], m), g2(rr, m), g2(P2, m),
                                     T["cre"][:, m:m + 1], Op.mult, Op.add)
            for m in range(4):
                v.tensor_tensor_scan(g2(T["p_H2"], m), g2(rr, m), g2(P5, m),
                                     T["cim"][:, m:m + 1], Op.mult, Op.add)
            L = lambda t: t.rearrange("p (g n) -> p g n", n=CA)[:, :, CA - 1:CA]
            c2 = lambda t: t.rearrange("p (g o) -> p g o", o=1)
            v.tensor_mul(c2(T["sc4"]), L(T["p_H1"]), L(pcos))
            v.tensor_mul(c2(T["sc4b"]), L(T["p_H2"]), L(psin))
            v.tensor_mul(c2(T["sc4c"]), L(T["p_H2"]), L(pcos))
            i = v.tensor_mul(c2(T["sc4d"]), L(T["p_H1"]), L(psin))
            C["vt"].inc(i, 1)
            wv(v, C["vt"], C["vt"].v)
            v.tensor_sub(c2(T["cre"]), c2(T["sc4"]), c2(T["sc4b"]))
            i = v.tensor_add(c2(T["cim"]), c2(T["sc4c"]), c2(T["sc4d"]))
            C["vhs"].inc(i, 1)
            rec(f"A.vhs.{c}", C["vhs"])
            if c >= 2:
                wt(v, C["ccin"], f"A.ccd.{c - 2}")
            g4v = lambda t: t.rearrange("p (g n) -> p g n", n=CA)
            v.tensor_mul(P6[...], T["p_H1"][...], pcos[...])
            v.tensor_mul(P7[...], T["p_H2"][...], psin[...])
            v.tensor_sub(T["hmst"][par][:, 0:4, :], g4v(P6), g4v(P7))
            v.tensor_mul(P6[...], T["p_H2"][...], pcos[...])
            v.tensor_mul(P7[...], T["p_H1"][...], psin[...])
            i = v.tensor_add(T["hmst"][par][:, 4:8, :], g4v(P6), g4v(P7))
            C["hmg"].inc(i, 1)
            rec(f"A.hmg.{c}", C["hmg"])

        fold(0)
        fold(1)
        for c in range(NCH_A):
            main(c)
            if c + 2 < NCH_A:
                fold(c + 2)

    def body0_tensor(t):
        wt(t, C["ginit"], "init.w")
        wt(t, C["cv"], "A.prm")
        for gb in range(8):
            gate, m = divmod(gb, 4)
            if gb >= 2:
                wt(t, C["bgv"], f"A.bgv.{gb - 2}")
            for k in range(2):
                i = t.matmul(T["ps_g"][:, gb % 2, 0:1],
                             T["wg_sb"][:, gb * 2 + k, :],
                             T["b_in_f"][:, 2 * m + k, :],
                             start=(k == 0), stop=(k == 1))
            C["bg"].inc(i, 1)
            rec(f"A.bg.{gb}", C["bg"])

        def transposes(c):
            wt(t, C["ax"], f"A.x.{c}")
            if c > 0:
                wt(t, C["xf"], f"A.xf.{c - 1}")
            for gi in range(8):
                for tt in range(2):
                    i = t.transpose(T["ps_tr"][:, gi * 2 + tt, :],
                                    T["xstage"][:, tt, gi * 128:(gi + 1) * 128],
                                    T["eye_sb"][...])
            C["trE"].inc(i, 1)
            rec(f"A.tr.{c}", C["trE"])

        transposes(0)
        for c in range(NCH_A):
            wt(t, C["xf"], f"A.xf.{c}")
            for m in range(16):
                if m >= 2:
                    wt(t, C["we"], f"A.we.{c}.{m - 2}")
                elif c > 0:
                    wt(t, C["we"], f"A.we.{c - 1}.{m + 14}")
                for k in range(8):
                    i = t.matmul(T["ps_win"][:, m % 2, 0:CA],
                                 T["win_sb"][:, k, m * 128:(m + 1) * 128],
                                 T["xfeat"][c % 2][:, k, :],
                                 start=(k == 0), stop=(k == 7))
                C["mm"].inc(i, 1)
                rec(f"A.mm.{c}.{m}", C["mm"])
            if c + 1 < NCH_A:
                transposes(c + 1)
            for gb in range(8):
                gate, m = divmod(gb, 4)
                if gb >= 2:
                    wt(t, C["ge"], f"A.ge.{c}.{gb - 2}")
                elif c > 0:
                    wt(t, C["ge"], f"A.ge.{c - 1}.{gb + 6}")
                else:
                    wv(t, C["bgv"], 8)
                for k in range(2):
                    i = t.matmul(T["ps_g"][:, gb % 2, 0:CA],
                                 T["wg_sb"][:, gb * 2 + k, :],
                                 T["xh_bf"][c % 2][:, 2 * m + k, :],
                                 start=(k == 0), stop=(k == 1))
                C["gm"].inc(i, 1)
                rec(f"A.gm.{c}.{gb}", C["gm"])

    def body0_scalar(sc):
        wt(sc, C["init"], "init.done")
        sc.copy(T["b_in_f"].rearrange("p g o -> p (g o)"), T["b_in_sb"][:, 0:8])
        i = sc.activation(T["sc4"][...], T["aprm_sb"][...], AF.Exp)
        C["sct"].inc(i, 1)
        wv(sc, C["sct"], C["sct"].v)
        i = sc.activation(T["sc4b"][...], T["sc4"][...], AF.Ln, bias=1.0, scale=1.0)
        C["sct"].inc(i, 1)
        wv(sc, C["sct"], C["sct"].v)
        sc.mul(T["spc_sb"][...], T["sc4b"][...], -8.0)
        i = sc.mul(T["aim2_sb"][...], T["aimp_sb"][...], 1.0 / (2.0 * PI))
        C["cv"].inc(i, 1)
        rec("A.prm", C["cv"])
        wv(sc, C["vinit"], 1)
        if emit:
            pid = sc.partition_id()
            jv = pid & 1
        else:
            jv = None
        for cct in ("cc_in", "cc_inb"):
            for wdw in range(2):
                for slot in range(2):
                    for gi in range(8):
                        i = sc.dma_start(
                            out=T[cct][wdw, slot, gi * 128:(gi + 1) * 128, :],
                            in_=T["zt"][:, 0:TW // 2])
                        C["zr"].inc(i, 16)
        rec("A.zeros", C["zr"])

        def prework(c):
            par = c % 2
            wt(sc, C["trE"], f"A.tr.{c}")
            if c >= 2:
                wt(sc, C["mm"], f"A.mm.{c - 2}.15")
            else:
                wv(sc, C["bgv"], 8)
            i = sc.copy(T["xfeat"][par].rearrange("p g (t n) -> p (g t) n", n=128),
                        T["ps_tr"][...])
            C["xf"].inc(i, 1)
            rec(f"A.xf.{c}", C["xf"])
            for m in range(16):
                if m == 8 and c >= 2:
                    wt(sc, C["vx3"], f"A.vx3.{c - 2}")
                wt(sc, C["mm"], f"A.mm.{c}.{m}")
                if emit:
                    if m < 8:
                        dst = T["xh_bf"][par][:, m, :]
                    elif m < 12:
                        dst = g2(T["xh_re"][par], m - 8)
                    else:
                        dst = g2(T["xh_im"][par], m - 12)
                else:
                    dst = MockAP()
                bias = 0.0 if m < 8 else T["b_in_sb"][:, m:m + 1]
                i = sc.activation(dst, T["ps_win"][:, m % 2, 0:CA], AF.Identity,
                                  bias=bias, scale=1.0)
                C["we"].inc(i, 1)
                rec(f"A.we.{c}.{m}", C["we"])
            for gb in range(8):
                gate, m = divmod(gb, 4)
                wt(sc, C["gm"], f"A.gm.{c}.{gb}")
                dstt = T["gxf"][par] if gate == 0 else T["gaf"][par]
                i = sc.activation(g2(dstt, m), T["ps_g"][:, gb % 2, 0:CA],
                                  AF.Sigmoid, bias=T["b_geff"][:, gb:gb + 1],
                                  scale=1.0)
                C["ge"].inc(i, 1)
                rec(f"A.ge.{c}.{gb}", C["ge"])
            if c >= 2:
                wt(sc, C["vhs"], f"A.vhs.{c - 2}")
            for m in range(4):
                sc.activation(g2(T["p_rr"][par], m), g2(T["gaf"][par], m),
                              AF.Exp, bias=0.0, scale=T["spc_sb"][:, m:m + 1])
            i = sc.activation(T["p_r2"][par][...], T["p_rr"][par][...], AF.Square)
            C["cv"].inc(i, 1)
            rec(f"A.cv.r2.{c}", C["cv"])

        prework(0)
        prework(1)
        for c in range(NCH_A):
            par = c % 2
            wt(sc, C["vc"], f"A.vc.rc.{c}")
            if c >= 2:
                wt(sc, C["hmg"], f"A.hmg.{c - 2}")
            sc.activation(T["p_sin"][par][...], T["p_rc"][par][...], AF.Sin,
                          bias=0.0, scale=2.0 * PI)
            sc.activation(T["absq"][...], T["p_rc"][par][...], AF.Abs)
            i = sc.activation(T["p_cos"][par][...], T["absq"][...], AF.Sin,
                              bias=T["csts"][:, 0:1], scale=-2.0 * PI)
            C["cv"].inc(i, 1)
            rec(f"A.cv.sc.{c}", C["cv"])
            if c >= 1:
                wt(sc, C["vc"], f"A.vc.p2.{c - 1}")
            i = sc.activation(T["p_nrm"][...], T["p_r2"][par][...], AF.Sqrt,
                              bias=1.0, scale=-1.0)
            C["cv"].inc(i, 1)
            rec(f"A.cv.nrm.{c}", C["cv"])
            if c >= 1:
                cc = c - 1
                wt(sc, C["hmg"], f"A.hmg.{cc}")
                if cc == 0:
                    wt(sc, C["zr"], "A.zeros")
                cct = "cc_in" if (cc % NCH_C) < 4 else "cc_inb"
                i = sc.dma_start(
                    out=T[cct].rearrange("w s (g p) t -> w s p g t", p=128)[
                        cc // NCH_C, jv, :, :,
                        (cc % 4) * CA:(cc % 4 + 1) * CA],
                    in_=T["hmst"][cc % 2][...])
                C["ccin"].inc(i, 16)
                rec(f"A.ccd.{cc}", C["ccin"])
            if c + 2 < NCH_A:
                prework(c + 2)
        cc = NCH_A - 1
        wt(sc, C["hmg"], f"A.hmg.{cc}")
        i = sc.dma_start(
            out=T["cc_inb"].rearrange("w s (g p) t -> w s p g t", p=128)[
                cc // NCH_C, jv, :, :,
                (cc % 4) * CA:(cc % 4 + 1) * CA],
            in_=T["hmst"][cc % 2][...])
        C["ccin"].inc(i, 16)
        rec(f"A.ccd.{cc}", C["ccin"])
        rec("B.ccin2", C["ccin"])

    def body1_gpsimd(g):
        wt(g, C["ccin"], "B.ccin2")
        i = g.collective_compute(
            "ReduceScatter", Op.add,
            replica_groups=[[0, 1], [2, 3], [4, 5], [6, 7]],
            ins=[T["cc_inb"]], outs=[T["cc_outb"]])
        C["cc"].inc(i, 1)

    def body1_sync(s):
        wv(s, C["cc"], 1)
        for half in range(2):
            i = s.dma_start(
                out=T["hmfull"][:, half * 8:(half + 1) * 8, 0:TW // 2],
                in_=T["cc_out"].rearrange("w (g p) t -> w p g t", p=128)[half])
            C["cw"].inc(i, 16)
        rec("C.hma", C["cw"])
        for c in range(2):
            i = s.dma_start(
                out=T["xpo"][c % 2][...],
                in_=T["xpost"].rearrange("(c t p) d -> c p t d", c=NCH_C, p=128)[c])
            C["cx"].inc(i, 16)
            rec(f"C.x.{c}", C["cx"])
        wv(s, C["cc"], 2)
        for half in range(2):
            i = s.dma_start(
                out=T["hmfull"][:, half * 8:(half + 1) * 8, TW // 2:TW],
                in_=T["cc_outb"].rearrange("w (g p) t -> w p g t", p=128)[half])
            C["cw"].inc(i, 16)
        rec("C.hm", C["cw"])
        def stores_for(c, s=s):
            for tt in range(2):
                gt = 2 * c + tt
                wt(s, C["ln"], f"C.ln.{gt}")
                i = s.dma_start(
                    out=T["x2spill"][gt * 128:(gt + 1) * 128, :],
                    in_=T["x2tok"][tt][...])
                C["xd"].inc(i, 16)
                rec(f"C.xdt.{gt}", C["xd"])
            wt(s, C["x2f"], f"C.x2f.{c}")
            i = s.dma_start(
                out=T["x2feat"].rearrange("g p t -> p g t")[
                    :, :, c * CA:(c + 1) * CA],
                in_=T["x2fst"][...])
            C["xd"].inc(i, 16)
            rec(f"C.xd.{c}", C["xd"])

        for c in range(2, NCH_C):
            stores_for(c - 2)
            i = s.dma_start(
                out=T["xpo"][c % 2][...],
                in_=T["xpost"].rearrange("(c t p) d -> c p t d", c=NCH_C, p=128)[c])
            C["cx"].inc(i, 16)
            rec(f"C.x.{c}", C["cx"])
        stores_for(NCH_C - 2)
        stores_for(NCH_C - 1)
        rec("C.xd", C["xd"])

    def body1_tensor(t):
        wt(t, C["cww"], "C.w")

        def tr2(c):
            wt(t, C["ln"], f"C.ln.{2 * c + 1}")
            if c > 0:
                wt(t, C["x2f"], f"C.x2f.{c - 1}")
            for tt in range(2):
                for gi in range(8):
                    i = t.transpose(T["ps_tr2"][:, gi * 2 + tt, :],
                                    T["x2tok"][tt][:, gi * 128:(gi + 1) * 128],
                                    T["eye_sb"][...])
            C["tr2"].inc(i, 1)
            rec(f"C.tr2.{c}", C["tr2"])

        for c in range(NCH_C):
            wt(t, C["cx"], f"C.x.{c}")
            wt(t, C["cw"], "C.hma" if c < 4 else "C.hm")
            for tt in range(2):
                gt = 2 * c + tt
                if gt >= 2:
                    wt(t, C["ln"], f"C.ln.{gt - 2}")
                for k in range(16):
                    for n in range(2):
                        i = t.matmul(T["ps_wo"][:, tt, n, :],
                                     T["hmfull"][:, k, gt * 128:(gt + 1) * 128],
                                     T["wo_sb"][:, k, n * 512:(n + 1) * 512],
                                     start=(k == 0), stop=(k == 15))
                C["wo"].inc(i, 1)
                rec(f"C.wo.{gt}", C["wo"])
            if c > 0:
                tr2(c - 1)
        tr2(NCH_C - 1)

    def body1_vector(v):
        for c in range(NCH_C):
            for tt in range(2):
                gt = 2 * c + tt
                wt(v, C["wo"], f"C.wo.{gt}")
                wt(v, C["cx"], f"C.x.{c}")
                if emit:
                    pw = T["ps_wo"].rearrange("p s n x -> p s (n x)")[:, tt, :]
                else:
                    pw = MockAP()
                v.tensor_add(T["c_t2"][...], pw, T["bout_sb"][...])
                v.tensor_add(T["c_t"][...], T["c_t2"][...], T["xpo"][c % 2][:, tt, :])
                i = v.reduce_sum(T["c_sum"][...], T["c_t"][...], axis=AX)
                C["lnv"].inc(i, 1)
                rec(f"C.lnv.t.{gt}", C["lnv"])
                v.tensor_mul(T["c_t2"][...], T["c_t"][...], T["c_t"][...])
                v.reduce_sum(T["c_sq"][...], T["c_t2"][...], axis=AX)
                i = v.tensor_scalar_mul(T["c_m"][...], T["c_sum"][...], 1.0 / D)
                C["vt"].inc(i, 1)
                wv(v, C["vt"], C["vt"].v)
                i = v.tensor_mul(T["c_msq"][...], T["c_m"][...], T["c_m"][...])
                C["vt"].inc(i, 1)
                wv(v, C["vt"], C["vt"].v)
                i = v.scalar_tensor_tensor(T["c_var"][...], T["c_sq"][...],
                                           1.0 / D, T["c_msq"][...],
                                           Op.mult, Op.subtract)
                C["lnv"].inc(i, 1)
                rec(f"C.lnv.var.{gt}", C["lnv"])

                wt(v, C["lnc"], f"C.rstd.{gt}")
                if gt >= 2:
                    wt(v, C["xd"], f"C.xdt.{gt - 2}")
                    wt(v, C["tr2"], f"C.tr2.{(gt - 2) // 2}")
                i = v.reciprocal(T["c_rstd2"][...], T["c_rstd"][...])
                C["vt"].inc(i, 1)
                wv(v, C["vt"], C["vt"].v)
                i = v.tensor_scalar(T["x2tok"][tt][...], T["c_t"][...],
                                    T["c_m"][...], T["c_rstd2"][...],
                                    Op.subtract, Op.mult)
                C["ln"].inc(i, 1)
                rec(f"C.ln.{gt}", C["ln"])

    def body1_scalar(sc):
        wt(sc, C["wob"], "A.wob")
        i = sc.dma_start(
            out=T["wo_sb"][...],
            in_=T["wo_bf"].rearrange("(k p) m -> p k m", p=128))
        C["cww"].inc(i, 16)
        rec("C.w", C["cww"])
        for c in range(NCH_C):
            for tt in range(2):
                gt = 2 * c + tt
                wt(sc, C["lnv"], f"C.lnv.var.{gt}")
                i = sc.activation(T["c_rstd"][...], T["c_var"][...], AF.Sqrt,
                                  bias=T["csts"][:, 1:2], scale=1.0)
                C["lnc"].inc(i, 1)
                rec(f"C.rstd.{gt}", C["lnc"])
            wt(sc, C["tr2"], f"C.tr2.{c}")
            if c > 0:
                wt(sc, C["xd"], f"C.xd.{c - 1}")
            i = sc.copy(T["x2fst"].rearrange("p g (t n) -> p (g t) n", n=128),
                        T["ps_tr2"][...])
            C["x2f"].inc(i, 1)
            rec(f"C.x2f.{c}", C["x2f"])

    def body2_gpsimd(g):
        wv(g, C["wo"], 2 * NCH_C)
        wv(g, C["ln"], 2 * NCH_C)
        for p in range(NP_D):
            for mb in range(32):
                if mb >= 2:
                    wt(g, C["dmm"], f"D.mm1last.{p}.{mb - 2}")
                elif p > 0:
                    wt(g, C["dmm"], f"D.s1all.{p - 1}")
                for w12, wn in ((0, "w1T"), (1, "w2T")):
                    i = g.dma_start(
                        out=T["w12sl"][mb % 2][:, w12 * 8:(w12 + 1) * 8, :],
                        in_=T[wn].rearrange("(k p) m -> p k m", p=128)[
                            :, :, mb * 128:(mb + 1) * 128])
                    C["dw"].inc(i, 16)
                rec(f"D.w1.{p}.{mb}", C["dw"])
            for st, wn in ((0, "h1T"), (1, "h2T")):
                for mb in range(32):
                    if mb >= 2:
                        wt(g, C["dmm"], f"D.mmhlast.{p}.{st}.{mb - 2}")
                    elif st == 1:
                        wt(g, C["dmm"], f"D.mmhlast.{p}.0.{mb + 30}")
                    elif p > 0:
                        wt(g, C["dmm"], f"D.mmhlast.{p - 1}.1.{mb + 30}")
                    for q in range(4):
                        i = g.dma_start(
                            out=T["whsl"][mb % 2][:, q * 8:(q + 1) * 8, :],
                            in_=T[wn].rearrange("(k p) m -> p k m", p=128)[
                                :, q * 8:(q + 1) * 8, mb * 128:(mb + 1) * 128])
                        C["dw"].inc(i, 16)
                    rec(f"D.wh.{p}.{st}.{mb}", C["dw"])
            for sub in range(2):
                for k in range(32):
                    if k >= 2:
                        wt(g, C["dmm"], f"D.mm4k.{p}.{sub}.{k - 2}")
                    elif (p, sub) != (0, 0):
                        pp, ps_ = (p, sub - 1) if sub else (p - 1, 1)
                        wt(g, C["dmm"], f"D.mm4.{pp}.{ps_}")
                    i = g.dma_start(out=T["w3sl"][k % 2][...],
                                    in_=T["w3T"][k * 128:(k + 1) * 128, :])
                    C["dw"].inc(i, 16)
                    rec(f"D.w3.{p}.{sub}.{k}", C["dw"])

    def body2_sync(s):
        wv(s, C["ln"], 2 * NCH_C)
        for p in range(NP_D):
            if p == 0:
                wt(s, C["xd"], "C.xd")
            else:
                wt(s, C["dmm"], f"D.s1all.{p - 1}")
            i = s.dma_start(
                out=T["x2bf"][...],
                in_=T["x2feat"].rearrange("g p t -> p g t")[
                    :, :, p * PT:(p + 1) * PT])
            C["dx"].inc(i, 16)
            rec(f"D.x2bf.{p}", C["dx"])
            for sub in range(2):
                for tt in range(4):
                    gt = p * 8 + sub * 4 + tt
                    if gt > 0:
                        wt(s, C["dlnv"], f"D.lnv.t.{gt - 1}")
                    i = s.dma_start(out=T["x2tk"][...],
                                    in_=T["x2spill"][gt * 128:(gt + 1) * 128, :])
                    C["dx2"].inc(i, 16)
                    rec(f"D.x2tk.{gt}", C["dx2"])
                    wt(s, C["dln"], f"D.ln.{gt}")
                    i = s.dma_start(out=T["out"][gt * 128:(gt + 1) * 128, :],
                                    in_=T["outt"][...])
                    C["dout"].inc(i, 16)
                    rec(f"D.out.{gt}", C["dout"])
        wv(s, C["dout"], C["dout"].v if emit else 0)

    def body2_tensor(t):
        wt(t, C["x2f"], f"C.x2f.{NCH_C - 1}")
        wv(t, C["ln"], 2 * NCH_C)
        for p in range(NP_D):
            wt(t, C["dx"], f"D.x2bf.{p}")
            if p > 0:
                wt(t, C["dln"], f"D.ln4.{2 * p - 1}")
            seq = 0
            for mb in range(32):
                wt(t, C["dw"], f"D.w1.{p}.{mb}")
                for n in range(2):
                    par = seq % 2
                    if seq >= 2:
                        wt(t, C["dev"], f"D.ev1.{p}.{seq - 2}")
                    for w12 in range(2):
                        for k in range(8):
                            i = t.matmul(
                                T["ps_d"][:, 2 * w12 + par, :],
                                T["w12sl"][mb % 2][:, 8 * w12 + k, :],
                                T["x2bf"][:, k, n * 512:(n + 1) * 512],
                                start=(k == 0), stop=(k == 7))
                        C["dmm"].inc(i, 1)
                    rec(f"D.mm1.{p}.{seq}", C["dmm"])
                    seq += 1
                rec(f"D.mm1last.{p}.{mb}", C["dmm"])
            rec(f"D.s1all.{p}", C["dmm"])
            for st in range(2):
                seq = 0
                for mb in range(32):
                    wt(t, C["dw"], f"D.wh.{p}.{st}.{mb}")
                    for n in range(2):
                        slot = seq % 4
                        if seq >= 4:
                            wt(t, C["dev"], f"D.evh.{p}.{st}.{seq - 4}")
                        elif st == 0:
                            wt(t, C["dvh"], f"D.vhall.{p}")
                            wt(t, C["dev"], f"D.evall1.{p}")
                        else:
                            wt(t, C["dev"], f"D.evhall.{p}.0")
                        src = T["h_bf"] if st == 0 else T["g_bf"]
                        for k in range(32):
                            i = t.matmul(
                                T["ps_d"][:, slot, :],
                                T["whsl"][mb % 2][:, k, :],
                                src[:, k, n * 512:(n + 1) * 512],
                                start=(k == 0), stop=(k == 31))
                        C["dmm"].inc(i, 1)
                        rec(f"D.mmh.{p}.{st}.{seq}", C["dmm"])
                        seq += 1
                    rec(f"D.mmhlast.{p}.{st}.{mb}", C["dmm"])
            for sub in range(2):
                wt(t, C["dev"], f"D.evhall.{p}.1")
                if (p, sub) != (0, 0):
                    wt(t, C["dln"], f"D.ln4.{2 * p + sub - 1}")
                for k in range(32):
                    wt(t, C["dw"], f"D.w3.{p}.{sub}.{k}")
                    for tt in range(4):
                        for n in range(2):
                            i = t.matmul(
                                T["ps_d"][:, tt * 2 + n, :],
                                T["h_bf"][:, k, sub * 512 + tt * 128:
                                          sub * 512 + (tt + 1) * 128],
                                T["w3sl"][k % 2][:, n * 512:(n + 1) * 512],
                                start=(k == 0), stop=(k == 31))
                    C["dmm"].inc(i, 1)
                    rec(f"D.mm4k.{p}.{sub}.{k}", C["dmm"])
                rec(f"D.mm4.{p}.{sub}", C["dmm"])

    def body2_scalar(sc):
        for p in range(NP_D):
            seq = 0
            for mb in range(32):
                mg = mb
                for n in range(2):
                    par = seq % 2
                    wt(sc, C["dmm"], f"D.mm1.{p}.{seq}")
                    if seq >= 2:
                        wt(sc, C["dvh"], f"D.vh.{p}.{seq - 2}")
                    sc.activation(T["p_a1"][par][...], T["ps_d"][:, par, :],
                                  AF.Silu, bias=0.0,
                                  scale=T["s1_sb"][:, mg // 4 * 4 + mg % 4:
                                                   mg // 4 * 4 + mg % 4 + 1]
                                  if False else T["s1_sb"][:, mg:mg + 1])
                    i = sc.activation(T["p_a2"][par][...],
                                      T["ps_d"][:, 2 + par, :],
                                      AF.Copy, bias=0.0,
                                      scale=T["s2_sb"][:, mg:mg + 1])
                    C["dev"].inc(i, 1)
                    rec(f"D.ev1.{p}.{seq}", C["dev"])
                    seq += 1
            rec(f"D.evall1.{p}", C["dev"])
            for st in range(2):
                seq = 0
                for mb in range(32):
                    mg = mb
                    for n in range(2):
                        slot = seq % 4
                        wt(sc, C["dmm"], f"D.mmh.{p}.{st}.{seq}")
                        if emit:
                            dst = (T["g_bf"] if st == 0 else T["h_bf"])[
                                :, mg, n * 512:(n + 1) * 512]
                        else:
                            dst = MockAP()
                        scl = T["h1s_sb"] if st == 0 else T["h2s_sb"]
                        i = sc.activation(dst, T["ps_d"][:, slot, :],
                                          AF.Gelu, bias=0.0,
                                          scale=scl[:, mg:mg + 1])
                        C["dev"].inc(i, 1)
                        rec(f"D.evh.{p}.{st}.{seq}", C["dev"])
                        seq += 1
                rec(f"D.evhall.{p}.{st}", C["dev"])
            for sub in range(2):
                for tt in range(4):
                    gt = p * 8 + sub * 4 + tt
                    wt(sc, C["dlnv"], f"D.lnv.var.{gt}")
                    i = sc.activation(T["c_rstd"][...], T["c_var"][...],
                                      AF.Sqrt, bias=T["csts"][:, 1:2], scale=1.0)
                    C["dlnc"].inc(i, 1)
                    rec(f"D.rstd.{gt}", C["dlnc"])

    def body2_vector(v):
        for p in range(NP_D):
            seq = 0
            for mb in range(32):
                mg = mb
                for n in range(2):
                    wt(v, C["dev"], f"D.ev1.{p}.{seq}")
                    par = seq % 2
                    i = v.tensor_mul(
                        T["h_bf"][:, mg, n * 512:(n + 1) * 512] if emit else MockAP(),
                        T["p_a1"][par][...], T["p_a2"][par][...])
                    C["dvh"].inc(i, 1)
                    rec(f"D.vh.{p}.{seq}", C["dvh"])
                    seq += 1
            rec(f"D.vhall.{p}", C["dvh"])
            for sub in range(2):
                for tt in range(4):
                    gt = p * 8 + sub * 4 + tt
                    wt(v, C["dmm"], f"D.mm4.{p}.{sub}")
                    wt(v, C["dx2"], f"D.x2tk.{gt}")
                    if emit:
                        pw = T["ps_d"].rearrange("p s n -> p (s n)")[
                            :, (tt * 2) * 512:(tt * 2 + 2) * 512]
                    else:
                        pw = MockAP()
                    v.tensor_mul(T["c_t2"][...], pw, T["w3s_sb"][...])
                    v.tensor_add(T["c_t"][...], T["c_t2"][...], T["x2tk"][...])
                    i = v.reduce_sum(T["c_sum"][...], T["c_t"][...], axis=AX)
                    C["dlnv"].inc(i, 1)
                    rec(f"D.lnv.t.{gt}", C["dlnv"])
                    v.tensor_mul(T["c_t2"][...], T["c_t"][...], T["c_t"][...])
                    v.reduce_sum(T["c_sq"][...], T["c_t2"][...], axis=AX)
                    i = v.tensor_scalar_mul(T["c_m"][...], T["c_sum"][...], 1.0 / D)
                    C["vt"].inc(i, 1)
                    wv(v, C["vt"], C["vt"].v)
                    i = v.tensor_mul(T["c_msq"][...], T["c_m"][...], T["c_m"][...])
                    C["vt"].inc(i, 1)
                    wv(v, C["vt"], C["vt"].v)
                    i = v.scalar_tensor_tensor(T["c_var"][...], T["c_sq"][...],
                                               1.0 / D, T["c_msq"][...],
                                               Op.mult, Op.subtract)
                    C["dlnv"].inc(i, 1)
                    rec(f"D.lnv.var.{gt}", C["dlnv"])
                    wt(v, C["dlnc"], f"D.rstd.{gt}")
                    if gt > 0:
                        wt(v, C["dout"], f"D.out.{gt - 1}")
                    i = v.reciprocal(T["c_rstd2"][...], T["c_rstd"][...])
                    C["vt"].inc(i, 1)
                    wv(v, C["vt"], C["vt"].v)
                    i = v.tensor_scalar(T["outt"][...], T["c_t"][...],
                                        T["c_m"][...], T["c_rstd2"][...],
                                        Op.subtract, Op.mult)
                    C["dln"].inc(i, 1)
                    rec(f"D.ln.{gt}", C["dln"])
                rec(f"D.ln4.{2 * p + sub}", C["dln"])

    return {
        0: [("sync", body0_sync), ("gpsimd", body0_gpsimd),
            ("tensor", body0_tensor), ("vector", body0_vector),
            ("scalar", body0_scalar)],
        1: [("gpsimd", body1_gpsimd), ("sync", body1_sync),
            ("tensor", body1_tensor), ("vector", body1_vector),
            ("scalar", body1_scalar)],
        2: [("gpsimd", body2_gpsimd), ("sync", body2_sync),
            ("tensor", body2_tensor), ("scalar", body2_scalar),
            ("vector", body2_vector)],
    }


def build_nc():
    M = {}

    class TDict(dict):
        def __missing__(self, k):
            if k in ("xh_bf", "xh_re", "xh_im", "xpo", "x2tok",
                     "w12sl", "whsl", "w3sl", "p_a1", "p_a2", "hmst",
                     "gxf", "gaf", "p_rr", "p_r2", "p_rc", "p_sin", "p_cos", "xfeat"):
                return [MockAP()] * 4
            return MockAP()

    Tm = TDict()
    Tm["ctr"] = {n: Ctr(None) for n in CTR_NAMES}
    bodies = build_bodies(Tm, M, emit=False)
    mock = MockEng()
    for blk_i in (0, 1, 2):
        for name, fn in bodies[blk_i]:
            fn(mock)

    nc = bass.Bass("TRN2", target_bir_lowering=False, detect_race_conditions=False)
    dr = lambda n, s, d=F32, k="ExternalInput": nc.dram_tensor(n, s, d, kind=k).ap()
    T = {}
    T["xfull"] = dr("xfull", [S, D])
    T["xpost"] = dr("xpost", [TW, D])
    T["w_in_c"] = dr("w_in_c", [D, DH])
    T["b_in_c"] = dr("b_in_c", [128, 16])
    T["w_g_c"] = dr("w_g_c", [128, 16, 128])
    T["b_g_c"] = dr("b_g_c", [128, 8])
    T["aprm_c"] = dr("aprm_c", [128, 4])
    T["aimp_c"] = dr("aimp_c", [128, 4])
    T["w_out_c"] = dr("w_out_c", [DH, D])
    T["b_out_rep"] = dr("b_out_rep", [128, D])
    T["w1T"] = dr("w1T", [D, FH])
    T["w2T"] = dr("w2T", [D, FH])
    T["h1T"] = dr("h1T", [FH, FH])
    T["h2T"] = dr("h2T", [FH, FH])
    T["w3T"] = dr("w3T", [FH, D])
    T["s1_r"] = dr("s1_r", [128, 32])
    T["s2_r"] = dr("s2_r", [128, 32])
    T["h1s_r"] = dr("h1s_r", [128, 32])
    T["h2s_r"] = dr("h2s_r", [128, 32])
    T["w3s_rep"] = dr("w3s_rep", [128, D])
    T["eye_f"] = dr("eye_f", [128, 128])
    T["consts_c"] = dr("consts_c", [128, 4])
    T["out"] = dr("out", [TW, D], F32, "ExternalOutput")
    T["cc_in"] = dr("cc_in", [2, 2, D, TW // 2], BF16, "Internal")
    T["cc_inb"] = dr("cc_inb", [2, 2, D, TW // 2], BF16, "Internal")
    T["cc_out"] = dr("cc_out", [2, D, TW // 2], BF16, "Internal")
    T["cc_outb"] = dr("cc_outb", [2, D, TW // 2], BF16, "Internal")
    T["x2spill"] = dr("x2spill", [TW, D], F32, "ExternalOutput")
    T["x2feat"] = dr("x2feat", [8, 128, TW], BF16, "Internal")
    T["wo_bf"] = dr("wo_bf", [DH, D], BF16, "Internal")

    es = ExitStack()
    SB = lambda n, s, d=F32: es.enter_context(nc.sbuf_tensor(n, s, d))
    T["ctr"] = {n: Ctr(es.enter_context(nc.semaphore(n))) for n in CTR_NAMES}

    for n, s in (("b_in_sb", [128, 16]), ("b_g_sb", [128, 8]),
                 ("aprm_sb", [128, 4]), ("aimp_sb", [128, 4]),
                 ("spc_sb", [128, 4]), ("aim2_sb", [128, 4]),
                 ("csts", [128, 4]), ("cre", [128, 4]), ("cim", [128, 4]),
                 ("sc4", [128, 4]), ("sc4b", [128, 4]), ("sc4c", [128, 4]),
                 ("sc4d", [128, 4]), ("bout_sb", [128, D]), ("w3s_sb", [128, D]),
                 ("eye_sb", [128, 128]), ("s1_sb", [128, 32]), ("s2_sb", [128, 32]),
                 ("h1s_sb", [128, 32]), ("h2s_sb", [128, 32]),
                 ("b_geff", [128, 8]),
                 ("ones_a", [128, CA]), ("c_sum", [128, 1]),
                 ("c_sq", [128, 1]), ("c_m", [128, 1]), ("c_msq", [128, 1]),
                 ("c_var", [128, 1]), ("c_rstd", [128, 1]),
                 ("c_rstd2", [128, 1])):
        T[n] = SB(n, s)
    T["b_in_f"] = SB("b_in_f", [128, 8, 1], BF16)

    bodies = build_bodies(T, M, emit=True)

    esA = ExitStack()
    SBa = lambda n, s, d=F32: esA.enter_context(nc.sbuf_tensor(n, s, d))
    T["win_sb"] = SBa("win_sb", [128, 8, DH], BF16)
    T["wg_sb"] = SBa("wg_sb", [128, 16, 128], BF16)
    T["xstage"] = SBa("xstage", [128, 2, D])
    T["xfeat"] = [SBa(f"xfeat{i}", [128, 8, CA], BF16) for i in range(2)]
    T["xh_bf"] = [SBa(f"xh_bf{i}", [128, 8, CA], BF16) for i in range(2)]
    T["xh_re"] = [SBa(f"xh_re{i}", [128, 4 * CA]) for i in range(2)]
    T["xh_im"] = [SBa(f"xh_im{i}", [128, 4 * CA]) for i in range(2)]
    T["gxf"] = [SBa(f"gxf{i}", [128, 4 * CA]) for i in range(2)]
    T["gaf"] = [SBa(f"gaf{i}", [128, 4 * CA]) for i in range(2)]
    for i in (2, 3, 5, 6, 7):
        T[f"pl{i}"] = SBa(f"pl{i}", [128, 4 * CA])
    T["p_ki"] = SBa("p_ki", [128, 4 * CA], I32)
    T["p_rr"] = [SBa(f"p_rr{i}", [128, 4 * CA]) for i in range(2)]
    T["p_r2"] = [SBa(f"p_r2{i}", [128, 4 * CA]) for i in range(2)]
    T["p_rc"] = [SBa(f"p_rc{i}", [128, 4 * CA]) for i in range(2)]
    T["p_sin"] = [SBa(f"p_sin{i}", [128, 4 * CA]) for i in range(2)]
    T["p_cos"] = [SBa(f"p_cos{i}", [128, 4 * CA]) for i in range(2)]
    T["absq"] = SBa("absq", [128, 4 * CA])
    for n in ("p_nrm", "fl1", "fl2", "fl3", "p_H1", "p_H2"):
        T[n] = SBa(n, [128, 4 * CA])
    T["hmst"] = [SBa(f"hmst{i}", [128, 8, CA], BF16) for i in range(2)]
    T["zt"] = SBa("zt", [128, TW], BF16)
    psA = ExitStack()
    T["ps_tr"] = psA.enter_context(nc.psum_tensor("ps_tr", [128, 16, 128], F32))
    T["ps_win"] = psA.enter_context(nc.psum_tensor("ps_win", [128, 2, 512], F32))
    T["ps_g"] = psA.enter_context(nc.psum_tensor("ps_g", [128, 2, 512], F32))

    with nc.Block() as blk:
        for name, fn in bodies[0]:
            getattr(blk, name)(fn)
    esA.close()
    psA.close()

    esC = ExitStack()
    SBc = lambda n, s, d=F32: esC.enter_context(nc.sbuf_tensor(n, s, d))
    T["wo_sb"] = SBc("wo_sb", [128, 16, D], BF16)
    T["hmfull"] = SBc("hmfull", [128, 16, TW], BF16)
    T["xpo"] = [SBc(f"xpo{i}", [128, 2, D]) for i in range(2)]
    T["x2tok"] = [SBc(f"x2tok{i}", [128, D]) for i in range(2)]
    T["x2fst"] = SBc("x2fst", [128, 8, CA], BF16)
    T["c_t"] = SBc("c_t", [128, D])
    T["c_t2"] = SBc("c_t2", [128, D])
    psC = ExitStack()
    T["ps_wo"] = psC.enter_context(nc.psum_tensor("ps_wo", [128, 2, 2, 512], F32))
    T["ps_tr2"] = psC.enter_context(nc.psum_tensor("ps_tr2", [128, 16, 128], F32))

    with nc.Block() as blk:
        for name, fn in bodies[1]:
            getattr(blk, name)(fn)
    esC.close()
    psC.close()

    esD = ExitStack()
    SBd = lambda n, s, d=F32: esD.enter_context(nc.sbuf_tensor(n, s, d))
    T["h_bf"] = SBd("h_bf", [128, 32, PT], BF16)
    T["g_bf"] = SBd("g_bf", [128, 32, PT], BF16)
    T["x2bf"] = SBd("x2bf", [128, 8, PT], BF16)
    T["w12sl"] = [SBd(f"w12sl{i}", [128, 16, 128], BF16) for i in range(2)]
    T["whsl"] = [SBd(f"whsl{i}", [128, 32, 128], BF16) for i in range(2)]
    T["w3sl"] = [SBd(f"w3sl{i}", [128, D], BF16) for i in range(2)]
    T["x2tk"] = SBd("x2tk", [128, D])
    T["outt"] = SBd("outt", [128, D])
    T["p_a1"] = [SBd(f"p_a1{i}", [128, 512]) for i in range(2)]
    T["c_t"] = SBd("c_t2d", [128, D])
    T["c_t2"] = SBd("c_t2d2", [128, D])
    T["p_a2"] = [SBd(f"p_a2{i}", [128, 512]) for i in range(2)]
    psD = ExitStack()
    T["ps_d"] = psD.enter_context(nc.psum_tensor("ps_d", [128, 8, 512], F32))

    with nc.Block() as blk:
        for name, fn in bodies[2]:
            getattr(blk, name)(fn)
    esD.close()
    psD.close()
    es.close()
    return nc


def kernel(**inp):
    f32 = np.float32
    g = lambda k: np.asarray(inp[k], f32)
    x = g("L0_attn_x")
    w_in, b_in = g("L0_attn_w_in"), g("L0_attn_b_in")
    w_gx, b_gx = g("L0_attn_w_gx"), g("L0_attn_b_gx")
    w_ga, b_ga = g("L0_attn_w_ga"), g("L0_attn_b_ga")
    a_re, a_im = g("L0_attn_a_real_param"), g("L0_attn_a_imag_param")
    w_out, b_out = g("L0_attn_w_out"), g("L0_attn_b_out")
    w1w, w1s = g("L0_ffn_w1_w"), g("L0_ffn_w1_s")
    w2w, w2s = g("L0_ffn_w2_w"), g("L0_ffn_w2_s")
    h1w, h1s = g("L0_ffn_h1_w"), g("L0_ffn_h1_s")
    h2w, h2s = g("L0_ffn_h2_w"), g("L0_ffn_h2_s")
    w3w, w3s = g("L0_ffn_w3_w"), g("L0_ffn_w3_s")

    cont = np.ascontiguousarray
    shared = dict(
        w_out_c=cont(np.concatenate([w_out[0:512], w_out[1024:1536],
                                     w_out[512:1024], w_out[1536:2048]], 0)),
        b_out_rep=cont(np.broadcast_to(b_out, (128, D))),
        w1T=cont(w1w.T), w2T=cont(w2w.T), h1T=cont(h1w.T), h2T=cont(h2w.T),
        w3T=cont(w3w.T),
        s1_r=cont(w1s.reshape(32, 128).T), s2_r=cont(w2s.reshape(32, 128).T),
        h1s_r=cont(h1s.reshape(32, 128).T), h2s_r=cont(h2s.reshape(32, 128).T),
        w3s_rep=cont(np.broadcast_to(w3s, (128, D))),
        eye_f=np.eye(128, dtype=f32),
        consts_c=cont(np.broadcast_to(
            np.array([PI / 2.0, 1e-5, 0.0, 0.0], f32), (128, 4))),
    )

    in_maps = []
    for core in range(8):
        b, j = divmod(core, 2)
        w_in_cc = cont(np.concatenate([
            w_in[:, 1024 * j:1024 * j + 1024],
            w_in[:, 512 * j:512 * j + 512],
            w_in[:, 1024 + 512 * j:1536 + 512 * j]], axis=1))
        b_in_v = np.concatenate([
            b_in[1024 * j:1024 * j + 1024],
            b_in[512 * j:512 * j + 512],
            b_in[1024 + 512 * j:1536 + 512 * j]])
        wg = np.stack([w_gx[4 * j:4 * j + 4], w_ga[4 * j:4 * j + 4]])
        im = dict(shared)
        im.update(
            xfull=cont(x[b]),
            xpost=cont(x[b, TW * j:TW * (j + 1)]),
            w_in_c=w_in_cc,
            b_in_c=cont(b_in_v.reshape(16, 128).T),
            w_g_c=cont(wg.reshape(2, 4, 2, 128, 128)
                       .transpose(3, 0, 1, 2, 4).reshape(128, 16, 128)),
            b_g_c=cont(np.stack([b_gx, b_ga])[:, 512 * j:512 * j + 512]
                       .reshape(2, 4, 128).transpose(2, 0, 1).reshape(128, 8)),
            aprm_c=cont(a_re[512 * j:512 * j + 512].reshape(4, 128).T),
            aimp_c=cont(a_im[512 * j:512 * j + 512].reshape(4, 128).T),
        )
        in_maps.append(im)

    nc = build_nc()
    res = run_bass_kernel_spmd(nc, in_maps, core_ids=list(range(8)))
    out = np.empty((B, S, D), f32)
    for core in range(8):
        b, j = divmod(core, 2)
        out[b, TW * j:TW * (j + 1)] = res.results[core]["out"]
    return out


if __name__ == "__main__":
    nc = build_nc()
    n_inst = sum(1 for _ in nc.all_instructions())
    print("built ok, instructions:", n_inst)
